# version 05
# baseline (speedup 1.0000x reference)
import os
import numpy as np
import concourse.bass as bass
import concourse.mybir as mybir
from contextlib import ExitStack
from types import SimpleNamespace

F32 = mybir.dt.float32
BF16 = mybir.dt.bfloat16
AF = mybir.ActivationFunctionType
ALU = mybir.AluOpType
AX = mybir.AxisListType.X

P = 128
D = 2048
SEQ = 4096
TOWN = 1024
CT = D // P
DKV = 512
NQ, NH, DH = 4, 8, 64
E, FD, FRT = 8, 1408, 11
EPS = 1e-5
NKT = SEQ // P
NCH = SEQ // 512
SCALE = 0.125

ENGS = ("sync", "gpsimd", "tensor", "vector", "scalar")


class Plan:
    def __init__(self):
        self.steps = []
        self.val = {}

    def cur(self, sem):
        v = self.val.get(sem, 0)
        return (sem, v) if v else None

    def step(self, eng, fn, inc=None, waits=(), dma_n=0, dma_sem=None):
        w = []
        for t in waits:
            if t is None:
                continue
            if isinstance(t, list):
                w.extend(xx for xx in t if xx is not None)
            else:
                w.append(t)
        self.steps.append(dict(eng=eng, fn=fn, inc=inc, waits=w))
        if inc is not None:
            s, a = inc
            self.val[s] = self.val.get(s, 0) + a
            return (s, self.val[s])
        if dma_n:
            self.val[dma_sem] = self.val.get(dma_sem, 0) + 16 * dma_n
            return (dma_sem, self.val[dma_sem])
        return None


def later(*toks):
    best = {}
    for t in toks:
        if t is None:
            continue
        if isinstance(t, list):
            for s, v in t:
                best[s] = max(best.get(s, 0), v)
        else:
            s, v = t
            best[s] = max(best.get(s, 0), v)
    return [(s, v) for s, v in best.items()]


def build(debug=False, upto=4):
    nc = bass.Bass("TRN2", target_bir_lowering=False)

    x = nc.dram_tensor("x", [SEQ, D], F32, kind="ExternalInput")
    xq = nc.dram_tensor("xq", [TOWN, D], F32, kind="ExternalInput")
    wqT = nc.dram_tensor("wqT", [D, D], F32, kind="ExternalInput")
    wkT = nc.dram_tensor("wkT", [D, DKV], F32, kind="ExternalInput")
    wvT = nc.dram_tensor("wvT", [D, DKV], F32, kind="ExternalInput")
    mem = nc.dram_tensor("mem", [DKV, DKV], F32, kind="ExternalInput")
    memnorm = nc.dram_tensor("memnorm", [P, 4], F32, kind="ExternalInput")
    memw = nc.dram_tensor("memw", [1, 1], F32, kind="ExternalInput")
    gwT = nc.dram_tensor("gwT", [D, E], F32, kind="ExternalInput")
    gpT = nc.dram_tensor("gpT", [E, D, FD], F32, kind="ExternalInput")
    upT = nc.dram_tensor("upT", [E, D, FD], F32, kind="ExternalInput")
    dpT = nc.dram_tensor("dpT", [E, FD, D], F32, kind="ExternalInput")
    out = nc.dram_tensor("out", [D, TOWN], F32, kind="ExternalOutput")

    dbg = {}
    if debug:
        dbg["kT"] = nc.dram_tensor("d_kT", [P, 4 * SEQ], BF16, kind="ExternalOutput")
        dbg["vb"] = nc.dram_tensor("d_vb", [P, NKT * 520], BF16, kind="ExternalOutput")
        dbg["qT"] = nc.dram_tensor("d_qT", [P, CT * TOWN], BF16, kind="ExternalOutput")
        dbg["y1"] = nc.dram_tensor("d_y1", [P, CT * TOWN], F32, kind="ExternalOutput")
        dbg["y2"] = nc.dram_tensor("d_y2", [P, CT * TOWN], F32, kind="ExternalOutput")
        dbg["ynT"] = nc.dram_tensor("d_ynT", [P, CT * TOWN], BF16, kind="ExternalOutput")
        dbg["tw"] = nc.dram_tensor("d_tw", [P, 64], F32, kind="ExternalOutput")

    pl = Plan()
    SEM = {}
    n_out_dma = [0]
    marks = []

    es = ExitStack()
    with es:
        yT = es.enter_context(nc.sbuf_tensor("yT", [P, CT * TOWN], F32))
        identbf = es.enter_context(nc.sbuf_tensor("identbf", [P, P], BF16))
        ident32 = es.enter_context(nc.sbuf_tensor("ident32", [P, P], F32))
        onesbf = es.enter_context(nc.sbuf_tensor("onesbf", [1, P], BF16))
        onescol = es.enter_context(nc.sbuf_tensor("onescol", [P, 1], F32))
        epsc = es.enter_context(nc.sbuf_tensor("epsc", [P, 1], F32))
        wmix = es.enter_context(nc.sbuf_tensor("wmix", [1, 4], F32))
        m2s = es.enter_context(nc.sbuf_tensor("m2s", [P, 8], F32))
        r2s = es.enter_context(nc.sbuf_tensor("r2s", [P, 8], F32))

        def yTv(rt):
            return yT[:, rt * TOWN:(rt + 1) * TOWN]

        BC = SimpleNamespace(bhi=None, blo=None, blt=None, free=None)

        def emit_bcast(src_ap, n, ps_ap, wait_src):
            bhi, blo, blt = BC.bhi, BC.blo, BC.blt

            def f_hi(v):
                return v.tensor_copy(bhi[:, 0:n], src_ap)
            th = pl.step("vector", f_hi, inc=("dve", 1),
                         waits=later(wait_src, BC.free))

            def f_lo(v):
                return v.scalar_tensor_tensor(blt[:, 0:n], bhi[:, 0:n], -1.0,
                                              src_ap, ALU.mult, ALU.add)
            tl = pl.step("vector", f_lo, inc=("dve", 1), waits=[th])

            def f_lo2(v):
                return v.tensor_copy(blo[:, 0:n], blt[:, 0:n])
            tl2 = pl.step("vector", f_lo2, inc=("dve", 1), waits=[tl])

            def f_mm(t):
                last = None
                for j in range(0, n, 512):
                    j1 = min(n, j + 512)
                    t.matmul(ps_ap[:, j:j1], onesbf[:], bhi[:, j:j1],
                             start=True, stop=False)
                    last = t.matmul(ps_ap[:, j:j1], onesbf[:], blo[:, j:j1],
                                    start=False, stop=True)
                return last
            tmm = pl.step("tensor", f_mm, inc=("pe", 1), waits=later(th, tl2))
            BC.free = tmm
            return tmm

        def emit_ln_stats(x_ap, st6, sub, mean_ap, var_ap, rstd_ap, nb_ap,
                          waits):
            def f_stats(v):
                last = None
                for j in range(4):
                    last = v.bn_stats(st6[:, j * 6:(j + 1) * 6],
                                      x_ap[:, j * 512:(j + 1) * 512])
                return last
            t1 = pl.step("vector", f_stats, inc=("dve", 1), waits=waits)
            st6v = st6[:].rearrange("p (a b) -> p a b", b=3)
            means = st6v[:, :, 1:2].rearrange("p a one -> p (a one)")
            cvars = st6v[:, :, 2:3].rearrange("p a one -> p (a one)")

            def f_red(v):
                v.reduce_sum(mean_ap, means, axis=AX)
                return v.reduce_sum(var_ap, cvars, axis=AX)
            t2 = pl.step("vector", f_red, inc=("dve", 1), waits=[t1])

            def f_cp(v):
                return v.tensor_copy(sub[:, 8:16], means)
            t2b = pl.step("vector", f_cp, inc=("dve", 1), waits=[t1])

            def f_mn(v):
                return v.tensor_scalar(mean_ap, mean_ap, 0.125, 0.0,
                                       ALU.mult, ALU.add)
            t3 = pl.step("vector", f_mn, inc=("dve", 1), waits=[t2])

            def f_dev(v):
                return v.tensor_scalar(sub[:, 0:8], sub[:, 8:16], mean_ap, 0.0,
                                       ALU.subtract, ALU.add)
            t4 = pl.step("vector", f_dev, inc=("dve", 1), waits=[t3, t2b])

            def f_sq(v):
                return v.tensor_mul(sub[:, 0:8], sub[:, 0:8], sub[:, 0:8])
            t5 = pl.step("vector", f_sq, inc=("dve", 1), waits=[t4])

            def f_vs(v):
                return v.reduce_sum(sub[:, 16:17], sub[:, 0:8], axis=AX)
            t6 = pl.step("vector", f_vs, inc=("dve", 1), waits=[t5])

            def f_v1(v):
                return v.tensor_scalar(var_ap, var_ap, 1.0 / 2048.0, 0.0,
                                       ALU.mult, ALU.add)
            t7 = pl.step("vector", f_v1, inc=("dve", 1), waits=[t2])

            def f_v2(v):
                return v.scalar_tensor_tensor(var_ap, sub[:, 16:17], 0.125,
                                              var_ap, ALU.mult, ALU.add)
            t8 = pl.step("vector", f_v2, inc=("dve", 1), waits=[t6, t7])

            def f_rstd(a):
                a.activation(rstd_ap, var_ap, AF.Ln, bias=epsc[:])
                return a.activation(rstd_ap, rstd_ap, AF.Exp, scale=-0.5)
            t9 = pl.step("scalar", f_rstd, inc=("act", 1), waits=[t8])

            def f_nb(v):
                return v.scalar_tensor_tensor(nb_ap, mean_ap, -1.0, rstd_ap,
                                              ALU.mult, ALU.mult)
            t10 = pl.step("vector", f_nb, inc=("dve", 1), waits=[t9])
            return t9, t10

        esM = ExitStack()
        last_evac = {"kT": None, "vb": None, "qT": None, "yT": None}
        CX = SimpleNamespace(qTv=None)
        with esM:
            kT = esM.enter_context(nc.sbuf_tensor("kT", [P, 4 * SEQ], BF16))
            vb = esM.enter_context(nc.sbuf_tensor("vb", [P, NKT * 520], BF16))

            def kTv(r):
                return kT[:, r * SEQ:(r + 1) * SEQ]

            def f_ident(gp):
                gp.memset(identbf[:], 0.0)
                gp.affine_select(out=identbf[:], in_=identbf[:],
                                 compare_op=ALU.not_equal, fill=1.0, base=0,
                                 pattern=[[-1, P]], channel_multiplier=1)
                gp.memset(ident32[:], 0.0)
                return gp.affine_select(out=ident32[:], in_=ident32[:],
                                        compare_op=ALU.not_equal, fill=1.0,
                                        base=0, pattern=[[-1, P]],
                                        channel_multiplier=1)
            t_ident = pl.step("gpsimd", f_ident, inc=("gp", 1))

            def f_const(v):
                v.memset(onesbf[:], 1.0)
                v.memset(onescol[:], 1.0)
                v.memset(epsc[:], EPS)
                return v.memset(vb[:], 1.0)
            t_const = pl.step("vector", f_const, inc=("dve", 1))

            def f_mw(sync):
                sync.dma_start(out=wmix[0:1, 0:1], in_=memw[:, :]
                               ).then_inc(SEM["xdma"], 16)
                return None
            t_mw = pl.step("sync", f_mw, dma_n=1, dma_sem="xdma")

            def f_s1(a):
                return a.activation(wmix[0:1, 1:2], wmix[0:1, 0:1], AF.Exp,
                                    scale=-1.0)
            t_s1 = pl.step("scalar", f_s1, inc=("act", 1), waits=[t_mw])

            def f_s2(v):
                return v.tensor_scalar(wmix[0:1, 2:3], wmix[0:1, 1:2], 1.0, 0.0,
                                       ALU.add, ALU.add)
            t_s2 = pl.step("vector", f_s2, inc=("dve", 1), waits=[t_s1])

            def f_s3(v):
                return v.reciprocal(wmix[0:1, 0:1], wmix[0:1, 2:3])
            t_s3 = pl.step("vector", f_s3, inc=("dve", 1), waits=[t_s2])

            def f_s4(v):
                return v.tensor_mul(wmix[0:1, 1:2], wmix[0:1, 1:2],
                                    wmix[0:1, 0:1])
            t_s4 = pl.step("vector", f_s4, inc=("dve", 1), waits=[t_s3])

            def emit_p0_pass(B, R, srcten, n_chunks, own, t_w):
                for ch in range(n_chunks):
                    cp = ch % B.xnTn
                    ev_toks = []
                    for i in range(4):
                        tt = ch * 4 + i
                        tp = tt % B.xtn
                        xtap = B.xt[:, tp * D:(tp + 1) * D]

                        def f_x(sync, tt=tt, tp=tp, B=B, srcten=srcten):
                            sync.dma_start(out=B.xt[:, tp * D:(tp + 1) * D],
                                           in_=srcten[tt * P:(tt + 1) * P, :]
                                           ).then_inc(SEM["xdma"], 16)
                            return None
                        tx = pl.step("sync", f_x, dma_n=1, dma_sem="xdma",
                                     waits=later(R.xt_free[tp]))
                        sp_ = tt % 2
                        t_rs, t_nb = emit_ln_stats(
                            xtap, B.st6[:, sp_ * 24:(sp_ + 1) * 24],
                            B.sub[:, sp_ * 17:(sp_ + 1) * 17],
                            B.mn0[:, sp_:sp_ + 1], B.vr0[:, sp_:sp_ + 1],
                            B.rs0[:, sp_:sp_ + 1], B.nb0[:, sp_:sp_ + 1],
                            later(tx, R.st_free[sp_]))
                        R.st_free[sp_] = None

                        def f_aff(a, xtap=xtap, B=B, sp_=sp_):
                            return a.activation(B.xn[:], xtap, AF.Identity,
                                                bias=B.nb0[:, sp_:sp_ + 1],
                                                scale=B.rs0[:, sp_:sp_ + 1])
                        t_xn = pl.step("scalar", f_aff, inc=("act", 1),
                                       waits=later(t_rs, t_nb, R.xn_free[0]))
                        R.st_free[sp_] = t_xn

                        xt_consumers = [t_xn, t_rs]
                        if own:
                            for g in range(4):
                                gb = g % 2

                                def f_trx(t, g=g, gb=gb, xtap=xtap, B=B):
                                    last = None
                                    for idx in range(4):
                                        c = g * 4 + idx
                                        last = t.transpose(
                                            B.psX[:, gb * 512 + idx * 128:
                                                  gb * 512 + (idx + 1) * 128],
                                            xtap[:, c * P:(c + 1) * P],
                                            ident32[:])
                                    return last
                                ttr = pl.step("tensor", f_trx, inc=("pe", 1),
                                              waits=later(tx, R.psX_free[gb],
                                                          t_ident))

                                def f_evx(v, g=g, gb=gb, tt=tt, B=B):
                                    last = None
                                    for idx in range(4):
                                        c = g * 4 + idx
                                        last = v.tensor_copy(
                                            yTv(c)[:, tt * P:(tt + 1) * P],
                                            B.psX[:, gb * 512 + idx * 128:
                                                  gb * 512 + (idx + 1) * 128])
                                    return last
                                tev = pl.step("vector", f_evx, inc=("dve", 1),
                                              waits=[ttr])
                                R.psX_free[gb] = tev
                                xt_consumers.append(ttr)
                                last_evac["yT"] = tev
                        R.xt_free[tp] = later(*xt_consumers)

                        tr_toks = []
                        for g in range(2):
                            def f_trn(t, g=g, B=B):
                                last = None
                                for idx in range(8):
                                    c = g * 8 + idx
                                    last = t.transpose(
                                        B.psTR[:, g * 1024 + idx * P:
                                               g * 1024 + (idx + 1) * P],
                                        B.xn[:, c * P:(c + 1) * P], identbf[:])
                                return last
                            ttr = pl.step("tensor", f_trn, inc=("pe", 1),
                                          waits=later(t_xn, R.psTR_free[g],
                                                      t_ident))

                            def f_evn(eng, g=g, cp=cp, i=i, B=B):
                                cp_fn = (eng.copy if g == 0
                                         else eng.tensor_copy)
                                last = None
                                for idx in range(8):
                                    c = g * 8 + idx
                                    last = cp_fn(
                                        B.xnT[:, cp * CT * 512 + c * 512 +
                                              i * P:cp * CT * 512 + c * 512 +
                                              (i + 1) * P],
                                        B.psTR[:, g * 1024 + idx * P:
                                               g * 1024 + (idx + 1) * P])
                                return last
                            ev_eng = "scalar" if g == 0 else "vector"
                            tev = pl.step(ev_eng, f_evn,
                                          inc=("act" if g == 0 else "dve", 1),
                                          waits=later(ttr, R.xnT_free[cp]))
                            R.psTR_free[g] = tev
                            tr_toks.append(ttr)
                            ev_toks.append(tev)
                        R.xn_free[0] = later(*tr_toks)

                    if not own:
                        for r in range(4):
                            kp = r % 2

                            def f_k(t, r=r, cp=cp, kp=kp, B=B):
                                last = None
                                for c in range(CT):
                                    last = t.matmul(
                                        B.psKQ[:, kp * 512:(kp + 1) * 512],
                                        B.wk_sb[:, c * DKV + r * P:
                                                c * DKV + (r + 1) * P],
                                        B.xnT[:, cp * CT * 512 + c * 512:
                                              cp * CT * 512 + (c + 1) * 512],
                                        start=(c == 0), stop=(c == CT - 1))
                                return last
                            tk = pl.step("tensor", f_k, inc=("pe", 1),
                                         waits=later(*ev_toks, R.psKQ_free[kp],
                                                     t_w))

                            def f_ek(v, r=r, ch=ch, kp=kp, B=B):
                                return v.tensor_copy(
                                    kTv(r)[:, ch * 512:(ch + 1) * 512],
                                    B.psKQ[:, kp * 512:(kp + 1) * 512])
                            te = pl.step("vector", f_ek, inc=("dve", 1),
                                         waits=[tk])
                            R.psKQ_free[kp] = te
                            last_evac["kT"] = te
                        for i in range(4):
                            vp = i % 2
                            kt = ch * 4 + i

                            def f_v(t, i=i, cp=cp, vp=vp, B=B):
                                last = None
                                for c in range(CT):
                                    last = t.matmul(
                                        B.psV[:, vp * 512:(vp + 1) * 512],
                                        B.xnT[:, cp * CT * 512 + c * 512 +
                                              i * P:cp * CT * 512 + c * 512 +
                                              (i + 1) * P],
                                        B.wv_sb[:, c * DKV:(c + 1) * DKV],
                                        start=(c == 0), stop=(c == CT - 1))
                                return last
                            tv = pl.step("tensor", f_v, inc=("pe", 1),
                                         waits=later(*ev_toks, R.psV_free[vp],
                                                     t_w))

                            def f_ev(v, kt=kt, vp=vp, B=B):
                                dst = vb[:, kt * 520:(kt + 1) * 520
                                         ].rearrange("p (h d) -> p h d",
                                                     d=65)[:, :, 0:64]
                                srcv = B.psV[:, vp * 512:(vp + 1) * 512
                                             ].rearrange("p (h d) -> p h d",
                                                         d=64)
                                return v.tensor_copy(dst, srcv)
                            te = pl.step("vector", f_ev, inc=("dve", 1),
                                         waits=later(tv, t_const))
                            R.psV_free[vp] = te
                            last_evac["vb"] = te
                    else:
                        nslot = len(R.wqs_free)
                        for rt in range(CT):
                            kp = rt % 2
                            ws = rt % nslot

                            def f_wq(gp, rt=rt, ws=ws, B=B):
                                gp.dma_start(
                                    out=B.wqs[:, ws * CT * P:(ws + 1) * CT * P
                                              ].rearrange("p (c j) -> p c j",
                                                          j=P),
                                    in_=wqT[:, rt * P:(rt + 1) * P
                                            ].rearrange("(c p) j -> p c j",
                                                        p=P)
                                ).then_inc(SEM["wdma"], 16)
                                return None
                            twq = pl.step("gpsimd", f_wq, dma_n=1,
                                          dma_sem="wdma",
                                          waits=later(R.wqs_free[ws]))

                            def f_q(t, cp=cp, kp=kp, ws=ws, B=B):
                                last = None
                                for c in range(CT):
                                    last = t.matmul(
                                        B.psKQ[:, kp * 512:(kp + 1) * 512],
                                        B.wqs[:, ws * CT * P + c * P:
                                              ws * CT * P + (c + 1) * P],
                                        B.xnT[:, cp * CT * 512 + c * 512:
                                              cp * CT * 512 + (c + 1) * 512],
                                        start=(c == 0), stop=(c == CT - 1))
                                return last
                            tq = pl.step("tensor", f_q, inc=("pe", 1),
                                         waits=later(*ev_toks, R.psKQ_free[kp],
                                                     twq))
                            R.wqs_free[ws] = tq

                            def f_eq(v, rt=rt, ch=ch, kp=kp, B=B):
                                return v.tensor_copy(
                                    CX.qTv(rt)[:, ch * 512:(ch + 1) * 512],
                                    B.psKQ[:, kp * 512:(kp + 1) * 512])
                            te = pl.step("vector", f_eq, inc=("dve", 1),
                                         waits=[tq])
                            R.psKQ_free[kp] = te
                            last_evac["qT"] = te
                    R.xnT_free[cp] = ("pe", pl.val["pe"])

            esA = ExitStack()
            with esA:
                B1 = SimpleNamespace()
                B1.wk_sb = esA.enter_context(nc.sbuf_tensor("wk_sb", [P, CT * DKV], BF16))
                B1.wv_sb = esA.enter_context(nc.sbuf_tensor("wv_sb", [P, CT * DKV], BF16))
                B1.xt = esA.enter_context(nc.sbuf_tensor("xt", [P, 2 * D], F32))
                B1.xn = esA.enter_context(nc.sbuf_tensor("xn", [P, D], BF16))
                B1.xnT = esA.enter_context(nc.sbuf_tensor("xnT", [P, CT * 512], BF16))
                B1.xtn, B1.xnTn = 2, 1
                B1.st6 = esA.enter_context(nc.sbuf_tensor("st6", [P, 48], F32))
                B1.sub = esA.enter_context(nc.sbuf_tensor("sub", [P, 34], F32))
                B1.mn0 = esA.enter_context(nc.sbuf_tensor("mn0", [P, 2], F32))
                B1.vr0 = esA.enter_context(nc.sbuf_tensor("vr0", [P, 2], F32))
                B1.rs0 = esA.enter_context(nc.sbuf_tensor("rs0", [P, 2], F32))
                B1.nb0 = esA.enter_context(nc.sbuf_tensor("nb0", [P, 2], F32))
                B1.wqs = None
                B1.psTR = esA.enter_context(nc.psum_tensor("psTR", [P, 2 * 1024], BF16))
                B1.psKQ = esA.enter_context(nc.psum_tensor("psKQ", [P, 2 * 512], F32))
                B1.psV = esA.enter_context(nc.psum_tensor("psV", [P, 2 * 512], F32))
                B1.psX = esA.enter_context(nc.psum_tensor("psX", [P, 2 * 512], F32))

                def f_wkv(gp, B=B1):
                    gp.dma_start(out=B.wk_sb[:].rearrange("p (c j) -> p c j",
                                                          j=DKV),
                                 in_=wkT[:, :].rearrange("(c p) j -> p c j",
                                                         p=P)
                                 ).then_inc(SEM["wdma"], 16)
                    gp.dma_start(out=B.wv_sb[:].rearrange("p (c j) -> p c j",
                                                          j=DKV),
                                 in_=wvT[:, :].rearrange("(c p) j -> p c j",
                                                         p=P)
                                 ).then_inc(SEM["wdma"], 16)
                    return None
                t_wkv = pl.step("gpsimd", f_wkv, dma_n=2, dma_sem="wdma",
                                waits=[t_ident])

                R1 = SimpleNamespace(
                    st_free=[None, None],
                    xt_free=[None, None], xn_free=[None],
                    xnT_free=[None], psTR_free=[None, None],
                    psKQ_free=[None, None], psV_free=[None, None],
                    psX_free=[None, None], wqs_free=[None])
                emit_p0_pass(B1, R1, x, NCH, False, t_wkv)

            qT = esM.enter_context(nc.sbuf_tensor("qT", [P, CT * TOWN], BF16))

            def qTv(r):
                return qT[:, r * TOWN:(r + 1) * TOWN]
            CX.qTv = qTv

            kv_barrier = later(pl.cur("pe"), pl.cur("dve"), pl.cur("act"))
            esA2 = ExitStack()
            with esA2:
                B2 = SimpleNamespace()
                B2.xt = esA2.enter_context(nc.sbuf_tensor("xt2", [P, D], F32))
                B2.xn = esA2.enter_context(nc.sbuf_tensor("xn2", [P, D], BF16))
                B2.xnT = esA2.enter_context(nc.sbuf_tensor("xnT2", [P, CT * 512], BF16))
                B2.xtn, B2.xnTn = 1, 1
                B2.st6 = esA2.enter_context(nc.sbuf_tensor("st62", [P, 48], F32))
                B2.sub = esA2.enter_context(nc.sbuf_tensor("sub2", [P, 34], F32))
                B2.mn0 = esA2.enter_context(nc.sbuf_tensor("mn02", [P, 2], F32))
                B2.vr0 = esA2.enter_context(nc.sbuf_tensor("vr02", [P, 2], F32))
                B2.rs0 = esA2.enter_context(nc.sbuf_tensor("rs02", [P, 2], F32))
                B2.nb0 = esA2.enter_context(nc.sbuf_tensor("nb02", [P, 2], F32))
                B2.wqs = esA2.enter_context(nc.sbuf_tensor("wqs", [P, 2 * CT * P], BF16))
                B2.wk_sb = None
                B2.wv_sb = None
                B2.psTR = esA2.enter_context(nc.psum_tensor("psTR2", [P, 2 * 1024], BF16))
                B2.psKQ = esA2.enter_context(nc.psum_tensor("psKQ2", [P, 2 * 512], F32))
                B2.psV = None
                B2.psX = esA2.enter_context(nc.psum_tensor("psX2", [P, 2 * 512], F32))

                R2 = SimpleNamespace(
                    st_free=[kv_barrier, kv_barrier],
                    xt_free=[kv_barrier], xn_free=[kv_barrier],
                    xnT_free=[kv_barrier],
                    psTR_free=[kv_barrier, kv_barrier],
                    psKQ_free=[kv_barrier, kv_barrier],
                    psV_free=[None, None],
                    psX_free=[kv_barrier, kv_barrier],
                    wqs_free=[kv_barrier, kv_barrier])
                emit_p0_pass(B2, R2, xq, 2, True, None)

            dbg_gate = None
            if debug:
                def f_d0(sync):
                    sync.dma_start(out=dbg["kT"][:, :], in_=kT[:]
                                   ).then_inc(SEM["odma"], 16)
                    sync.dma_start(out=dbg["vb"][:, :], in_=vb[:]
                                   ).then_inc(SEM["odma"], 16)
                    sync.dma_start(out=dbg["qT"][:, :], in_=qT[:]
                                   ).then_inc(SEM["odma"], 16)
                    sync.dma_start(out=dbg["y1"][:, :], in_=yT[:]
                                   ).then_inc(SEM["odma"], 16)
                    return None
                dbg_gate = pl.step("sync", f_d0, dma_n=4, dma_sem="odma",
                                   waits=later(pl.cur("dve")))
                n_out_dma[0] += 4

            marks.append((len(pl.steps), dict(pl.val)))
            esB = ExitStack()
            with esB:
                mem_sb = esB.enter_context(nc.sbuf_tensor("mem_sb", [P, 4 * DKV], BF16))
                mn_sb = esB.enter_context(nc.sbuf_tensor("mn_sb", [P, 4], BF16))
                sig = esB.enter_context(nc.sbuf_tensor("sig", [P, 4 * TOWN], BF16))
                tmin = esB.enter_context(nc.sbuf_tensor("tmin", [P, TOWN], F32))
                e32 = esB.enter_context(nc.sbuf_tensor("e32", [P, TOWN], F32))
                tmax = esB.enter_context(nc.sbuf_tensor("tmax", [P, TOWN], BF16))
                dn = esB.enter_context(nc.sbuf_tensor("dn", [1, TOWN], F32))
                rw = esB.enter_context(nc.sbuf_tensor("rw", [1, TOWN], F32))
                rwb = esB.enter_context(nc.sbuf_tensor("rwb", [P, TOWN], F32))
                tmpa = esB.enter_context(nc.sbuf_tensor("tmpa", [P, TOWN], F32))
                BC.bhi = esB.enter_context(nc.sbuf_tensor("bhi1", [1, TOWN], BF16))
                BC.blo = esB.enter_context(nc.sbuf_tensor("blo1", [1, TOWN], BF16))
                BC.blt = esB.enter_context(nc.sbuf_tensor("blt1", [1, TOWN], F32))
                BC.free = None
                psM = esB.enter_context(nc.psum_tensor("psM", [P, 2 * TOWN], F32))
                psDen = esB.enter_context(nc.psum_tensor("psDen", [1, TOWN], F32))
                psB2 = esB.enter_context(nc.psum_tensor("psB2", [P, TOWN], F32))

                def f_mload(gp):
                    gp.dma_start(out=mem_sb[:].rearrange("p (k j) -> p k j",
                                                         j=DKV),
                                 in_=mem[:, :].rearrange("(k p) j -> p k j",
                                                         p=P)
                                 ).then_inc(SEM["wdma"], 16)
                    gp.dma_start(out=mn_sb[:], in_=memnorm[:, :]
                                 ).then_inc(SEM["wdma"], 16)
                    return None
                t_mem = pl.step("gpsimd", f_mload, dma_n=2, dma_sem="wdma",
                                waits=later(kv_barrier))

                q_done = later(pl.cur("dve"), dbg_gate)
                psM_free = [None, None]
                sig_consumed = later(pl.cur("pe"))
                rw_consumed = None
                for qg in range(NQ):
                    sig_toks = []
                    for rj in range(4):
                        rt = qg * 4 + rj

                        def f_min(gp, rt=rt):
                            return gp.tensor_scalar(tmin[:], qTv(rt), 0.0, 0.0,
                                                    ALU.min, ALU.add)
                        t1 = pl.step("gpsimd", f_min, inc=("gp", 1),
                                     waits=later(q_done, sig_consumed))

                        def f_exp(a):
                            return a.activation(e32[:], tmin[:], AF.Exp)
                        t2 = pl.step("scalar", f_exp, inc=("act", 1),
                                     waits=[t1])

                        def f_max(gp, rt=rt):
                            return gp.tensor_scalar(tmax[:], qTv(rt), 0.0, 0.0,
                                                    ALU.max, ALU.add)
                        t3 = pl.step("gpsimd", f_max, inc=("gp", 1),
                                     waits=later(q_done, sig_consumed))

                        def f_sig(v, rj=rj):
                            return v.scalar_tensor_tensor(
                                sig[:, rj * TOWN:(rj + 1) * TOWN], e32[:],
                                -1.0, tmax[:], ALU.add, ALU.add)
                        t4 = pl.step("vector", f_sig, inc=("dve", 1),
                                     waits=[t2, t3])
                        sig_toks.append(t4)

                    def f_dn(t):
                        last = None
                        for h in range(2):
                            for kt in range(4):
                                last = t.matmul(
                                    psDen[0:1, h * 512:(h + 1) * 512],
                                    mn_sb[:, kt:kt + 1],
                                    sig[:, kt * TOWN + h * 512:
                                        kt * TOWN + h * 512 + 512],
                                    start=(kt == 0), stop=(kt == 3))
                        return last
                    tdn = pl.step("tensor", f_dn, inc=("pe", 1),
                                  waits=later(*sig_toks, t_mem, rw_consumed))

                    def f_edn(v):
                        return v.tensor_copy(dn[:], psDen[:])
                    tedn = pl.step("vector", f_edn, inc=("dve", 1),
                                   waits=[tdn])

                    def f_rec(v):
                        return v.reciprocal(rw[:], dn[:])
                    trec = pl.step("vector", f_rec, inc=("dve", 1),
                                   waits=[tedn])

                    def f_rw(v):
                        return v.tensor_scalar(rw[:], rw[:], wmix[0:1, 0:1],
                                               0.0, ALU.mult, ALU.add)
                    trw = pl.step("vector", f_rw, inc=("dve", 1),
                                  waits=later(trec, t_s4))
                    tb = emit_bcast(rw[0:1, :], TOWN, psB2, [trw])
                    rw_consumed = tb

                    def f_erb(v):
                        return v.tensor_copy(rwb[:], psB2[:])
                    terb = pl.step("vector", f_erb, inc=("dve", 1), waits=[tb])

                    am_toks = []
                    for jt in range(4):
                        mp = jt % 2
                        qr2 = qg * 4 + jt

                        def f_am(t, jt=jt, mp=mp):
                            last = None
                            for h in range(2):
                                for kt in range(4):
                                    last = t.matmul(
                                        psM[:, mp * TOWN + h * 512:
                                            mp * TOWN + h * 512 + 512],
                                        mem_sb[:, kt * DKV + jt * P:
                                               kt * DKV + (jt + 1) * P],
                                        sig[:, kt * TOWN + h * 512:
                                            kt * TOWN + h * 512 + 512],
                                        start=(kt == 0), stop=(kt == 3))
                            return last
                        tam = pl.step("tensor", f_am, inc=("pe", 1),
                                      waits=later(*sig_toks, psM_free[mp],
                                                  t_mem))
                        am_toks.append(tam)

                        def f_mul(v, mp=mp):
                            return v.tensor_mul(
                                tmpa[:], psM[:, mp * TOWN:(mp + 1) * TOWN],
                                rwb[:])
                        tm1 = pl.step("vector", f_mul, inc=("dve", 1),
                                      waits=later(tam, terb))
                        psM_free[mp] = tm1

                        def f_add(v, qr2=qr2):
                            return v.tensor_add(yTv(qr2), yTv(qr2), tmpa[:])
                        pl.step("vector", f_add, inc=("dve", 1),
                                waits=later(tm1, last_evac["yT"]))
                    sig_consumed = later(tdn, *am_toks)

            marks.append((len(pl.steps), dict(pl.val)))
            esC = ExitStack()
            with esC:
                PT = esC.enter_context(nc.sbuf_tensor("PT", [P, 4 * 512], BF16))
                rsr = esC.enter_context(nc.sbuf_tensor("rsr", [1, 1024], F32))
                rsb = esC.enter_context(nc.sbuf_tensor("rsb", [P, 2 * 512], F32))
                tmp2 = esC.enter_context(nc.sbuf_tensor("tmp2", [P, 2 * 512], F32))
                stg = esC.enter_context(nc.sbuf_tensor("stg", [P, 2 * 1024], F32))
                BC.bhi = esC.enter_context(nc.sbuf_tensor("bhi2", [1, 1024], BF16))
                BC.blo = esC.enter_context(nc.sbuf_tensor("blo2", [1, 1024], BF16))
                BC.blt = esC.enter_context(nc.sbuf_tensor("blt2", [1, 1024], F32))
                BC.free = None
                psS = esC.enter_context(nc.psum_tensor("psS", [P, 4 * 512], F32))
                psPV = esC.enter_context(nc.psum_tensor("psPV", [P, 2 * 512], F32))
                psBC = esC.enter_context(nc.psum_tensor("psBC", [P, 2 * 512], F32))

                kv_ready = later(pl.cur("dve"))
                psS_free = [None, None]
                PT_free = [None, None]
                psPV_free = None
                rsr_free = None
                stg_free = [None, None]
                pending_tail = [None]
                iters = [(qg, hp, chn) for qg in range(NQ)
                         for hp in range(NH // 2) for chn in range(2)]
                for it, (qg, hp, chn) in enumerate(iters):
                        qr = qg * 4 + hp
                        if True:
                            pv_last = None

                            def emit_pv(kt, g):
                                nonlocal pv_last

                                def f_pv(t, hp=hp, kt=kt, g=g):
                                    t.matmul(
                                        psPV[0:65, 0:512],
                                        vb[:, kt * 520 + (2 * hp) * 65:
                                           kt * 520 + (2 * hp) * 65 + 65],
                                        PT[:, (2 * g) * 512:(2 * g + 1) * 512],
                                        start=(kt == 0),
                                        stop=(kt == NKT - 1))
                                    return t.matmul(
                                        psPV[0:65, 512:1024],
                                        vb[:, kt * 520 + (2 * hp + 1) * 65:
                                           kt * 520 + (2 * hp + 1) * 65 + 65],
                                        PT[:, (2 * g + 1) * 512:
                                           (2 * g + 2) * 512],
                                        start=(kt == 0),
                                        stop=(kt == NKT - 1))
                                tpv = pl.step("tensor", f_pv, inc=("pe", 1),
                                              waits=later(exp_toks[kt],
                                                          psPV_free
                                                          if kt == 0 else None))
                                PT_free[g] = tpv
                                pv_last = tpv

                            exp_toks = {}
                            for kt in range(NKT):
                                g = kt % 2

                                def f_sc(t, hp=hp, qr=qr, chn=chn, kt=kt, g=g):
                                    t.matmul(
                                        psS[:, (2 * g) * 512:
                                            (2 * g + 1) * 512],
                                        kTv(hp)[0:64, kt * P:(kt + 1) * P],
                                        qTv(qr)[0:64,
                                                chn * 512:(chn + 1) * 512],
                                        start=True, stop=True,
                                        tile_position=(0, 0))
                                    return t.matmul(
                                        psS[:, (2 * g + 1) * 512:
                                            (2 * g + 2) * 512],
                                        kTv(hp)[64:128, kt * P:(kt + 1) * P],
                                        qTv(qr)[64:128,
                                                chn * 512:(chn + 1) * 512],
                                        start=True, stop=True,
                                        tile_position=(64, 0))
                                tsc = pl.step("tensor", f_sc, inc=("pe", 1),
                                              waits=later(kv_ready,
                                                          psS_free[g]))

                                def f_exp(a, g=g):
                                    return a.activation(
                                        PT[:, (2 * g) * 512:(2 * g + 2) * 512],
                                        psS[:, (2 * g) * 512:
                                            (2 * g + 2) * 512],
                                        AF.Exp, scale=SCALE)
                                if kt >= 2:
                                    emit_pv(kt - 2, (kt - 2) % 2)
                                if kt == 12 and pending_tail[0] is not None:
                                    pending_tail[0]()
                                    pending_tail[0] = None
                                tex = pl.step("scalar", f_exp, inc=("act", 1),
                                              waits=later(tsc, PT_free[g]))
                                psS_free[g] = tex
                                exp_toks[kt] = tex
                            emit_pv(NKT - 2, (NKT - 2) % 2)
                            emit_pv(NKT - 1, (NKT - 1) % 2)

                            si = (qg * 8 + hp * 2 + chn) % 2
                            def f_stg(v, si=si):
                                return v.tensor_copy(
                                    stg[:, si * 1024:(si + 1) * 1024],
                                    psPV[:, 0:1024])
                            tstg = pl.step("vector", f_stg, inc=("dve", 1),
                                           waits=later(pv_last, stg_free[si]))
                            psPV_free = tstg
                            sv = stg[:, si * 1024:(si + 1) * 1024]

                            def make_tail(sv=sv, si=si, qr=qr, chn=chn,
                                          tstg=tstg):
                                def emit_tail():
                                    nonlocal rsr_free

                                    def f_rec(v):
                                        return v.reciprocal(rsr[:],
                                                            sv[64:65, :])
                                    tr1 = pl.step("vector", f_rec,
                                                  inc=("dve", 1),
                                                  waits=later(tstg, rsr_free))

                                    def f_rs(v):
                                        return v.tensor_scalar(
                                            rsr[:], rsr[:], wmix[0:1, 1:2],
                                            0.0, ALU.mult, ALU.add)
                                    tr2 = pl.step("vector", f_rs,
                                                  inc=("dve", 1),
                                                  waits=later(tr1, t_s4))
                                    tb = emit_bcast(rsr[0:1, :], 1024, psBC,
                                                    [tr2])
                                    rsr_free = tb

                                    def f_eb(v):
                                        return v.tensor_copy(rsb[:],
                                                             psBC[:, 0:1024])
                                    teb = pl.step("vector", f_eb,
                                                  inc=("dve", 1), waits=[tb])

                                    head_toks = []
                                    for hh in range(2):
                                        def f_mul(v, hh=hh):
                                            return v.tensor_mul(
                                                tmp2[hh * 64:(hh + 1) * 64,
                                                     hh * 512:(hh + 1) * 512],
                                                sv[0:64,
                                                   hh * 512:(hh + 1) * 512],
                                                rsb[0:64,
                                                    hh * 512:(hh + 1) * 512])
                                        tm1 = pl.step("vector", f_mul,
                                                      inc=("dve", 1),
                                                      waits=[teb])

                                        def f_add(v, hh=hh):
                                            dst = yTv(qr)[
                                                hh * 64:(hh + 1) * 64,
                                                chn * 512:(chn + 1) * 512]
                                            return v.tensor_add(
                                                dst, dst,
                                                tmp2[hh * 64:(hh + 1) * 64,
                                                     hh * 512:(hh + 1) * 512])
                                        tm2 = pl.step("vector", f_add,
                                                      inc=("dve", 1),
                                                      waits=[tm1])
                                        head_toks.append(tm2)
                                    stg_free[si] = later(*head_toks)
                                return emit_tail
                            pending_tail[0] = make_tail()

                if pending_tail[0] is not None:
                    pending_tail[0]()
                    pending_tail[0] = None

            dbg_gate2 = None
            if debug:
                def f_d2(sync):
                    sync.dma_start(out=dbg["y2"][:, :], in_=yT[:]
                                   ).then_inc(SEM["odma"], 16)
                    return None
                dbg_gate2 = pl.step("sync", f_d2, dma_n=1, dma_sem="odma",
                                    waits=later(pl.cur("dve")))
                n_out_dma[0] += 1

        marks.append((len(pl.steps), dict(pl.val)))
        barrierF = later(pl.cur("pe"), pl.cur("dve"), pl.cur("act"),
                         dbg_gate2)
        esF = ExitStack()
        with esF:
            ynT = esF.enter_context(nc.sbuf_tensor("ynT", [P, CT * TOWN], BF16))
            twT = esF.enter_context(nc.sbuf_tensor("twT", [8, TOWN], F32))
            twr = esF.enter_context(nc.sbuf_tensor("twr", [1, TOWN], F32))
            twb = esF.enter_context(nc.sbuf_tensor("twb", [P, 2 * TOWN], F32))
            gw_sb = esF.enter_context(nc.sbuf_tensor("gw_sb", [P, CT * E], F32))
            ones32 = esF.enter_context(nc.sbuf_tensor("ones32", [1, P], F32))
            BC.bhi = esF.enter_context(nc.sbuf_tensor("bhi3", [1, TOWN], BF16))
            BC.blo = esF.enter_context(nc.sbuf_tensor("blo3", [1, TOWN], BF16))
            BC.blt = esF.enter_context(nc.sbuf_tensor("blt3", [1, TOWN], F32))
            BC.free = None

            def ynTv(rt):
                return ynT[:, rt * TOWN:(rt + 1) * TOWN]

            def f_gw(sync):
                sync.dma_start(out=gw_sb[:].rearrange("p (c e) -> p c e", e=E),
                               in_=gwT[:, :].rearrange("(c p) e -> p c e", p=P)
                               ).then_inc(SEM["xdma"], 16)
                return None
            t_gw = pl.step("sync", f_gw, dma_n=1, dma_sem="xdma",
                           waits=barrierF)

            def f_o32(v):
                return v.memset(ones32[:], 1.0)
            t_o32 = pl.step("vector", f_o32, inc=("dve", 1), waits=barrierF)

            esD = ExitStack()
            with esD:
                ynat = esD.enter_context(nc.sbuf_tensor("ynat", [P, 2 * D], F32))
                ynn = esD.enter_context(nc.sbuf_tensor("ynn", [P, D], BF16))
                st6b = esD.enter_context(nc.sbuf_tensor("st6b", [P, 24], F32))
                subb = esD.enter_context(nc.sbuf_tensor("subb", [P, 17], F32))
                vrb = esD.enter_context(nc.sbuf_tensor("vrb", [P, 1], F32))
                nbb = esD.enter_context(nc.sbuf_tensor("nbb", [P, 1], F32))
                L0T = esD.enter_context(nc.sbuf_tensor("L0T", [8, TOWN], F32))
                sgr = esD.enter_context(nc.sbuf_tensor("sgr", [8, 1], F32))
                sbc = esD.enter_context(nc.sbuf_tensor("sbc", [P, 8], F32))
                lgt = esD.enter_context(nc.sbuf_tensor("lgt", [P, 8], F32))
                lg2 = esD.enter_context(nc.sbuf_tensor("lg2", [P, 8], F32))
                msr = esD.enter_context(nc.sbuf_tensor("msr", [P, 1], F32))
                mm1 = esD.enter_context(nc.sbuf_tensor("mm1", [P, 1], F32))
                mm2 = esD.enter_context(nc.sbuf_tensor("mm2", [P, 1], F32))
                ee = esD.enter_context(nc.sbuf_tensor("ee", [P, 1], F32))
                ee2 = esD.enter_context(nc.sbuf_tensor("ee2", [P, 1], F32))
                w1 = esD.enter_context(nc.sbuf_tensor("w1", [P, 1], F32))
                w2 = esD.enter_context(nc.sbuf_tensor("w2", [P, 1], F32))
                mk1 = esD.enter_context(nc.sbuf_tensor("mk1", [P, 8], F32))
                mk2 = esD.enter_context(nc.sbuf_tensor("mk2", [P, 8], F32))
                twn = esD.enter_context(nc.sbuf_tensor("twn", [P, 64], F32))
                psT3 = esD.enter_context(nc.psum_tensor("psT3", [P, 2048], F32))
                psTN = esD.enter_context(nc.psum_tensor("psTN", [P, 2048], BF16))
                psR = esD.enter_context(nc.psum_tensor("psR", [8, 512], F32))
                psR2 = esD.enter_context(nc.psum_tensor("psR2", [P, 8], F32))

                psT3_free = [None] * 4
                psTN_free = [None, None]
                ynat_free = [None, None]
                ynn_free = [None]
                ynT_toks = []
                for tt in range(8):
                    tp = tt % 2
                    ev_toks = []
                    for g in range(4):
                        def f_tr(t, g=g, tt=tt):
                            last = None
                            for idx in range(4):
                                rt = g * 4 + idx
                                last = t.transpose(
                                    psT3[:, g * 512 + idx * 128:
                                         g * 512 + (idx + 1) * 128],
                                    yTv(rt)[:, tt * P:(tt + 1) * P],
                                    ident32[:])
                            return last
                        ttr = pl.step("tensor", f_tr, inc=("pe", 1),
                                      waits=later(barrierF, psT3_free[g],
                                                  ynat_free[tp]))

                        def f_ev(v, g=g, tp=tp):
                            return v.tensor_copy(
                                ynat[:, tp * D + g * 512:
                                     tp * D + (g + 1) * 512],
                                psT3[:, g * 512:(g + 1) * 512])
                        tev = pl.step("vector", f_ev, inc=("dve", 1),
                                      waits=[ttr])
                        psT3_free[g] = tev
                        ev_toks.append(tev)
                    ynap = ynat[:, tp * D:(tp + 1) * D]
                    t_rs, t_nb = emit_ln_stats(
                        ynap, st6b, subb, m2s[:, tt:tt + 1], vrb[:, 0:1],
                        r2s[:, tt:tt + 1], nbb[:, 0:1], ev_toks)

                    def f_aff(a, ynap=ynap, tt=tt):
                        return a.activation(ynn[:], ynap, AF.Identity,
                                            bias=nbb[:, 0:1],
                                            scale=r2s[:, tt:tt + 1])
                    t_yn = pl.step("scalar", f_aff, inc=("act", 1),
                                   waits=later(t_rs, t_nb, ynn_free[0]))
                    ynat_free[tp] = later(t_yn, t_rs)

                    tr_toks = []
                    for g in range(2):
                        def f_trn(t, g=g):
                            last = None
                            for idx in range(8):
                                c = g * 8 + idx
                                last = t.transpose(
                                    psTN[:, g * 1024 + idx * P:
                                         g * 1024 + (idx + 1) * P],
                                    ynn[:, c * P:(c + 1) * P], identbf[:])
                            return last
                        ttr = pl.step("tensor", f_trn, inc=("pe", 1),
                                      waits=later(t_yn, psTN_free[g]))

                        def f_evn(v, g=g, tt=tt):
                            last = None
                            for idx in range(8):
                                c = g * 8 + idx
                                last = v.tensor_copy(
                                    ynTv(c)[:, tt * P:(tt + 1) * P],
                                    psTN[:, g * 1024 + idx * P:
                                         g * 1024 + (idx + 1) * P])
                            return last
                        tev = pl.step("vector", f_evn, inc=("dve", 1),
                                      waits=[ttr])
                        psTN_free[g] = tev
                        tr_toks.append(ttr)
                        ynT_toks.append(tev)
                    ynn_free[0] = later(*tr_toks)
                ynT_ready = later(*ynT_toks)

                l0_toks = []
                psR_free = None
                for h in range(2):
                    def f_l0(t, h=h):
                        last = None
                        for c in range(CT):
                            last = t.matmul(
                                psR[:, :], gw_sb[:, c * E:(c + 1) * E],
                                yTv(c)[:, h * 512:(h + 1) * 512],
                                start=(c == 0), stop=(c == CT - 1))
                        return last
                    tl0 = pl.step("tensor", f_l0, inc=("pe", 1),
                                  waits=later(t_gw, barrierF, psR_free))

                    def f_el0(v, h=h):
                        return v.tensor_copy(L0T[:, h * 512:(h + 1) * 512],
                                             psR[:])
                    te = pl.step("vector", f_el0, inc=("dve", 1), waits=[tl0])
                    psR_free = te
                    l0_toks.append(te)

                def f_se(t):
                    last = None
                    for c in range(CT):
                        last = t.matmul(psR[:, 0:1],
                                        gw_sb[:, c * E:(c + 1) * E],
                                        onescol[:],
                                        start=(c == 0), stop=(c == CT - 1))
                    return last
                tse = pl.step("tensor", f_se, inc=("pe", 1),
                              waits=later(t_gw, psR_free))

                def f_ese(v):
                    return v.tensor_copy(sgr[:], psR[:, 0:1])
                tese = pl.step("vector", f_ese, inc=("dve", 1), waits=[tse])
                psR_free = tese

                def f_trs(t):
                    return t.transpose(psR2[0:1, 0:8], sgr[:],
                                       ident32[0:8, 0:8])
                ttrs = pl.step("tensor", f_trs, inc=("pe", 1),
                               waits=later(tese, t_ident))

                def f_etrs(v):
                    return v.tensor_copy(BC.blt[:, 0:8], psR2[0:1, 0:8])
                tets = pl.step("vector", f_etrs, inc=("dve", 1),
                               waits=later(ttrs, BC.free))

                def f_sbm(t):
                    return t.matmul(psR2[:, 0:8], ones32[:],
                                    BC.blt[0:1, 0:8], start=True, stop=True)
                tsbm = pl.step("tensor", f_sbm, inc=("pe", 1),
                               waits=later(tets, t_o32))

                def f_esb(v):
                    return v.tensor_copy(sbc[:], psR2[:, 0:8])
                tesb = pl.step("vector", f_esb, inc=("dve", 1), waits=[tsbm])
                BC.free = tsbm
                psR2_free = tesb

                tw_toks = []
                for tt in range(8):
                    def f_trl(t, tt=tt):
                        return t.transpose(psR2[:, 0:8],
                                           L0T[:, tt * P:(tt + 1) * P],
                                           ident32[0:8, 0:8])
                    ttl = pl.step("tensor", f_trl, inc=("pe", 1),
                                  waits=later(*l0_toks, psR2_free, t_ident))

                    def f_elg(v):
                        return v.tensor_copy(lgt[:], psR2[:, 0:8])
                    tlg = pl.step("vector", f_elg, inc=("dve", 1),
                                  waits=[ttl])
                    psR2_free = tlg

                    def f_msr(v, tt=tt):
                        return v.scalar_tensor_tensor(
                            msr[:], m2s[:, tt:tt + 1], -1.0,
                            r2s[:, tt:tt + 1], ALU.mult, ALU.mult)
                    t1 = pl.step("vector", f_msr, inc=("dve", 1),
                                 waits=[tlg])

                    def f_lg2(v, tt=tt):
                        return v.tensor_scalar(lg2[:], lgt[:],
                                               r2s[:, tt:tt + 1], 0.0,
                                               ALU.mult, ALU.add)
                    t2 = pl.step("vector", f_lg2, inc=("dve", 1), waits=[tlg])

                    def f_lg3(v):
                        return v.scalar_tensor_tensor(lg2[:], sbc[:], msr[:],
                                                      lg2[:], ALU.mult,
                                                      ALU.add)
                    t3 = pl.step("vector", f_lg3, inc=("dve", 1),
                                 waits=later(t1, t2, tesb))

                    def f_m1(v):
                        return v.reduce_max(mm1[:], lg2[:], axis=AX)
                    t4 = pl.step("vector", f_m1, inc=("dve", 1), waits=[t3])

                    def f_mk1(v):
                        return v.tensor_scalar(mk1[:], lg2[:], mm1[:], 0.0,
                                               ALU.subtract, ALU.is_ge)
                    t5 = pl.step("vector", f_mk1, inc=("dve", 1), waits=[t4])

                    def f_l2(v):
                        return v.scalar_tensor_tensor(lgt[:], mk1[:], -1e30,
                                                      lg2[:], ALU.mult,
                                                      ALU.add)
                    t6 = pl.step("vector", f_l2, inc=("dve", 1), waits=[t5])

                    def f_m2(v):
                        return v.reduce_max(mm2[:], lgt[:], axis=AX)
                    t7 = pl.step("vector", f_m2, inc=("dve", 1), waits=[t6])

                    def f_mk2(v):
                        return v.tensor_scalar(mk2[:], lgt[:], mm2[:], 0.0,
                                               ALU.subtract, ALU.is_ge)
                    t8 = pl.step("vector", f_mk2, inc=("dve", 1), waits=[t7])

                    def f_d(v):
                        return v.tensor_scalar(ee[:], mm2[:], mm1[:], 0.0,
                                               ALU.subtract, ALU.add)
                    t9 = pl.step("vector", f_d, inc=("dve", 1),
                                 waits=[t7, t4])

                    def f_ee(a):
                        return a.activation(ee2[:], ee[:], AF.Exp)
                    t10 = pl.step("scalar", f_ee, inc=("act", 1), waits=[t9])

                    def f_w1(v):
                        return v.tensor_scalar(w1[:], ee2[:], 1.0, 0.0,
                                               ALU.add, ALU.add)
                    t11 = pl.step("vector", f_w1, inc=("dve", 1), waits=[t10])

                    def f_w1r(v):
                        return v.reciprocal(w1[:], w1[:])
                    t12 = pl.step("vector", f_w1r, inc=("dve", 1),
                                  waits=[t11])

                    def f_w2(v):
                        return v.tensor_mul(w2[:], ee2[:], w1[:])
                    t13 = pl.step("vector", f_w2, inc=("dve", 1), waits=[t12])

                    def f_t1(v):
                        return v.tensor_scalar(mk1[:], mk1[:], w1[:], 0.0,
                                               ALU.mult, ALU.add)
                    t14 = pl.step("vector", f_t1, inc=("dve", 1),
                                  waits=[t12, t8])

                    def f_tw(v, tt=tt):
                        return v.scalar_tensor_tensor(
                            twn[:, tt * 8:(tt + 1) * 8], mk2[:], w2[:],
                            mk1[:], ALU.mult, ALU.add)
                    t15 = pl.step("vector", f_tw, inc=("dve", 1),
                                  waits=[t13, t14])
                    tw_toks.append(t15)

                if debug:
                    def f_dtw(sync):
                        sync.dma_start(out=dbg["tw"][:, :], in_=twn[:]
                                       ).then_inc(SEM["odma"], 16)
                        sync.dma_start(out=dbg["ynT"][:, :], in_=ynT[:]
                                       ).then_inc(SEM["odma"], 16)
                        return None
                    pl.step("sync", f_dtw, dma_n=2, dma_sem="odma",
                            waits=later(*tw_toks, ynT_ready))
                    n_out_dma[0] += 2

                psR_free2 = psR_free
                twT_toks = []
                for half in range(2):
                    def f_trt(t, half=half):
                        last = None
                        for j in range(4):
                            tt = half * 4 + j
                            last = t.transpose(
                                psR[:, j * 128:(j + 1) * 128],
                                twn[:, tt * 8:(tt + 1) * 8], ident32[:])
                        return last
                    ttt = pl.step("tensor", f_trt, inc=("pe", 1),
                                  waits=later(*tw_toks, psR_free2, t_ident))

                    def f_ett(v, half=half):
                        return v.tensor_copy(
                            twT[:, half * 512:(half + 1) * 512], psR[:])
                    tet = pl.step("vector", f_ett, inc=("dve", 1),
                                  waits=[ttt])
                    psR_free2 = tet
                    twT_toks.append(tet)
                twT_ready = later(*twT_toks)

            marks.append((len(pl.steps), dict(pl.val)))
            p3_done = later(pl.cur("pe"), pl.cur("dve"), pl.cur("act"))
            esE = ExitStack()
            with esE:
                gs = esE.enter_context(nc.sbuf_tensor("gs", [P, 4 * CT * P], BF16))
                us = esE.enter_context(nc.sbuf_tensor("us", [P, 4 * CT * P], BF16))
                ds = esE.enter_context(nc.sbuf_tensor("ds", [P, 4 * FRT * P], BF16))
                hbuf = esE.enter_context(nc.sbuf_tensor("hbuf", [P, FRT * TOWN], BF16))
                sgl = esE.enter_context(nc.sbuf_tensor("sgl", [P, 2 * TOWN], BF16))
                usc = esE.enter_context(nc.sbuf_tensor("usc", [P, 2 * TOWN], BF16))
                psG = esE.enter_context(nc.psum_tensor("psG", [P, TOWN], F32))
                psU = esE.enter_context(nc.psum_tensor("psU", [P, TOWN], F32))
                psDb = esE.enter_context(nc.psum_tensor("psDb", [P, 2 * TOWN], F32))

                gs_free = [p3_done] * 4
                us_free = [p3_done] * 4
                ds_free = [p3_done] * 4
                psG_free = [p3_done]
                psU_free = [p3_done]
                psD_free = [p3_done, p3_done]
                twb_free = [p3_done, p3_done]
                twr_free = [p3_done]
                h_free = [p3_done]
                yadd = [None] * CT
                twb_tok = [None, None]

                for e in range(E):
                    ep = e % 2

                    def f_twr(gp, e=e):
                        gp.dma_start(out=twr[:], in_=twT[e:e + 1, :]
                                     ).then_inc(SEM["wdma"], 16)
                        return None
                    ttwr = pl.step("gpsimd", f_twr, dma_n=1, dma_sem="wdma",
                                   waits=later(twT_ready, twr_free[0]))
                    tbc = emit_bcast(twr[0:1, 0:512], 512, psG,
                                     later(ttwr, psG_free[0]))
                    tbc2 = emit_bcast(twr[0:1, 512:1024], 512, psU,
                                      later(ttwr, psU_free[0]))
                    twr_free[0] = later(tbc, tbc2)

                    def f_etw(v, ep=ep):
                        v.tensor_copy(twb[:, ep * TOWN:ep * TOWN + 512],
                                      psG[:, 0:512])
                        return v.tensor_copy(
                            twb[:, ep * TOWN + 512:(ep + 1) * TOWN],
                            psU[:, 0:512])
                    tetw = pl.step("vector", f_etw, inc=("dve", 1),
                                   waits=later(tbc, tbc2, twb_free[ep]))
                    twb_tok[ep] = tetw
                    psG_free[0] = tetw
                    psU_free[0] = tetw

                    h_toks = []
                    for fr in range(FRT):
                        fpar = fr % 2
                        slot = fr % 4

                        def f_dg(gp, e=e, fr=fr, slot=slot):
                            gp.dma_start(
                                out=gs[:, slot * CT * P:(slot + 1) * CT * P
                                       ].rearrange("p (c j) -> p c j", j=P),
                                in_=gpT[e, :, fr * P:(fr + 1) * P
                                        ].rearrange("(c p) j -> p c j", p=P)
                            ).then_inc(SEM["wdma"], 16)
                            return None
                        tdg = pl.step("gpsimd", f_dg, dma_n=1, dma_sem="wdma",
                                      waits=later(gs_free[slot]))

                        def f_du(gp, e=e, fr=fr, slot=slot):
                            gp.dma_start(
                                out=us[:, slot * CT * P:(slot + 1) * CT * P
                                       ].rearrange("p (c j) -> p c j", j=P),
                                in_=upT[e, :, fr * P:(fr + 1) * P
                                        ].rearrange("(c p) j -> p c j", p=P)
                            ).then_inc(SEM["wdma"], 16)
                            return None
                        tdu = pl.step("gpsimd", f_du, dma_n=1, dma_sem="wdma",
                                      waits=later(us_free[slot]))

                        def f_g(t, slot=slot):
                            last = None
                            for h in range(2):
                                for c in range(CT):
                                    last = t.matmul(
                                        psG[:, h * 512:(h + 1) * 512],
                                        gs[:, slot * CT * P + c * P:
                                           slot * CT * P + (c + 1) * P],
                                        ynTv(c)[:, h * 512:(h + 1) * 512],
                                        start=(c == 0), stop=(c == CT - 1))
                            return last
                        tg = pl.step("tensor", f_g, inc=("pe", 1),
                                     waits=later(tdg, ynT_ready, psG_free[0]))
                        gs_free[slot] = tg

                        def f_silu(a, fpar=fpar):
                            return a.activation(
                                sgl[:, fpar * TOWN:(fpar + 1) * TOWN],
                                psG[:], AF.Silu)
                        tsl = pl.step("scalar", f_silu, inc=("act", 1),
                                      waits=later(tg, h_free[0]))
                        psG_free[0] = tsl

                        def f_u(t, slot=slot):
                            last = None
                            for h in range(2):
                                for c in range(CT):
                                    last = t.matmul(
                                        psU[:, h * 512:(h + 1) * 512],
                                        us[:, slot * CT * P + c * P:
                                           slot * CT * P + (c + 1) * P],
                                        ynTv(c)[:, h * 512:(h + 1) * 512],
                                        start=(c == 0), stop=(c == CT - 1))
                            return last
                        tu = pl.step("tensor", f_u, inc=("pe", 1),
                                     waits=later(tdu, ynT_ready, psU_free[0]))
                        us_free[slot] = tu

                        def f_usc(v, fpar=fpar, ep=ep):
                            return v.tensor_mul(
                                usc[:, fpar * TOWN:(fpar + 1) * TOWN], psU[:],
                                twb[:, ep * TOWN:(ep + 1) * TOWN])
                        tus = pl.step("vector", f_usc, inc=("dve", 1),
                                      waits=later(tu, twb_tok[ep], h_free[0]))
                        psU_free[0] = tus

                        def f_h(v, fr=fr, fpar=fpar):
                            return v.tensor_mul(
                                hbuf[:, fr * TOWN:(fr + 1) * TOWN],
                                sgl[:, fpar * TOWN:(fpar + 1) * TOWN],
                                usc[:, fpar * TOWN:(fpar + 1) * TOWN])
                        th = pl.step("vector", f_h, inc=("dve", 1),
                                     waits=later(tsl, tus, h_free[0]))
                        h_toks.append(th)

                    down_toks = []
                    for dr in range(CT):
                        par = dr % 2
                        slot = dr % 4

                        def f_dd(gp, e=e, dr=dr, slot=slot):
                            gp.dma_start(
                                out=ds[:, slot * FRT * P:(slot + 1) * FRT * P
                                       ].rearrange("p (f j) -> p f j", j=P),
                                in_=dpT[e, :, dr * P:(dr + 1) * P
                                        ].rearrange("(f p) j -> p f j", p=P)
                            ).then_inc(SEM["wdma"], 16)
                            return None
                        tdd = pl.step("gpsimd", f_dd, dma_n=1, dma_sem="wdma",
                                      waits=later(ds_free[slot]))

                        def f_d(t, par=par, slot=slot):
                            last = None
                            for h in range(2):
                                for f in range(FRT):
                                    last = t.matmul(
                                        psDb[:, par * TOWN + h * 512:
                                             par * TOWN + h * 512 + 512],
                                        ds[:, slot * FRT * P + f * P:
                                           slot * FRT * P + (f + 1) * P],
                                        hbuf[:, f * TOWN + h * 512:
                                             f * TOWN + h * 512 + 512],
                                        start=(f == 0), stop=(f == FRT - 1))
                            return last
                        td = pl.step("tensor", f_d, inc=("pe", 1),
                                     waits=later(tdd, *h_toks,
                                                 psD_free[par]))
                        ds_free[slot] = td
                        down_toks.append(td)

                        def f_acc(v, dr=dr, par=par):
                            return v.tensor_add(
                                yTv(dr), yTv(dr),
                                psDb[:, par * TOWN:(par + 1) * TOWN])
                        ta = pl.step("vector", f_acc, inc=("dve", 1),
                                     waits=later(td, yadd[dr], barrierF))
                        psD_free[par] = ta
                        yadd[dr] = ta
                    h_free[0] = later(*down_toks)
                    twb_free[ep] = later(*down_toks)

            marks.append((len(pl.steps), dict(pl.val)))
            if upto < 5:
                cut, snap = marks[upto]
                pl.steps = pl.steps[:cut]
                pl.val = dict(snap)
                n_out_dma[0] = snap.get("odma", 0) // 16
                yadd = [(s, v) for s, v in snap.items()]

            def f_out(sync):
                sync.dma_start(
                    out=out[:, :].rearrange("(r p) n -> p r n", p=P),
                    in_=yT[:].rearrange("p (r n) -> p r n", n=TOWN)
                ).then_inc(SEM["odma"], 16)
                return None
            pl.step("sync", f_out, dma_n=1, dma_sem="odma",
                    waits=later(*yadd))
            n_out_dma[0] += 1

            def f_fin(sync):
                return sync.nop()
            pl.step("sync", f_fin, waits=[("odma", 16 * n_out_dma[0])])

            for name in ("xdma", "wdma", "odma", "pe", "act", "dve", "gp"):
                SEM[name] = es.enter_context(nc.semaphore(name))

            block = es.enter_context(nc.Block())

            def emit_engine(eng_name, eng):
                emitted = {}
                for st in pl.steps:
                    if st["eng"] != eng_name:
                        continue
                    for s, v in st["waits"]:
                        if emitted.get(s, 0) < v:
                            eng.wait_ge(SEM[s], v)
                            emitted[s] = v
                    inst = st["fn"](eng)
                    if st["inc"] is not None:
                        s, a = st["inc"]
                        assert inst is not None
                        inst.then_inc(SEM[s], a)

            @block.sync
            def _(sync):
                emit_engine("sync", sync)

            @block.gpsimd
            def _(gp):
                emit_engine("gpsimd", gp)

            @block.tensor
            def _(t):
                emit_engine("tensor", t)

            @block.vector
            def _(v):
                emit_engine("vector", v)

            @block.scalar
            def _(a):
                emit_engine("scalar", a)

    return nc, dbg


def make_inputs(inp):
    x = np.asarray(inp["L0_attn_x"], np.float32)
    wq = np.asarray(inp["L0_attn_Wq"], np.float32)
    wk = np.asarray(inp["L0_attn_Wk"], np.float32)
    wv = np.asarray(inp["L0_attn_Wv"], np.float32)
    memory = np.ascontiguousarray(np.asarray(inp["L0_attn_memory"], np.float32))
    memnorm = np.ascontiguousarray(np.asarray(inp["L0_attn_memory_norm"], np.float32))
    memw = np.asarray(inp["L0_attn_memory_weight"], np.float32).reshape(1, 1)
    gw = np.asarray(inp["L0_ffn_gate_w"], np.float32)
    gp = np.asarray(inp["L0_ffn_gate_proj"], np.float32)
    up = np.asarray(inp["L0_ffn_up_proj"], np.float32)
    dp = np.asarray(inp["L0_ffn_down_proj"], np.float32)

    shared = {
        "wqT": np.ascontiguousarray(wq.T),
        "wkT": np.ascontiguousarray(wk.T),
        "wvT": np.ascontiguousarray(wv.T),
        "mem": memory,
        "memnorm": np.ascontiguousarray(memnorm.reshape(4, P).T),
        "memw": np.ascontiguousarray(memw),
        "gwT": np.ascontiguousarray(gw.T),
        "gpT": np.ascontiguousarray(gp.transpose(0, 2, 1)),
        "upT": np.ascontiguousarray(up.transpose(0, 2, 1)),
        "dpT": np.ascontiguousarray(dp.transpose(0, 2, 1)),
    }
    in_maps = []
    for c in range(8):
        beta, r0 = c // 4, (c % 4) * TOWN
        m = dict(shared)
        m["x"] = np.ascontiguousarray(x[beta])
        m["xq"] = np.ascontiguousarray(x[beta, r0:r0 + TOWN])
        in_maps.append(m)
    return in_maps


def assemble(results):
    outp = np.empty((2, SEQ, D), np.float32)
    for c in range(8):
        beta, r0 = c // 4, (c % 4) * TOWN
        outp[beta, r0:r0 + TOWN, :] = np.asarray(results[c]["out"]).T
    return outp


_NC_CACHE = {}


def kernel(**inputs):
    from concourse.bass_utils import run_bass_kernel_spmd
    in_maps = make_inputs(inputs)
    if "nc" not in _NC_CACHE:
        nc, _ = build(debug=False)
        _NC_CACHE["nc"] = nc
    res = run_bass_kernel_spmd(_NC_CACHE["nc"], in_maps, core_ids=list(range(8)))
    return assemble(res.results)


# version 06
# speedup vs baseline: 1.0905x; 1.0905x over previous
import os
import numpy as np
import concourse.bass as bass
import concourse.mybir as mybir
from contextlib import ExitStack
from types import SimpleNamespace

F32 = mybir.dt.float32
BF16 = mybir.dt.bfloat16
AF = mybir.ActivationFunctionType
ALU = mybir.AluOpType
AX = mybir.AxisListType.X

P = 128
D = 2048
SEQ = 4096
TOWN = 1024
CT = D // P
DKV = 512
NQ, NH, DH = 4, 8, 64
E, FD, FRT = 8, 1408, 11
EPS = 1e-5
NKT = SEQ // P
NCH = SEQ // 512
SCALE = 0.125

ENGS = ("sync", "gpsimd", "tensor", "vector", "scalar")


class Plan:
    def __init__(self):
        self.steps = []
        self.val = {}

    def cur(self, sem):
        v = self.val.get(sem, 0)
        return (sem, v) if v else None

    def step(self, eng, fn, inc=None, waits=(), dma_n=0, dma_sem=None):
        w = []
        for t in waits:
            if t is None:
                continue
            if isinstance(t, list):
                w.extend(xx for xx in t if xx is not None)
            else:
                w.append(t)
        self.steps.append(dict(eng=eng, fn=fn, inc=inc, waits=w))
        if inc is not None:
            s, a = inc
            self.val[s] = self.val.get(s, 0) + a
            return (s, self.val[s])
        if dma_n:
            self.val[dma_sem] = self.val.get(dma_sem, 0) + 16 * dma_n
            return (dma_sem, self.val[dma_sem])
        return None


def later(*toks):
    best = {}
    for t in toks:
        if t is None:
            continue
        if isinstance(t, list):
            for s, v in t:
                best[s] = max(best.get(s, 0), v)
        else:
            s, v = t
            best[s] = max(best.get(s, 0), v)
    return [(s, v) for s, v in best.items()]


def build(debug=False, upto=4):
    nc = bass.Bass("TRN2", target_bir_lowering=False)

    x = nc.dram_tensor("x", [SEQ, D], F32, kind="ExternalInput")
    xq = nc.dram_tensor("xq", [TOWN, D], F32, kind="ExternalInput")
    wqT = nc.dram_tensor("wqT", [D, D], F32, kind="ExternalInput")
    wkT = nc.dram_tensor("wkT", [D, DKV], F32, kind="ExternalInput")
    wvT = nc.dram_tensor("wvT", [D, DKV], F32, kind="ExternalInput")
    mem = nc.dram_tensor("mem", [DKV, DKV], F32, kind="ExternalInput")
    memnorm = nc.dram_tensor("memnorm", [P, 4], F32, kind="ExternalInput")
    memw = nc.dram_tensor("memw", [1, 1], F32, kind="ExternalInput")
    gwT = nc.dram_tensor("gwT", [D, E], F32, kind="ExternalInput")
    gpT = nc.dram_tensor("gpT", [E, D, FD], F32, kind="ExternalInput")
    upT = nc.dram_tensor("upT", [E, D, FD], F32, kind="ExternalInput")
    dpT = nc.dram_tensor("dpT", [E, FD, D], F32, kind="ExternalInput")
    out = nc.dram_tensor("out", [D, TOWN], F32, kind="ExternalOutput")

    dbg = {}
    if debug:
        dbg["kT"] = nc.dram_tensor("d_kT", [P, 4 * SEQ], BF16, kind="ExternalOutput")
        dbg["vb"] = nc.dram_tensor("d_vb", [P, NKT * 520], BF16, kind="ExternalOutput")
        dbg["qT"] = nc.dram_tensor("d_qT", [P, CT * TOWN], BF16, kind="ExternalOutput")
        dbg["y1"] = nc.dram_tensor("d_y1", [P, CT * TOWN], F32, kind="ExternalOutput")
        dbg["y2"] = nc.dram_tensor("d_y2", [P, CT * TOWN], F32, kind="ExternalOutput")
        dbg["ynT"] = nc.dram_tensor("d_ynT", [P, CT * TOWN], BF16, kind="ExternalOutput")
        dbg["tw"] = nc.dram_tensor("d_tw", [P, 64], F32, kind="ExternalOutput")

    pl = Plan()
    SEM = {}
    n_out_dma = [0]
    marks = []

    es = ExitStack()
    with es:
        yT = es.enter_context(nc.sbuf_tensor("yT", [P, CT * TOWN], F32))
        identbf = es.enter_context(nc.sbuf_tensor("identbf", [P, P], BF16))
        ident32 = es.enter_context(nc.sbuf_tensor("ident32", [P, P], F32))
        onesbf = es.enter_context(nc.sbuf_tensor("onesbf", [1, P], BF16))
        onescol = es.enter_context(nc.sbuf_tensor("onescol", [P, 1], F32))
        epsc = es.enter_context(nc.sbuf_tensor("epsc", [P, 1], F32))
        wmix = es.enter_context(nc.sbuf_tensor("wmix", [1, 4], F32))
        m2s = es.enter_context(nc.sbuf_tensor("m2s", [P, 8], F32))
        r2s = es.enter_context(nc.sbuf_tensor("r2s", [P, 8], F32))

        def yTv(rt):
            return yT[:, rt * TOWN:(rt + 1) * TOWN]

        BC = SimpleNamespace(bhi=None, blo=None, blt=None, free=None)

        def emit_bcast(src_ap, n, ps_ap, wait_src):
            bhi, blo, blt = BC.bhi, BC.blo, BC.blt

            def f_hi(v):
                return v.tensor_copy(bhi[:, 0:n], src_ap)
            th = pl.step("vector", f_hi, inc=("dve", 1),
                         waits=later(wait_src, BC.free))

            def f_lo(v):
                return v.scalar_tensor_tensor(blt[:, 0:n], bhi[:, 0:n], -1.0,
                                              src_ap, ALU.mult, ALU.add)
            tl = pl.step("vector", f_lo, inc=("dve", 1), waits=[th])

            def f_lo2(v):
                return v.tensor_copy(blo[:, 0:n], blt[:, 0:n])
            tl2 = pl.step("vector", f_lo2, inc=("dve", 1), waits=[tl])

            def f_mm(t):
                last = None
                for j in range(0, n, 512):
                    j1 = min(n, j + 512)
                    t.matmul(ps_ap[:, j:j1], onesbf[:], bhi[:, j:j1],
                             start=True, stop=False)
                    last = t.matmul(ps_ap[:, j:j1], onesbf[:], blo[:, j:j1],
                                    start=False, stop=True)
                return last
            tmm = pl.step("tensor", f_mm, inc=("pe", 1), waits=later(th, tl2))
            BC.free = tmm
            return tmm

        def emit_ln_stats(x_ap, st6, sub, mean_ap, var_ap, rstd_ap, nb_ap,
                          waits):
            def f_stats(v):
                last = None
                for j in range(4):
                    last = v.bn_stats(st6[:, j * 6:(j + 1) * 6],
                                      x_ap[:, j * 512:(j + 1) * 512])
                return last
            t1 = pl.step("vector", f_stats, inc=("dve", 1), waits=waits)
            st6v = st6[:].rearrange("p (a b) -> p a b", b=3)
            means = st6v[:, :, 1:2].rearrange("p a one -> p (a one)")
            cvars = st6v[:, :, 2:3].rearrange("p a one -> p (a one)")

            def f_red(v):
                v.reduce_sum(mean_ap, means, axis=AX)
                return v.reduce_sum(var_ap, cvars, axis=AX)
            t2 = pl.step("vector", f_red, inc=("dve", 1), waits=[t1])

            def f_cp(v):
                return v.tensor_copy(sub[:, 8:16], means)
            t2b = pl.step("vector", f_cp, inc=("dve", 1), waits=[t1])

            def f_mn(v):
                return v.tensor_scalar(mean_ap, mean_ap, 0.125, 0.0,
                                       ALU.mult, ALU.add)
            t3 = pl.step("vector", f_mn, inc=("dve", 1), waits=[t2])

            def f_dev(v):
                return v.tensor_scalar(sub[:, 0:8], sub[:, 8:16], mean_ap, 0.0,
                                       ALU.subtract, ALU.add)
            t4 = pl.step("vector", f_dev, inc=("dve", 1), waits=[t3, t2b])

            def f_sq(v):
                return v.tensor_mul(sub[:, 0:8], sub[:, 0:8], sub[:, 0:8])
            t5 = pl.step("vector", f_sq, inc=("dve", 1), waits=[t4])

            def f_vs(v):
                return v.reduce_sum(sub[:, 16:17], sub[:, 0:8], axis=AX)
            t6 = pl.step("vector", f_vs, inc=("dve", 1), waits=[t5])

            def f_v1(v):
                return v.tensor_scalar(var_ap, var_ap, 1.0 / 2048.0, 0.0,
                                       ALU.mult, ALU.add)
            t7 = pl.step("vector", f_v1, inc=("dve", 1), waits=[t2])

            def f_v2(v):
                return v.scalar_tensor_tensor(var_ap, sub[:, 16:17], 0.125,
                                              var_ap, ALU.mult, ALU.add)
            t8 = pl.step("vector", f_v2, inc=("dve", 1), waits=[t6, t7])

            def f_rstd(a):
                a.activation(rstd_ap, var_ap, AF.Ln, bias=epsc[:])
                return a.activation(rstd_ap, rstd_ap, AF.Exp, scale=-0.5)
            t9 = pl.step("scalar", f_rstd, inc=("act", 1), waits=[t8])

            def f_nb(v):
                return v.scalar_tensor_tensor(nb_ap, mean_ap, -1.0, rstd_ap,
                                              ALU.mult, ALU.mult)
            t10 = pl.step("vector", f_nb, inc=("dve", 1), waits=[t9])
            return t9, t10

        esM = ExitStack()
        last_evac = {"kT": None, "vb": None, "qT": None, "yT": None}
        CX = SimpleNamespace(qTv=None)
        with esM:
            kT = esM.enter_context(nc.sbuf_tensor("kT", [P, 4 * SEQ], BF16))
            vb = esM.enter_context(nc.sbuf_tensor("vb", [P, NKT * 520], BF16))

            def kTv(r):
                return kT[:, r * SEQ:(r + 1) * SEQ]

            def f_ident(gp):
                gp.memset(identbf[:], 0.0)
                gp.affine_select(out=identbf[:], in_=identbf[:],
                                 compare_op=ALU.not_equal, fill=1.0, base=0,
                                 pattern=[[-1, P]], channel_multiplier=1)
                gp.memset(ident32[:], 0.0)
                return gp.affine_select(out=ident32[:], in_=ident32[:],
                                        compare_op=ALU.not_equal, fill=1.0,
                                        base=0, pattern=[[-1, P]],
                                        channel_multiplier=1)
            t_ident = pl.step("gpsimd", f_ident, inc=("gp", 1))

            def f_const(v):
                v.memset(onesbf[:], 1.0)
                v.memset(onescol[:], 1.0)
                v.memset(epsc[:], EPS)
                return v.memset(vb[:], 1.0)
            t_const = pl.step("vector", f_const, inc=("dve", 1))

            def f_mw(sync):
                sync.dma_start(out=wmix[0:1, 0:1], in_=memw[:, :]
                               ).then_inc(SEM["xdma"], 16)
                return None
            t_mw = pl.step("sync", f_mw, dma_n=1, dma_sem="xdma")

            def f_s1(a):
                return a.activation(wmix[0:1, 1:2], wmix[0:1, 0:1], AF.Exp,
                                    scale=-1.0)
            t_s1 = pl.step("scalar", f_s1, inc=("act", 1), waits=[t_mw])

            def f_s2(v):
                return v.tensor_scalar(wmix[0:1, 2:3], wmix[0:1, 1:2], 1.0, 0.0,
                                       ALU.add, ALU.add)
            t_s2 = pl.step("vector", f_s2, inc=("dve", 1), waits=[t_s1])

            def f_s3(v):
                return v.reciprocal(wmix[0:1, 0:1], wmix[0:1, 2:3])
            t_s3 = pl.step("vector", f_s3, inc=("dve", 1), waits=[t_s2])

            def f_s4(v):
                return v.tensor_mul(wmix[0:1, 1:2], wmix[0:1, 1:2],
                                    wmix[0:1, 0:1])
            t_s4 = pl.step("vector", f_s4, inc=("dve", 1), waits=[t_s3])

            def emit_p0_pass(B, R, srcten, n_chunks, own, t_w):
                for ch in range(n_chunks):
                    cp = ch % B.xnTn
                    ev_toks = []
                    for i in range(4):
                        tt = ch * 4 + i
                        tp = tt % B.xtn
                        xtap = B.xt[:, tp * D:(tp + 1) * D]

                        def f_x(sync, tt=tt, tp=tp, B=B, srcten=srcten):
                            sync.dma_start(out=B.xt[:, tp * D:(tp + 1) * D],
                                           in_=srcten[tt * P:(tt + 1) * P, :]
                                           ).then_inc(SEM["xdma"], 16)
                            return None
                        tx = pl.step("sync", f_x, dma_n=1, dma_sem="xdma",
                                     waits=later(R.xt_free[tp]))
                        sp_ = tt % 2
                        t_rs, t_nb = emit_ln_stats(
                            xtap, B.st6[:, sp_ * 24:(sp_ + 1) * 24],
                            B.sub[:, sp_ * 17:(sp_ + 1) * 17],
                            B.mn0[:, sp_:sp_ + 1], B.vr0[:, sp_:sp_ + 1],
                            B.rs0[:, sp_:sp_ + 1], B.nb0[:, sp_:sp_ + 1],
                            later(tx, R.st_free[sp_]))
                        R.st_free[sp_] = None

                        def f_aff(a, xtap=xtap, B=B, sp_=sp_):
                            return a.activation(B.xn[:], xtap, AF.Identity,
                                                bias=B.nb0[:, sp_:sp_ + 1],
                                                scale=B.rs0[:, sp_:sp_ + 1])
                        t_xn = pl.step("scalar", f_aff, inc=("act", 1),
                                       waits=later(t_rs, t_nb, R.xn_free[0]))
                        R.st_free[sp_] = t_xn

                        xt_consumers = [t_xn, t_rs]
                        if own:
                            for g in range(4):
                                gb = g % 2

                                def f_trx(t, g=g, gb=gb, xtap=xtap, B=B):
                                    last = None
                                    for idx in range(4):
                                        c = g * 4 + idx
                                        last = t.transpose(
                                            B.psX[:, gb * 512 + idx * 128:
                                                  gb * 512 + (idx + 1) * 128],
                                            xtap[:, c * P:(c + 1) * P],
                                            ident32[:])
                                    return last
                                ttr = pl.step("tensor", f_trx, inc=("pe", 1),
                                              waits=later(tx, R.psX_free[gb],
                                                          t_ident))

                                def f_evx(v, g=g, gb=gb, tt=tt, B=B):
                                    last = None
                                    for idx in range(4):
                                        c = g * 4 + idx
                                        last = v.tensor_copy(
                                            yTv(c)[:, tt * P:(tt + 1) * P],
                                            B.psX[:, gb * 512 + idx * 128:
                                                  gb * 512 + (idx + 1) * 128])
                                    return last
                                tev = pl.step("vector", f_evx, inc=("dve", 1),
                                              waits=[ttr])
                                R.psX_free[gb] = tev
                                xt_consumers.append(ttr)
                                last_evac["yT"] = tev
                        R.xt_free[tp] = later(*xt_consumers)

                        tr_toks = []
                        for g in range(2):
                            def f_trn(t, g=g, B=B):
                                last = None
                                for idx in range(8):
                                    c = g * 8 + idx
                                    last = t.transpose(
                                        B.psTR[:, g * 1024 + idx * P:
                                               g * 1024 + (idx + 1) * P],
                                        B.xn[:, c * P:(c + 1) * P], identbf[:])
                                return last
                            ttr = pl.step("tensor", f_trn, inc=("pe", 1),
                                          waits=later(t_xn, R.psTR_free[g],
                                                      t_ident))

                            def f_evn(eng, g=g, cp=cp, i=i, B=B):
                                cp_fn = (eng.copy if g == 0
                                         else eng.tensor_copy)
                                last = None
                                for idx in range(8):
                                    c = g * 8 + idx
                                    last = cp_fn(
                                        B.xnT[:, cp * CT * 512 + c * 512 +
                                              i * P:cp * CT * 512 + c * 512 +
                                              (i + 1) * P],
                                        B.psTR[:, g * 1024 + idx * P:
                                               g * 1024 + (idx + 1) * P])
                                return last
                            ev_eng = "scalar" if g == 0 else "vector"
                            tev = pl.step(ev_eng, f_evn,
                                          inc=("act" if g == 0 else "dve", 1),
                                          waits=later(ttr, R.xnT_free[cp]))
                            R.psTR_free[g] = tev
                            tr_toks.append(ttr)
                            ev_toks.append(tev)
                        R.xn_free[0] = later(*tr_toks)

                    if not own:
                        for r in range(4):
                            kp = r % 2

                            def f_k(t, r=r, cp=cp, kp=kp, B=B):
                                last = None
                                for c in range(CT):
                                    last = t.matmul(
                                        B.psKQ[:, kp * 512:(kp + 1) * 512],
                                        B.wk_sb[:, c * DKV + r * P:
                                                c * DKV + (r + 1) * P],
                                        B.xnT[:, cp * CT * 512 + c * 512:
                                              cp * CT * 512 + (c + 1) * 512],
                                        start=(c == 0), stop=(c == CT - 1))
                                return last
                            tk = pl.step("tensor", f_k, inc=("pe", 1),
                                         waits=later(*ev_toks, R.psKQ_free[kp],
                                                     t_w))

                            def f_ek(v, r=r, ch=ch, kp=kp, B=B):
                                return v.tensor_copy(
                                    kTv(r)[:, ch * 512:(ch + 1) * 512],
                                    B.psKQ[:, kp * 512:(kp + 1) * 512])
                            te = pl.step("vector", f_ek, inc=("dve", 1),
                                         waits=[tk])
                            R.psKQ_free[kp] = te
                            last_evac["kT"] = te
                        for i in range(4):
                            vp = i % 2
                            kt = ch * 4 + i

                            def f_v(t, i=i, cp=cp, vp=vp, B=B):
                                last = None
                                for c in range(CT):
                                    last = t.matmul(
                                        B.psV[:, vp * 512:(vp + 1) * 512],
                                        B.xnT[:, cp * CT * 512 + c * 512 +
                                              i * P:cp * CT * 512 + c * 512 +
                                              (i + 1) * P],
                                        B.wv_sb[:, c * DKV:(c + 1) * DKV],
                                        start=(c == 0), stop=(c == CT - 1))
                                return last
                            tv = pl.step("tensor", f_v, inc=("pe", 1),
                                         waits=later(*ev_toks, R.psV_free[vp],
                                                     t_w))

                            def f_ev(v, kt=kt, vp=vp, B=B):
                                dst = vb[:, kt * 520:(kt + 1) * 520
                                         ].rearrange("p (h d) -> p h d",
                                                     d=65)[:, :, 0:64]
                                srcv = B.psV[:, vp * 512:(vp + 1) * 512
                                             ].rearrange("p (h d) -> p h d",
                                                         d=64)
                                return v.tensor_copy(dst, srcv)
                            te = pl.step("vector", f_ev, inc=("dve", 1),
                                         waits=later(tv, t_const))
                            R.psV_free[vp] = te
                            last_evac["vb"] = te
                    else:
                        nslot = len(R.wqs_free)
                        for rt in range(CT):
                            kp = rt % 2
                            ws = rt % nslot

                            def f_wq(gp, rt=rt, ws=ws, B=B):
                                gp.dma_start(
                                    out=B.wqs[:, ws * CT * P:(ws + 1) * CT * P
                                              ].rearrange("p (c j) -> p c j",
                                                          j=P),
                                    in_=wqT[:, rt * P:(rt + 1) * P
                                            ].rearrange("(c p) j -> p c j",
                                                        p=P)
                                ).then_inc(SEM["wdma"], 16)
                                return None
                            twq = pl.step("gpsimd", f_wq, dma_n=1,
                                          dma_sem="wdma",
                                          waits=later(R.wqs_free[ws]))

                            def f_q(t, cp=cp, kp=kp, ws=ws, B=B):
                                last = None
                                for c in range(CT):
                                    last = t.matmul(
                                        B.psKQ[:, kp * 512:(kp + 1) * 512],
                                        B.wqs[:, ws * CT * P + c * P:
                                              ws * CT * P + (c + 1) * P],
                                        B.xnT[:, cp * CT * 512 + c * 512:
                                              cp * CT * 512 + (c + 1) * 512],
                                        start=(c == 0), stop=(c == CT - 1))
                                return last
                            tq = pl.step("tensor", f_q, inc=("pe", 1),
                                         waits=later(*ev_toks, R.psKQ_free[kp],
                                                     twq))
                            R.wqs_free[ws] = tq

                            def f_eq(v, rt=rt, ch=ch, kp=kp, B=B):
                                return v.tensor_copy(
                                    CX.qTv(rt)[:, ch * 512:(ch + 1) * 512],
                                    B.psKQ[:, kp * 512:(kp + 1) * 512])
                            te = pl.step("vector", f_eq, inc=("dve", 1),
                                         waits=[tq])
                            R.psKQ_free[kp] = te
                            last_evac["qT"] = te
                    R.xnT_free[cp] = ("pe", pl.val["pe"])

            esA = ExitStack()
            with esA:
                B1 = SimpleNamespace()
                B1.wk_sb = esA.enter_context(nc.sbuf_tensor("wk_sb", [P, CT * DKV], BF16))
                B1.wv_sb = esA.enter_context(nc.sbuf_tensor("wv_sb", [P, CT * DKV], BF16))
                B1.xt = esA.enter_context(nc.sbuf_tensor("xt", [P, 2 * D], F32))
                B1.xn = esA.enter_context(nc.sbuf_tensor("xn", [P, D], BF16))
                B1.xnT = esA.enter_context(nc.sbuf_tensor("xnT", [P, CT * 512], BF16))
                B1.xtn, B1.xnTn = 2, 1
                B1.st6 = esA.enter_context(nc.sbuf_tensor("st6", [P, 48], F32))
                B1.sub = esA.enter_context(nc.sbuf_tensor("sub", [P, 34], F32))
                B1.mn0 = esA.enter_context(nc.sbuf_tensor("mn0", [P, 2], F32))
                B1.vr0 = esA.enter_context(nc.sbuf_tensor("vr0", [P, 2], F32))
                B1.rs0 = esA.enter_context(nc.sbuf_tensor("rs0", [P, 2], F32))
                B1.nb0 = esA.enter_context(nc.sbuf_tensor("nb0", [P, 2], F32))
                B1.wqs = None
                B1.psTR = esA.enter_context(nc.psum_tensor("psTR", [P, 2 * 1024], BF16))
                B1.psKQ = esA.enter_context(nc.psum_tensor("psKQ", [P, 2 * 512], F32))
                B1.psV = esA.enter_context(nc.psum_tensor("psV", [P, 2 * 512], F32))
                B1.psX = esA.enter_context(nc.psum_tensor("psX", [P, 2 * 512], F32))

                def f_wkv(gp, B=B1):
                    gp.dma_start(out=B.wk_sb[:].rearrange("p (c j) -> p c j",
                                                          j=DKV),
                                 in_=wkT[:, :].rearrange("(c p) j -> p c j",
                                                         p=P)
                                 ).then_inc(SEM["wdma"], 16)
                    gp.dma_start(out=B.wv_sb[:].rearrange("p (c j) -> p c j",
                                                          j=DKV),
                                 in_=wvT[:, :].rearrange("(c p) j -> p c j",
                                                         p=P)
                                 ).then_inc(SEM["wdma"], 16)
                    return None
                t_wkv = pl.step("gpsimd", f_wkv, dma_n=2, dma_sem="wdma",
                                waits=[t_ident])

                R1 = SimpleNamespace(
                    st_free=[None, None],
                    xt_free=[None, None], xn_free=[None],
                    xnT_free=[None], psTR_free=[None, None],
                    psKQ_free=[None, None], psV_free=[None, None],
                    psX_free=[None, None], wqs_free=[None])
                emit_p0_pass(B1, R1, x, NCH, False, t_wkv)

            qT = esM.enter_context(nc.sbuf_tensor("qT", [P, CT * TOWN], BF16))

            def qTv(r):
                return qT[:, r * TOWN:(r + 1) * TOWN]
            CX.qTv = qTv

            kv_barrier = later(pl.cur("pe"), pl.cur("dve"), pl.cur("act"))
            esA2 = ExitStack()
            with esA2:
                B2 = SimpleNamespace()
                B2.xt = esA2.enter_context(nc.sbuf_tensor("xt2", [P, D], F32))
                B2.xn = esA2.enter_context(nc.sbuf_tensor("xn2", [P, D], BF16))
                B2.xnT = esA2.enter_context(nc.sbuf_tensor("xnT2", [P, CT * 512], BF16))
                B2.xtn, B2.xnTn = 1, 1
                B2.st6 = esA2.enter_context(nc.sbuf_tensor("st62", [P, 48], F32))
                B2.sub = esA2.enter_context(nc.sbuf_tensor("sub2", [P, 34], F32))
                B2.mn0 = esA2.enter_context(nc.sbuf_tensor("mn02", [P, 2], F32))
                B2.vr0 = esA2.enter_context(nc.sbuf_tensor("vr02", [P, 2], F32))
                B2.rs0 = esA2.enter_context(nc.sbuf_tensor("rs02", [P, 2], F32))
                B2.nb0 = esA2.enter_context(nc.sbuf_tensor("nb02", [P, 2], F32))
                B2.wqs = esA2.enter_context(nc.sbuf_tensor("wqs", [P, 2 * CT * P], BF16))
                B2.wk_sb = None
                B2.wv_sb = None
                B2.psTR = esA2.enter_context(nc.psum_tensor("psTR2", [P, 2 * 1024], BF16))
                B2.psKQ = esA2.enter_context(nc.psum_tensor("psKQ2", [P, 2 * 512], F32))
                B2.psV = None
                B2.psX = esA2.enter_context(nc.psum_tensor("psX2", [P, 2 * 512], F32))

                R2 = SimpleNamespace(
                    st_free=[kv_barrier, kv_barrier],
                    xt_free=[kv_barrier], xn_free=[kv_barrier],
                    xnT_free=[kv_barrier],
                    psTR_free=[kv_barrier, kv_barrier],
                    psKQ_free=[kv_barrier, kv_barrier],
                    psV_free=[None, None],
                    psX_free=[kv_barrier, kv_barrier],
                    wqs_free=[kv_barrier, kv_barrier])
                emit_p0_pass(B2, R2, xq, 2, True, None)

            dbg_gate = None
            if debug:
                def f_d0(sync):
                    sync.dma_start(out=dbg["kT"][:, :], in_=kT[:]
                                   ).then_inc(SEM["odma"], 16)
                    sync.dma_start(out=dbg["vb"][:, :], in_=vb[:]
                                   ).then_inc(SEM["odma"], 16)
                    sync.dma_start(out=dbg["qT"][:, :], in_=qT[:]
                                   ).then_inc(SEM["odma"], 16)
                    sync.dma_start(out=dbg["y1"][:, :], in_=yT[:]
                                   ).then_inc(SEM["odma"], 16)
                    return None
                dbg_gate = pl.step("sync", f_d0, dma_n=4, dma_sem="odma",
                                   waits=later(pl.cur("dve")))
                n_out_dma[0] += 4

            marks.append((len(pl.steps), dict(pl.val)))
            esB = ExitStack()
            with esB:
                mem_sb = esB.enter_context(nc.sbuf_tensor("mem_sb", [P, 4 * DKV], BF16))
                mn_sb = esB.enter_context(nc.sbuf_tensor("mn_sb", [P, 4], BF16))
                sig = esB.enter_context(nc.sbuf_tensor("sig", [P, 4 * TOWN], BF16))
                tmin = esB.enter_context(nc.sbuf_tensor("tmin", [P, TOWN], F32))
                e32 = esB.enter_context(nc.sbuf_tensor("e32", [P, TOWN], F32))
                tmax = esB.enter_context(nc.sbuf_tensor("tmax", [P, TOWN], BF16))
                dn = esB.enter_context(nc.sbuf_tensor("dn", [1, TOWN], F32))
                rw = esB.enter_context(nc.sbuf_tensor("rw", [1, TOWN], F32))
                rwb = esB.enter_context(nc.sbuf_tensor("rwb", [P, TOWN], F32))
                tmpa = esB.enter_context(nc.sbuf_tensor("tmpa", [P, TOWN], F32))
                BC.bhi = esB.enter_context(nc.sbuf_tensor("bhi1", [1, TOWN], BF16))
                BC.blo = esB.enter_context(nc.sbuf_tensor("blo1", [1, TOWN], BF16))
                BC.blt = esB.enter_context(nc.sbuf_tensor("blt1", [1, TOWN], F32))
                BC.free = None
                psM = esB.enter_context(nc.psum_tensor("psM", [P, 2 * TOWN], F32))
                psDen = esB.enter_context(nc.psum_tensor("psDen", [1, TOWN], F32))
                psB2 = esB.enter_context(nc.psum_tensor("psB2", [P, TOWN], F32))

                def f_mload(gp):
                    gp.dma_start(out=mem_sb[:].rearrange("p (k j) -> p k j",
                                                         j=DKV),
                                 in_=mem[:, :].rearrange("(k p) j -> p k j",
                                                         p=P)
                                 ).then_inc(SEM["wdma"], 16)
                    gp.dma_start(out=mn_sb[:], in_=memnorm[:, :]
                                 ).then_inc(SEM["wdma"], 16)
                    return None
                t_mem = pl.step("gpsimd", f_mload, dma_n=2, dma_sem="wdma",
                                waits=later(kv_barrier))

                q_done = later(pl.cur("dve"), dbg_gate)
                psM_free = [None, None]
                sig_consumed = later(pl.cur("pe"))
                rw_consumed = None
                for qg in range(NQ):
                    sig_toks = []
                    for rj in range(4):
                        rt = qg * 4 + rj

                        def f_min(v, rt=rt):
                            return v.tensor_scalar(tmin[:], qTv(rt), 0.0, 0.0,
                                                   ALU.min, ALU.add)
                        t1 = pl.step("vector", f_min, inc=("dve", 1),
                                     waits=later(q_done, sig_consumed))

                        def f_exp(a):
                            return a.activation(e32[:], tmin[:], AF.Exp)
                        t2 = pl.step("scalar", f_exp, inc=("act", 1),
                                     waits=[t1])

                        def f_max(v, rt=rt):
                            return v.tensor_scalar(tmax[:], qTv(rt), 0.0, 0.0,
                                                   ALU.max, ALU.add)
                        t3 = pl.step("vector", f_max, inc=("dve", 1),
                                     waits=later(q_done, sig_consumed))

                        def f_sig(v, rj=rj):
                            return v.scalar_tensor_tensor(
                                sig[:, rj * TOWN:(rj + 1) * TOWN], e32[:],
                                -1.0, tmax[:], ALU.add, ALU.add)
                        t4 = pl.step("vector", f_sig, inc=("dve", 1),
                                     waits=[t2, t3])
                        sig_toks.append(t4)

                    def f_dn(t):
                        last = None
                        for h in range(2):
                            for kt in range(4):
                                last = t.matmul(
                                    psDen[0:1, h * 512:(h + 1) * 512],
                                    mn_sb[:, kt:kt + 1],
                                    sig[:, kt * TOWN + h * 512:
                                        kt * TOWN + h * 512 + 512],
                                    start=(kt == 0), stop=(kt == 3))
                        return last
                    tdn = pl.step("tensor", f_dn, inc=("pe", 1),
                                  waits=later(*sig_toks, t_mem, rw_consumed))

                    def f_edn(v):
                        return v.tensor_copy(dn[:], psDen[:])
                    tedn = pl.step("vector", f_edn, inc=("dve", 1),
                                   waits=[tdn])

                    def f_rec(v):
                        return v.reciprocal(rw[:], dn[:])
                    trec = pl.step("vector", f_rec, inc=("dve", 1),
                                   waits=[tedn])

                    def f_rw(v):
                        return v.tensor_scalar(rw[:], rw[:], wmix[0:1, 0:1],
                                               0.0, ALU.mult, ALU.add)
                    trw = pl.step("vector", f_rw, inc=("dve", 1),
                                  waits=later(trec, t_s4))
                    tb = emit_bcast(rw[0:1, :], TOWN, psB2, [trw])
                    rw_consumed = tb

                    def f_erb(v):
                        return v.tensor_copy(rwb[:], psB2[:])
                    terb = pl.step("vector", f_erb, inc=("dve", 1), waits=[tb])

                    am_toks = []
                    for jt in range(4):
                        mp = jt % 2
                        qr2 = qg * 4 + jt

                        def f_am(t, jt=jt, mp=mp):
                            last = None
                            for h in range(2):
                                for kt in range(4):
                                    last = t.matmul(
                                        psM[:, mp * TOWN + h * 512:
                                            mp * TOWN + h * 512 + 512],
                                        mem_sb[:, kt * DKV + jt * P:
                                               kt * DKV + (jt + 1) * P],
                                        sig[:, kt * TOWN + h * 512:
                                            kt * TOWN + h * 512 + 512],
                                        start=(kt == 0), stop=(kt == 3))
                            return last
                        tam = pl.step("tensor", f_am, inc=("pe", 1),
                                      waits=later(*sig_toks, psM_free[mp],
                                                  t_mem))
                        am_toks.append(tam)

                        def f_mul(v, mp=mp):
                            return v.tensor_mul(
                                tmpa[:], psM[:, mp * TOWN:(mp + 1) * TOWN],
                                rwb[:])
                        tm1 = pl.step("vector", f_mul, inc=("dve", 1),
                                      waits=later(tam, terb))
                        psM_free[mp] = tm1

                        def f_add(v, qr2=qr2):
                            return v.tensor_add(yTv(qr2), yTv(qr2), tmpa[:])
                        pl.step("vector", f_add, inc=("dve", 1),
                                waits=later(tm1, last_evac["yT"]))
                    sig_consumed = later(tdn, *am_toks)

            marks.append((len(pl.steps), dict(pl.val)))
            esC = ExitStack()
            with esC:
                PT = esC.enter_context(nc.sbuf_tensor("PT", [P, 4 * 512], BF16))
                rsr = esC.enter_context(nc.sbuf_tensor("rsr", [1, 1024], F32))
                rsb = esC.enter_context(nc.sbuf_tensor("rsb", [P, 2 * 512], F32))
                tmp2 = esC.enter_context(nc.sbuf_tensor("tmp2", [P, 2 * 512], F32))
                stg = esC.enter_context(nc.sbuf_tensor("stg", [P, 2 * 1024], F32))
                BC.bhi = esC.enter_context(nc.sbuf_tensor("bhi2", [1, 1024], BF16))
                BC.blo = esC.enter_context(nc.sbuf_tensor("blo2", [1, 1024], BF16))
                BC.blt = esC.enter_context(nc.sbuf_tensor("blt2", [1, 1024], F32))
                BC.free = None
                psS = esC.enter_context(nc.psum_tensor("psS", [P, 4 * 512], F32))
                psPV = esC.enter_context(nc.psum_tensor("psPV", [P, 2 * 512], F32))
                psBC = esC.enter_context(nc.psum_tensor("psBC", [P, 2 * 512], F32))

                kv_ready = later(pl.cur("dve"))
                psS_free = [None, None]
                PT_free = [None, None]
                psPV_free = None
                rsr_free = None
                stg_free = [None, None]
                pending_tail = [None]
                iters = [(qg, hp, chn) for qg in range(NQ)
                         for hp in range(NH // 2) for chn in range(2)]
                for it, (qg, hp, chn) in enumerate(iters):
                        qr = qg * 4 + hp
                        if True:
                            pv_last = None

                            def emit_pv(kt, g):
                                nonlocal pv_last

                                def f_pv(t, hp=hp, kt=kt, g=g):
                                    t.matmul(
                                        psPV[0:65, 0:512],
                                        vb[:, kt * 520 + (2 * hp) * 65:
                                           kt * 520 + (2 * hp) * 65 + 65],
                                        PT[:, (2 * g) * 512:(2 * g + 1) * 512],
                                        start=(kt == 0),
                                        stop=(kt == NKT - 1))
                                    return t.matmul(
                                        psPV[0:65, 512:1024],
                                        vb[:, kt * 520 + (2 * hp + 1) * 65:
                                           kt * 520 + (2 * hp + 1) * 65 + 65],
                                        PT[:, (2 * g + 1) * 512:
                                           (2 * g + 2) * 512],
                                        start=(kt == 0),
                                        stop=(kt == NKT - 1))
                                tpv = pl.step("tensor", f_pv, inc=("pe", 1),
                                              waits=later(exp_toks[kt],
                                                          psPV_free
                                                          if kt == 0 else None))
                                PT_free[g] = tpv
                                pv_last = tpv

                            exp_toks = {}
                            for kt in range(NKT):
                                g = kt % 2

                                def f_sc(t, hp=hp, qr=qr, chn=chn, kt=kt, g=g):
                                    t.matmul(
                                        psS[:, (2 * g) * 512:
                                            (2 * g + 1) * 512],
                                        kTv(hp)[0:64, kt * P:(kt + 1) * P],
                                        qTv(qr)[0:64,
                                                chn * 512:(chn + 1) * 512],
                                        start=True, stop=True,
                                        tile_position=(0, 0))
                                    return t.matmul(
                                        psS[:, (2 * g + 1) * 512:
                                            (2 * g + 2) * 512],
                                        kTv(hp)[64:128, kt * P:(kt + 1) * P],
                                        qTv(qr)[64:128,
                                                chn * 512:(chn + 1) * 512],
                                        start=True, stop=True,
                                        tile_position=(64, 0))
                                tsc = pl.step("tensor", f_sc, inc=("pe", 1),
                                              waits=later(kv_ready,
                                                          psS_free[g]))

                                def f_exp(a, g=g):
                                    return a.activation(
                                        PT[:, (2 * g) * 512:(2 * g + 2) * 512],
                                        psS[:, (2 * g) * 512:
                                            (2 * g + 2) * 512],
                                        AF.Exp, scale=SCALE)
                                if kt >= 2:
                                    emit_pv(kt - 2, (kt - 2) % 2)
                                if kt == 12 and pending_tail[0] is not None:
                                    pending_tail[0]()
                                    pending_tail[0] = None
                                tex = pl.step("scalar", f_exp, inc=("act", 1),
                                              waits=later(tsc, PT_free[g]))
                                psS_free[g] = tex
                                exp_toks[kt] = tex
                            emit_pv(NKT - 2, (NKT - 2) % 2)
                            emit_pv(NKT - 1, (NKT - 1) % 2)

                            si = (qg * 8 + hp * 2 + chn) % 2
                            def f_stg(v, si=si):
                                return v.tensor_copy(
                                    stg[:, si * 1024:(si + 1) * 1024],
                                    psPV[:, 0:1024])
                            tstg = pl.step("vector", f_stg, inc=("dve", 1),
                                           waits=later(pv_last, stg_free[si]))
                            psPV_free = tstg
                            sv = stg[:, si * 1024:(si + 1) * 1024]

                            def make_tail(sv=sv, si=si, qr=qr, chn=chn,
                                          tstg=tstg):
                                def emit_tail():
                                    nonlocal rsr_free

                                    def f_rec(v):
                                        return v.reciprocal(rsr[:],
                                                            sv[64:65, :])
                                    tr1 = pl.step("vector", f_rec,
                                                  inc=("dve", 1),
                                                  waits=later(tstg, rsr_free))

                                    def f_rs(v):
                                        return v.tensor_scalar(
                                            rsr[:], rsr[:], wmix[0:1, 1:2],
                                            0.0, ALU.mult, ALU.add)
                                    tr2 = pl.step("vector", f_rs,
                                                  inc=("dve", 1),
                                                  waits=later(tr1, t_s4))
                                    tb = emit_bcast(rsr[0:1, :], 1024, psBC,
                                                    [tr2])
                                    rsr_free = tb

                                    def f_eb(v):
                                        return v.tensor_copy(rsb[:],
                                                             psBC[:, 0:1024])
                                    teb = pl.step("vector", f_eb,
                                                  inc=("dve", 1), waits=[tb])

                                    head_toks = []
                                    for hh in range(2):
                                        def f_mul(v, hh=hh):
                                            return v.tensor_mul(
                                                tmp2[hh * 64:(hh + 1) * 64,
                                                     hh * 512:(hh + 1) * 512],
                                                sv[0:64,
                                                   hh * 512:(hh + 1) * 512],
                                                rsb[0:64,
                                                    hh * 512:(hh + 1) * 512])
                                        tm1 = pl.step("vector", f_mul,
                                                      inc=("dve", 1),
                                                      waits=[teb])

                                        def f_add(v, hh=hh):
                                            dst = yTv(qr)[
                                                hh * 64:(hh + 1) * 64,
                                                chn * 512:(chn + 1) * 512]
                                            return v.tensor_add(
                                                dst, dst,
                                                tmp2[hh * 64:(hh + 1) * 64,
                                                     hh * 512:(hh + 1) * 512])
                                        tm2 = pl.step("vector", f_add,
                                                      inc=("dve", 1),
                                                      waits=[tm1])
                                        head_toks.append(tm2)
                                    stg_free[si] = later(*head_toks)
                                return emit_tail
                            pending_tail[0] = make_tail()

                if pending_tail[0] is not None:
                    pending_tail[0]()
                    pending_tail[0] = None

            dbg_gate2 = None
            if debug:
                def f_d2(sync):
                    sync.dma_start(out=dbg["y2"][:, :], in_=yT[:]
                                   ).then_inc(SEM["odma"], 16)
                    return None
                dbg_gate2 = pl.step("sync", f_d2, dma_n=1, dma_sem="odma",
                                    waits=later(pl.cur("dve")))
                n_out_dma[0] += 1

        marks.append((len(pl.steps), dict(pl.val)))
        barrierF = later(pl.cur("pe"), pl.cur("dve"), pl.cur("act"),
                         dbg_gate2)
        esF = ExitStack()
        with esF:
            ynT = esF.enter_context(nc.sbuf_tensor("ynT", [P, CT * TOWN], BF16))
            twT = esF.enter_context(nc.sbuf_tensor("twT", [8, TOWN], F32))
            twr = esF.enter_context(nc.sbuf_tensor("twr", [1, TOWN], F32))
            twb = esF.enter_context(nc.sbuf_tensor("twb", [P, 2 * TOWN], F32))
            gw_sb = esF.enter_context(nc.sbuf_tensor("gw_sb", [P, CT * E], F32))
            ones32 = esF.enter_context(nc.sbuf_tensor("ones32", [1, P], F32))
            BC.bhi = esF.enter_context(nc.sbuf_tensor("bhi3", [1, TOWN], BF16))
            BC.blo = esF.enter_context(nc.sbuf_tensor("blo3", [1, TOWN], BF16))
            BC.blt = esF.enter_context(nc.sbuf_tensor("blt3", [1, TOWN], F32))
            BC.free = None

            def ynTv(rt):
                return ynT[:, rt * TOWN:(rt + 1) * TOWN]

            def f_gw(sync):
                sync.dma_start(out=gw_sb[:].rearrange("p (c e) -> p c e", e=E),
                               in_=gwT[:, :].rearrange("(c p) e -> p c e", p=P)
                               ).then_inc(SEM["xdma"], 16)
                return None
            t_gw = pl.step("sync", f_gw, dma_n=1, dma_sem="xdma",
                           waits=barrierF)

            def f_o32(v):
                return v.memset(ones32[:], 1.0)
            t_o32 = pl.step("vector", f_o32, inc=("dve", 1), waits=barrierF)

            esD = ExitStack()
            with esD:
                ynat = esD.enter_context(nc.sbuf_tensor("ynat", [P, 2 * D], F32))
                ynn = esD.enter_context(nc.sbuf_tensor("ynn", [P, D], BF16))
                st6b = esD.enter_context(nc.sbuf_tensor("st6b", [P, 24], F32))
                subb = esD.enter_context(nc.sbuf_tensor("subb", [P, 17], F32))
                vrb = esD.enter_context(nc.sbuf_tensor("vrb", [P, 1], F32))
                nbb = esD.enter_context(nc.sbuf_tensor("nbb", [P, 1], F32))
                L0T = esD.enter_context(nc.sbuf_tensor("L0T", [8, TOWN], F32))
                sgr = esD.enter_context(nc.sbuf_tensor("sgr", [8, 1], F32))
                sbc = esD.enter_context(nc.sbuf_tensor("sbc", [P, 8], F32))
                lgt = esD.enter_context(nc.sbuf_tensor("lgt", [P, 8], F32))
                lg2 = esD.enter_context(nc.sbuf_tensor("lg2", [P, 8], F32))
                msr = esD.enter_context(nc.sbuf_tensor("msr", [P, 1], F32))
                mm1 = esD.enter_context(nc.sbuf_tensor("mm1", [P, 1], F32))
                mm2 = esD.enter_context(nc.sbuf_tensor("mm2", [P, 1], F32))
                ee = esD.enter_context(nc.sbuf_tensor("ee", [P, 1], F32))
                ee2 = esD.enter_context(nc.sbuf_tensor("ee2", [P, 1], F32))
                w1 = esD.enter_context(nc.sbuf_tensor("w1", [P, 1], F32))
                w2 = esD.enter_context(nc.sbuf_tensor("w2", [P, 1], F32))
                mk1 = esD.enter_context(nc.sbuf_tensor("mk1", [P, 8], F32))
                mk2 = esD.enter_context(nc.sbuf_tensor("mk2", [P, 8], F32))
                twn = esD.enter_context(nc.sbuf_tensor("twn", [P, 64], F32))
                psT3 = esD.enter_context(nc.psum_tensor("psT3", [P, 2048], F32))
                psTN = esD.enter_context(nc.psum_tensor("psTN", [P, 2048], BF16))
                psR = esD.enter_context(nc.psum_tensor("psR", [8, 512], F32))
                psR2 = esD.enter_context(nc.psum_tensor("psR2", [P, 8], F32))

                psT3_free = [None] * 4
                psTN_free = [None, None]
                ynat_free = [None, None]
                ynn_free = [None]
                ynT_toks = []
                for tt in range(8):
                    tp = tt % 2
                    ev_toks = []
                    for g in range(4):
                        def f_tr(t, g=g, tt=tt):
                            last = None
                            for idx in range(4):
                                rt = g * 4 + idx
                                last = t.transpose(
                                    psT3[:, g * 512 + idx * 128:
                                         g * 512 + (idx + 1) * 128],
                                    yTv(rt)[:, tt * P:(tt + 1) * P],
                                    ident32[:])
                            return last
                        ttr = pl.step("tensor", f_tr, inc=("pe", 1),
                                      waits=later(barrierF, psT3_free[g],
                                                  ynat_free[tp]))

                        def f_ev(v, g=g, tp=tp):
                            return v.tensor_copy(
                                ynat[:, tp * D + g * 512:
                                     tp * D + (g + 1) * 512],
                                psT3[:, g * 512:(g + 1) * 512])
                        tev = pl.step("vector", f_ev, inc=("dve", 1),
                                      waits=[ttr])
                        psT3_free[g] = tev
                        ev_toks.append(tev)
                    ynap = ynat[:, tp * D:(tp + 1) * D]
                    t_rs, t_nb = emit_ln_stats(
                        ynap, st6b, subb, m2s[:, tt:tt + 1], vrb[:, 0:1],
                        r2s[:, tt:tt + 1], nbb[:, 0:1], ev_toks)

                    def f_aff(a, ynap=ynap, tt=tt):
                        return a.activation(ynn[:], ynap, AF.Identity,
                                            bias=nbb[:, 0:1],
                                            scale=r2s[:, tt:tt + 1])
                    t_yn = pl.step("scalar", f_aff, inc=("act", 1),
                                   waits=later(t_rs, t_nb, ynn_free[0]))
                    ynat_free[tp] = later(t_yn, t_rs)

                    tr_toks = []
                    for g in range(2):
                        def f_trn(t, g=g):
                            last = None
                            for idx in range(8):
                                c = g * 8 + idx
                                last = t.transpose(
                                    psTN[:, g * 1024 + idx * P:
                                         g * 1024 + (idx + 1) * P],
                                    ynn[:, c * P:(c + 1) * P], identbf[:])
                            return last
                        ttr = pl.step("tensor", f_trn, inc=("pe", 1),
                                      waits=later(t_yn, psTN_free[g]))

                        def f_evn(v, g=g, tt=tt):
                            last = None
                            for idx in range(8):
                                c = g * 8 + idx
                                last = v.tensor_copy(
                                    ynTv(c)[:, tt * P:(tt + 1) * P],
                                    psTN[:, g * 1024 + idx * P:
                                         g * 1024 + (idx + 1) * P])
                            return last
                        tev = pl.step("vector", f_evn, inc=("dve", 1),
                                      waits=[ttr])
                        psTN_free[g] = tev
                        tr_toks.append(ttr)
                        ynT_toks.append(tev)
                    ynn_free[0] = later(*tr_toks)
                ynT_ready = later(*ynT_toks)

                l0_toks = []
                psR_free = None
                for h in range(2):
                    def f_l0(t, h=h):
                        last = None
                        for c in range(CT):
                            last = t.matmul(
                                psR[:, :], gw_sb[:, c * E:(c + 1) * E],
                                yTv(c)[:, h * 512:(h + 1) * 512],
                                start=(c == 0), stop=(c == CT - 1))
                        return last
                    tl0 = pl.step("tensor", f_l0, inc=("pe", 1),
                                  waits=later(t_gw, barrierF, psR_free))

                    def f_el0(v, h=h):
                        return v.tensor_copy(L0T[:, h * 512:(h + 1) * 512],
                                             psR[:])
                    te = pl.step("vector", f_el0, inc=("dve", 1), waits=[tl0])
                    psR_free = te
                    l0_toks.append(te)

                def f_se(t):
                    last = None
                    for c in range(CT):
                        last = t.matmul(psR[:, 0:1],
                                        gw_sb[:, c * E:(c + 1) * E],
                                        onescol[:],
                                        start=(c == 0), stop=(c == CT - 1))
                    return last
                tse = pl.step("tensor", f_se, inc=("pe", 1),
                              waits=later(t_gw, psR_free))

                def f_ese(v):
                    return v.tensor_copy(sgr[:], psR[:, 0:1])
                tese = pl.step("vector", f_ese, inc=("dve", 1), waits=[tse])
                psR_free = tese

                def f_trs(t):
                    return t.transpose(psR2[0:1, 0:8], sgr[:],
                                       ident32[0:8, 0:8])
                ttrs = pl.step("tensor", f_trs, inc=("pe", 1),
                               waits=later(tese, t_ident))

                def f_etrs(v):
                    return v.tensor_copy(BC.blt[:, 0:8], psR2[0:1, 0:8])
                tets = pl.step("vector", f_etrs, inc=("dve", 1),
                               waits=later(ttrs, BC.free))

                def f_sbm(t):
                    return t.matmul(psR2[:, 0:8], ones32[:],
                                    BC.blt[0:1, 0:8], start=True, stop=True)
                tsbm = pl.step("tensor", f_sbm, inc=("pe", 1),
                               waits=later(tets, t_o32))

                def f_esb(v):
                    return v.tensor_copy(sbc[:], psR2[:, 0:8])
                tesb = pl.step("vector", f_esb, inc=("dve", 1), waits=[tsbm])
                BC.free = tsbm
                psR2_free = tesb

                tw_toks = []
                for tt in range(8):
                    def f_trl(t, tt=tt):
                        return t.transpose(psR2[:, 0:8],
                                           L0T[:, tt * P:(tt + 1) * P],
                                           ident32[0:8, 0:8])
                    ttl = pl.step("tensor", f_trl, inc=("pe", 1),
                                  waits=later(*l0_toks, psR2_free, t_ident))

                    def f_elg(v):
                        return v.tensor_copy(lgt[:], psR2[:, 0:8])
                    tlg = pl.step("vector", f_elg, inc=("dve", 1),
                                  waits=[ttl])
                    psR2_free = tlg

                    def f_msr(v, tt=tt):
                        return v.scalar_tensor_tensor(
                            msr[:], m2s[:, tt:tt + 1], -1.0,
                            r2s[:, tt:tt + 1], ALU.mult, ALU.mult)
                    t1 = pl.step("vector", f_msr, inc=("dve", 1),
                                 waits=[tlg])

                    def f_lg2(v, tt=tt):
                        return v.tensor_scalar(lg2[:], lgt[:],
                                               r2s[:, tt:tt + 1], 0.0,
                                               ALU.mult, ALU.add)
                    t2 = pl.step("vector", f_lg2, inc=("dve", 1), waits=[tlg])

                    def f_lg3(v):
                        return v.scalar_tensor_tensor(lg2[:], sbc[:], msr[:],
                                                      lg2[:], ALU.mult,
                                                      ALU.add)
                    t3 = pl.step("vector", f_lg3, inc=("dve", 1),
                                 waits=later(t1, t2, tesb))

                    def f_m1(v):
                        return v.reduce_max(mm1[:], lg2[:], axis=AX)
                    t4 = pl.step("vector", f_m1, inc=("dve", 1), waits=[t3])

                    def f_mk1(v):
                        return v.tensor_scalar(mk1[:], lg2[:], mm1[:], 0.0,
                                               ALU.subtract, ALU.is_ge)
                    t5 = pl.step("vector", f_mk1, inc=("dve", 1), waits=[t4])

                    def f_l2(v):
                        return v.scalar_tensor_tensor(lgt[:], mk1[:], -1e30,
                                                      lg2[:], ALU.mult,
                                                      ALU.add)
                    t6 = pl.step("vector", f_l2, inc=("dve", 1), waits=[t5])

                    def f_m2(v):
                        return v.reduce_max(mm2[:], lgt[:], axis=AX)
                    t7 = pl.step("vector", f_m2, inc=("dve", 1), waits=[t6])

                    def f_mk2(v):
                        return v.tensor_scalar(mk2[:], lgt[:], mm2[:], 0.0,
                                               ALU.subtract, ALU.is_ge)
                    t8 = pl.step("vector", f_mk2, inc=("dve", 1), waits=[t7])

                    def f_d(v):
                        return v.tensor_scalar(ee[:], mm2[:], mm1[:], 0.0,
                                               ALU.subtract, ALU.add)
                    t9 = pl.step("vector", f_d, inc=("dve", 1),
                                 waits=[t7, t4])

                    def f_ee(a):
                        return a.activation(ee2[:], ee[:], AF.Exp)
                    t10 = pl.step("scalar", f_ee, inc=("act", 1), waits=[t9])

                    def f_w1(v):
                        return v.tensor_scalar(w1[:], ee2[:], 1.0, 0.0,
                                               ALU.add, ALU.add)
                    t11 = pl.step("vector", f_w1, inc=("dve", 1), waits=[t10])

                    def f_w1r(v):
                        return v.reciprocal(w1[:], w1[:])
                    t12 = pl.step("vector", f_w1r, inc=("dve", 1),
                                  waits=[t11])

                    def f_w2(v):
                        return v.tensor_mul(w2[:], ee2[:], w1[:])
                    t13 = pl.step("vector", f_w2, inc=("dve", 1), waits=[t12])

                    def f_t1(v):
                        return v.tensor_scalar(mk1[:], mk1[:], w1[:], 0.0,
                                               ALU.mult, ALU.add)
                    t14 = pl.step("vector", f_t1, inc=("dve", 1),
                                  waits=[t12, t8])

                    def f_tw(v, tt=tt):
                        return v.scalar_tensor_tensor(
                            twn[:, tt * 8:(tt + 1) * 8], mk2[:], w2[:],
                            mk1[:], ALU.mult, ALU.add)
                    t15 = pl.step("vector", f_tw, inc=("dve", 1),
                                  waits=[t13, t14])
                    tw_toks.append(t15)

                if debug:
                    def f_dtw(sync):
                        sync.dma_start(out=dbg["tw"][:, :], in_=twn[:]
                                       ).then_inc(SEM["odma"], 16)
                        sync.dma_start(out=dbg["ynT"][:, :], in_=ynT[:]
                                       ).then_inc(SEM["odma"], 16)
                        return None
                    pl.step("sync", f_dtw, dma_n=2, dma_sem="odma",
                            waits=later(*tw_toks, ynT_ready))
                    n_out_dma[0] += 2

                psR_free2 = psR_free
                twT_toks = []
                for half in range(2):
                    def f_trt(t, half=half):
                        last = None
                        for j in range(4):
                            tt = half * 4 + j
                            last = t.transpose(
                                psR[:, j * 128:(j + 1) * 128],
                                twn[:, tt * 8:(tt + 1) * 8], ident32[:])
                        return last
                    ttt = pl.step("tensor", f_trt, inc=("pe", 1),
                                  waits=later(*tw_toks, psR_free2, t_ident))

                    def f_ett(v, half=half):
                        return v.tensor_copy(
                            twT[:, half * 512:(half + 1) * 512], psR[:])
                    tet = pl.step("vector", f_ett, inc=("dve", 1),
                                  waits=[ttt])
                    psR_free2 = tet
                    twT_toks.append(tet)
                twT_ready = later(*twT_toks)

            marks.append((len(pl.steps), dict(pl.val)))
            p3_done = later(pl.cur("pe"), pl.cur("dve"), pl.cur("act"))
            esE = ExitStack()
            with esE:
                gs = esE.enter_context(nc.sbuf_tensor("gs", [P, 4 * CT * P], BF16))
                us = esE.enter_context(nc.sbuf_tensor("us", [P, 4 * CT * P], BF16))
                ds = esE.enter_context(nc.sbuf_tensor("ds", [P, 4 * FRT * P], BF16))
                hbuf = esE.enter_context(nc.sbuf_tensor("hbuf", [P, FRT * TOWN], BF16))
                sgl = esE.enter_context(nc.sbuf_tensor("sgl", [P, 2 * TOWN], BF16))
                usc = esE.enter_context(nc.sbuf_tensor("usc", [P, 2 * TOWN], BF16))
                psG = esE.enter_context(nc.psum_tensor("psG", [P, TOWN], F32))
                psU = esE.enter_context(nc.psum_tensor("psU", [P, TOWN], F32))
                psDb = esE.enter_context(nc.psum_tensor("psDb", [P, 2 * TOWN], F32))

                gs_free = [p3_done] * 4
                us_free = [p3_done] * 4
                ds_free = [p3_done] * 4
                psG_free = [p3_done]
                psU_free = [p3_done]
                psD_free = [p3_done, p3_done]
                twb_free = [p3_done, p3_done]
                twr_free = [p3_done]
                h_free = [p3_done]
                yadd = [None] * CT
                twb_tok = [None, None]

                for e in range(E):
                    ep = e % 2

                    def f_twr(gp, e=e):
                        gp.dma_start(out=twr[:], in_=twT[e:e + 1, :]
                                     ).then_inc(SEM["wdma"], 16)
                        return None
                    ttwr = pl.step("gpsimd", f_twr, dma_n=1, dma_sem="wdma",
                                   waits=later(twT_ready, twr_free[0]))
                    tbc = emit_bcast(twr[0:1, 0:512], 512, psG,
                                     later(ttwr, psG_free[0]))
                    tbc2 = emit_bcast(twr[0:1, 512:1024], 512, psU,
                                      later(ttwr, psU_free[0]))
                    twr_free[0] = later(tbc, tbc2)

                    def f_etw(v, ep=ep):
                        v.tensor_copy(twb[:, ep * TOWN:ep * TOWN + 512],
                                      psG[:, 0:512])
                        return v.tensor_copy(
                            twb[:, ep * TOWN + 512:(ep + 1) * TOWN],
                            psU[:, 0:512])
                    tetw = pl.step("vector", f_etw, inc=("dve", 1),
                                   waits=later(tbc, tbc2, twb_free[ep]))
                    twb_tok[ep] = tetw
                    psG_free[0] = tetw
                    psU_free[0] = tetw

                    h_toks = []
                    for fr in range(FRT):
                        fpar = fr % 2
                        slot = fr % 4

                        def f_dg(gp, e=e, fr=fr, slot=slot):
                            gp.dma_start(
                                out=gs[:, slot * CT * P:(slot + 1) * CT * P
                                       ].rearrange("p (c j) -> p c j", j=P),
                                in_=gpT[e, :, fr * P:(fr + 1) * P
                                        ].rearrange("(c p) j -> p c j", p=P)
                            ).then_inc(SEM["wdma"], 16)
                            return None
                        tdg = pl.step("gpsimd", f_dg, dma_n=1, dma_sem="wdma",
                                      waits=later(gs_free[slot]))

                        def f_du(gp, e=e, fr=fr, slot=slot):
                            gp.dma_start(
                                out=us[:, slot * CT * P:(slot + 1) * CT * P
                                       ].rearrange("p (c j) -> p c j", j=P),
                                in_=upT[e, :, fr * P:(fr + 1) * P
                                        ].rearrange("(c p) j -> p c j", p=P)
                            ).then_inc(SEM["wdma"], 16)
                            return None
                        tdu = pl.step("gpsimd", f_du, dma_n=1, dma_sem="wdma",
                                      waits=later(us_free[slot]))

                        def f_g(t, slot=slot):
                            last = None
                            for h in range(2):
                                for c in range(CT):
                                    last = t.matmul(
                                        psG[:, h * 512:(h + 1) * 512],
                                        gs[:, slot * CT * P + c * P:
                                           slot * CT * P + (c + 1) * P],
                                        ynTv(c)[:, h * 512:(h + 1) * 512],
                                        start=(c == 0), stop=(c == CT - 1))
                            return last
                        tg = pl.step("tensor", f_g, inc=("pe", 1),
                                     waits=later(tdg, ynT_ready, psG_free[0]))
                        gs_free[slot] = tg

                        def f_silu(a, fpar=fpar):
                            return a.activation(
                                sgl[:, fpar * TOWN:(fpar + 1) * TOWN],
                                psG[:], AF.Silu)
                        tsl = pl.step("scalar", f_silu, inc=("act", 1),
                                      waits=later(tg, h_free[0]))
                        psG_free[0] = tsl

                        def f_u(t, slot=slot):
                            last = None
                            for h in range(2):
                                for c in range(CT):
                                    last = t.matmul(
                                        psU[:, h * 512:(h + 1) * 512],
                                        us[:, slot * CT * P + c * P:
                                           slot * CT * P + (c + 1) * P],
                                        ynTv(c)[:, h * 512:(h + 1) * 512],
                                        start=(c == 0), stop=(c == CT - 1))
                            return last
                        tu = pl.step("tensor", f_u, inc=("pe", 1),
                                     waits=later(tdu, ynT_ready, psU_free[0]))
                        us_free[slot] = tu

                        def f_usc(v, fpar=fpar, ep=ep):
                            return v.tensor_mul(
                                usc[:, fpar * TOWN:(fpar + 1) * TOWN], psU[:],
                                twb[:, ep * TOWN:(ep + 1) * TOWN])
                        tus = pl.step("vector", f_usc, inc=("dve", 1),
                                      waits=later(tu, twb_tok[ep], h_free[0]))
                        psU_free[0] = tus

                        def f_h(v, fr=fr, fpar=fpar):
                            return v.tensor_mul(
                                hbuf[:, fr * TOWN:(fr + 1) * TOWN],
                                sgl[:, fpar * TOWN:(fpar + 1) * TOWN],
                                usc[:, fpar * TOWN:(fpar + 1) * TOWN])
                        th = pl.step("vector", f_h, inc=("dve", 1),
                                     waits=later(tsl, tus, h_free[0]))
                        h_toks.append(th)

                    down_toks = []
                    for dr in range(CT):
                        par = dr % 2
                        slot = dr % 4

                        def f_dd(gp, e=e, dr=dr, slot=slot):
                            gp.dma_start(
                                out=ds[:, slot * FRT * P:(slot + 1) * FRT * P
                                       ].rearrange("p (f j) -> p f j", j=P),
                                in_=dpT[e, :, dr * P:(dr + 1) * P
                                        ].rearrange("(f p) j -> p f j", p=P)
                            ).then_inc(SEM["wdma"], 16)
                            return None
                        tdd = pl.step("gpsimd", f_dd, dma_n=1, dma_sem="wdma",
                                      waits=later(ds_free[slot]))

                        def f_d(t, par=par, slot=slot):
                            last = None
                            for h in range(2):
                                for f in range(FRT):
                                    last = t.matmul(
                                        psDb[:, par * TOWN + h * 512:
                                             par * TOWN + h * 512 + 512],
                                        ds[:, slot * FRT * P + f * P:
                                           slot * FRT * P + (f + 1) * P],
                                        hbuf[:, f * TOWN + h * 512:
                                             f * TOWN + h * 512 + 512],
                                        start=(f == 0), stop=(f == FRT - 1))
                            return last
                        td = pl.step("tensor", f_d, inc=("pe", 1),
                                     waits=later(tdd, *h_toks,
                                                 psD_free[par]))
                        ds_free[slot] = td
                        down_toks.append(td)

                        def f_acc(v, dr=dr, par=par):
                            return v.tensor_add(
                                yTv(dr), yTv(dr),
                                psDb[:, par * TOWN:(par + 1) * TOWN])
                        ta = pl.step("vector", f_acc, inc=("dve", 1),
                                     waits=later(td, yadd[dr], barrierF))
                        psD_free[par] = ta
                        yadd[dr] = ta
                    h_free[0] = later(*down_toks)
                    twb_free[ep] = later(*down_toks)

            marks.append((len(pl.steps), dict(pl.val)))
            if upto < 5:
                cut, snap = marks[upto]
                pl.steps = pl.steps[:cut]
                pl.val = dict(snap)
                n_out_dma[0] = snap.get("odma", 0) // 16
                yadd = [(s, v) for s, v in snap.items()]

            def f_out(sync):
                sync.dma_start(
                    out=out[:, :].rearrange("(r p) n -> p r n", p=P),
                    in_=yT[:].rearrange("p (r n) -> p r n", n=TOWN)
                ).then_inc(SEM["odma"], 16)
                return None
            pl.step("sync", f_out, dma_n=1, dma_sem="odma",
                    waits=later(*yadd))
            n_out_dma[0] += 1

            def f_fin(sync):
                return sync.nop()
            pl.step("sync", f_fin, waits=[("odma", 16 * n_out_dma[0])])

            for name in ("xdma", "wdma", "odma", "pe", "act", "dve", "gp"):
                SEM[name] = es.enter_context(nc.semaphore(name))

            block = es.enter_context(nc.Block())

            def emit_engine(eng_name, eng):
                emitted = {}
                for st in pl.steps:
                    if st["eng"] != eng_name:
                        continue
                    for s, v in st["waits"]:
                        if emitted.get(s, 0) < v:
                            eng.wait_ge(SEM[s], v)
                            emitted[s] = v
                    inst = st["fn"](eng)
                    if st["inc"] is not None:
                        s, a = st["inc"]
                        assert inst is not None
                        inst.then_inc(SEM[s], a)

            @block.sync
            def _(sync):
                emit_engine("sync", sync)

            @block.gpsimd
            def _(gp):
                emit_engine("gpsimd", gp)

            @block.tensor
            def _(t):
                emit_engine("tensor", t)

            @block.vector
            def _(v):
                emit_engine("vector", v)

            @block.scalar
            def _(a):
                emit_engine("scalar", a)

    return nc, dbg


def make_inputs(inp):
    x = np.asarray(inp["L0_attn_x"], np.float32)
    wq = np.asarray(inp["L0_attn_Wq"], np.float32)
    wk = np.asarray(inp["L0_attn_Wk"], np.float32)
    wv = np.asarray(inp["L0_attn_Wv"], np.float32)
    memory = np.ascontiguousarray(np.asarray(inp["L0_attn_memory"], np.float32))
    memnorm = np.ascontiguousarray(np.asarray(inp["L0_attn_memory_norm"], np.float32))
    memw = np.asarray(inp["L0_attn_memory_weight"], np.float32).reshape(1, 1)
    gw = np.asarray(inp["L0_ffn_gate_w"], np.float32)
    gp = np.asarray(inp["L0_ffn_gate_proj"], np.float32)
    up = np.asarray(inp["L0_ffn_up_proj"], np.float32)
    dp = np.asarray(inp["L0_ffn_down_proj"], np.float32)

    shared = {
        "wqT": np.ascontiguousarray(wq.T),
        "wkT": np.ascontiguousarray(wk.T),
        "wvT": np.ascontiguousarray(wv.T),
        "mem": memory,
        "memnorm": np.ascontiguousarray(memnorm.reshape(4, P).T),
        "memw": np.ascontiguousarray(memw),
        "gwT": np.ascontiguousarray(gw.T),
        "gpT": np.ascontiguousarray(gp.transpose(0, 2, 1)),
        "upT": np.ascontiguousarray(up.transpose(0, 2, 1)),
        "dpT": np.ascontiguousarray(dp.transpose(0, 2, 1)),
    }
    in_maps = []
    for c in range(8):
        beta, r0 = c // 4, (c % 4) * TOWN
        m = dict(shared)
        m["x"] = np.ascontiguousarray(x[beta])
        m["xq"] = np.ascontiguousarray(x[beta, r0:r0 + TOWN])
        in_maps.append(m)
    return in_maps


def assemble(results):
    outp = np.empty((2, SEQ, D), np.float32)
    for c in range(8):
        beta, r0 = c // 4, (c % 4) * TOWN
        outp[beta, r0:r0 + TOWN, :] = np.asarray(results[c]["out"]).T
    return outp


_NC_CACHE = {}


def kernel(**inputs):
    from concourse.bass_utils import run_bass_kernel_spmd
    in_maps = make_inputs(inputs)
    if "nc" not in _NC_CACHE:
        nc, _ = build(debug=False)
        _NC_CACHE["nc"] = nc
    res = run_bass_kernel_spmd(_NC_CACHE["nc"], in_maps, core_ids=list(range(8)))
    return assemble(res.results)


# version 07
# speedup vs baseline: 1.1380x; 1.0436x over previous
import os
import numpy as np
import concourse.bass as bass
import concourse.mybir as mybir
from contextlib import ExitStack
from types import SimpleNamespace

F32 = mybir.dt.float32
BF16 = mybir.dt.bfloat16
AF = mybir.ActivationFunctionType
ALU = mybir.AluOpType
AX = mybir.AxisListType.X

P = 128
D = 2048
SEQ = 4096
TOWN = 1024
CT = D // P
DKV = 512
NQ, NH, DH = 4, 8, 64
E, FD, FRT = 8, 1408, 11
CAP = 384
NSC = CAP // P
EPS = 1e-5
NKT = SEQ // P
NCH = SEQ // 512
SCALE = 0.125

ENGS = ("sync", "gpsimd", "tensor", "vector", "scalar")


class Plan:
    def __init__(self):
        self.steps = []
        self.val = {}

    def cur(self, sem):
        v = self.val.get(sem, 0)
        return (sem, v) if v else None

    def step(self, eng, fn, inc=None, waits=(), dma_n=0, dma_sem=None):
        w = []
        for t in waits:
            if t is None:
                continue
            if isinstance(t, list):
                w.extend(xx for xx in t if xx is not None)
            else:
                w.append(t)
        self.steps.append(dict(eng=eng, fn=fn, inc=inc, waits=w))
        if inc is not None:
            s, a = inc
            self.val[s] = self.val.get(s, 0) + a
            return (s, self.val[s])
        if dma_n:
            self.val[dma_sem] = self.val.get(dma_sem, 0) + 16 * dma_n
            return (dma_sem, self.val[dma_sem])
        return None


def later(*toks):
    best = {}
    for t in toks:
        if t is None:
            continue
        if isinstance(t, list):
            for s, v in t:
                best[s] = max(best.get(s, 0), v)
        else:
            s, v = t
            best[s] = max(best.get(s, 0), v)
    return [(s, v) for s, v in best.items()]


def build(debug=False, upto=4):
    nc = bass.Bass("TRN2", target_bir_lowering=False)

    x = nc.dram_tensor("x", [SEQ, D], F32, kind="ExternalInput")
    xq = nc.dram_tensor("xq", [TOWN, D], F32, kind="ExternalInput")
    wqT = nc.dram_tensor("wqT", [D, D], F32, kind="ExternalInput")
    wkT = nc.dram_tensor("wkT", [D, DKV], F32, kind="ExternalInput")
    wvT = nc.dram_tensor("wvT", [D, DKV], F32, kind="ExternalInput")
    mem = nc.dram_tensor("mem", [DKV, DKV], F32, kind="ExternalInput")
    memnorm = nc.dram_tensor("memnorm", [P, 4], F32, kind="ExternalInput")
    memw = nc.dram_tensor("memw", [1, 1], F32, kind="ExternalInput")
    gwT = nc.dram_tensor("gwT", [D, E], F32, kind="ExternalInput")
    gpT = nc.dram_tensor("gpT", [E, D, FD], F32, kind="ExternalInput")
    upT = nc.dram_tensor("upT", [E, D, FD], F32, kind="ExternalInput")
    dpT = nc.dram_tensor("dpT", [E, FD, D], F32, kind="ExternalInput")
    out = nc.dram_tensor("out", [D, TOWN], F32, kind="ExternalOutput")

    dbg = {}
    if debug:
        dbg["kT"] = nc.dram_tensor("d_kT", [P, 4 * SEQ], BF16, kind="ExternalOutput")
        dbg["vb"] = nc.dram_tensor("d_vb", [P, NKT * 520], BF16, kind="ExternalOutput")
        dbg["qT"] = nc.dram_tensor("d_qT", [P, CT * TOWN], BF16, kind="ExternalOutput")
        dbg["y1"] = nc.dram_tensor("d_y1", [P, CT * TOWN], F32, kind="ExternalOutput")
        dbg["y2"] = nc.dram_tensor("d_y2", [P, CT * TOWN], F32, kind="ExternalOutput")
        dbg["ynT"] = nc.dram_tensor("d_ynT", [P, CT * TOWN], BF16, kind="ExternalOutput")
        dbg["tw"] = nc.dram_tensor("d_tw", [P, 64], F32, kind="ExternalOutput")

    pl = Plan()
    SEM = {}
    n_out_dma = [0]
    marks = []

    es = ExitStack()
    with es:
        yT = es.enter_context(nc.sbuf_tensor("yT", [P, CT * TOWN], F32))
        identbf = es.enter_context(nc.sbuf_tensor("identbf", [P, P], BF16))
        ident32 = es.enter_context(nc.sbuf_tensor("ident32", [P, P], F32))
        onesbf = es.enter_context(nc.sbuf_tensor("onesbf", [1, P], BF16))
        onescol = es.enter_context(nc.sbuf_tensor("onescol", [P, 1], F32))
        epsc = es.enter_context(nc.sbuf_tensor("epsc", [P, 1], F32))
        wmix = es.enter_context(nc.sbuf_tensor("wmix", [1, 4], F32))
        m2s = es.enter_context(nc.sbuf_tensor("m2s", [P, 8], F32))
        r2s = es.enter_context(nc.sbuf_tensor("r2s", [P, 8], F32))

        def yTv(rt):
            return yT[:, rt * TOWN:(rt + 1) * TOWN]

        BC = SimpleNamespace(bhi=None, blo=None, blt=None, free=None)

        def emit_bcast(src_ap, n, ps_ap, wait_src):
            bhi, blo, blt = BC.bhi, BC.blo, BC.blt

            def f_hi(v):
                return v.tensor_copy(bhi[:, 0:n], src_ap)
            th = pl.step("vector", f_hi, inc=("dve", 1),
                         waits=later(wait_src, BC.free))

            def f_lo(v):
                return v.scalar_tensor_tensor(blt[:, 0:n], bhi[:, 0:n], -1.0,
                                              src_ap, ALU.mult, ALU.add)
            tl = pl.step("vector", f_lo, inc=("dve", 1), waits=[th])

            def f_lo2(v):
                return v.tensor_copy(blo[:, 0:n], blt[:, 0:n])
            tl2 = pl.step("vector", f_lo2, inc=("dve", 1), waits=[tl])

            def f_mm(t):
                last = None
                for j in range(0, n, 512):
                    j1 = min(n, j + 512)
                    t.matmul(ps_ap[:, j:j1], onesbf[:], bhi[:, j:j1],
                             start=True, stop=False)
                    last = t.matmul(ps_ap[:, j:j1], onesbf[:], blo[:, j:j1],
                                    start=False, stop=True)
                return last
            tmm = pl.step("tensor", f_mm, inc=("pe", 1), waits=later(th, tl2))
            BC.free = tmm
            return tmm

        def emit_ln_stats(x_ap, st6, sub, mean_ap, var_ap, rstd_ap, nb_ap,
                          waits):
            def f_stats(v):
                last = None
                for j in range(4):
                    last = v.bn_stats(st6[:, j * 6:(j + 1) * 6],
                                      x_ap[:, j * 512:(j + 1) * 512])
                return last
            t1 = pl.step("vector", f_stats, inc=("dve", 1), waits=waits)
            st6v = st6[:].rearrange("p (a b) -> p a b", b=3)
            means = st6v[:, :, 1:2].rearrange("p a one -> p (a one)")
            cvars = st6v[:, :, 2:3].rearrange("p a one -> p (a one)")

            def f_red(v):
                v.reduce_sum(mean_ap, means, axis=AX)
                return v.reduce_sum(var_ap, cvars, axis=AX)
            t2 = pl.step("vector", f_red, inc=("dve", 1), waits=[t1])

            def f_cp(v):
                return v.tensor_copy(sub[:, 8:16], means)
            t2b = pl.step("vector", f_cp, inc=("dve", 1), waits=[t1])

            def f_mn(v):
                return v.tensor_scalar(mean_ap, mean_ap, 0.125, 0.0,
                                       ALU.mult, ALU.add)
            t3 = pl.step("vector", f_mn, inc=("dve", 1), waits=[t2])

            def f_dev(v):
                return v.tensor_scalar(sub[:, 0:8], sub[:, 8:16], mean_ap, 0.0,
                                       ALU.subtract, ALU.add)
            t4 = pl.step("vector", f_dev, inc=("dve", 1), waits=[t3, t2b])

            def f_sq(v):
                return v.tensor_mul(sub[:, 0:8], sub[:, 0:8], sub[:, 0:8])
            t5 = pl.step("vector", f_sq, inc=("dve", 1), waits=[t4])

            def f_vs(v):
                return v.reduce_sum(sub[:, 16:17], sub[:, 0:8], axis=AX)
            t6 = pl.step("vector", f_vs, inc=("dve", 1), waits=[t5])

            def f_v1(v):
                return v.tensor_scalar(var_ap, var_ap, 1.0 / 2048.0, 0.0,
                                       ALU.mult, ALU.add)
            t7 = pl.step("vector", f_v1, inc=("dve", 1), waits=[t2])

            def f_v2(v):
                return v.scalar_tensor_tensor(var_ap, sub[:, 16:17], 0.125,
                                              var_ap, ALU.mult, ALU.add)
            t8 = pl.step("vector", f_v2, inc=("dve", 1), waits=[t6, t7])

            def f_rstd(a):
                a.activation(rstd_ap, var_ap, AF.Ln, bias=epsc[:])
                return a.activation(rstd_ap, rstd_ap, AF.Exp, scale=-0.5)
            t9 = pl.step("scalar", f_rstd, inc=("act", 1), waits=[t8])

            def f_nb(v):
                return v.scalar_tensor_tensor(nb_ap, mean_ap, -1.0, rstd_ap,
                                              ALU.mult, ALU.mult)
            t10 = pl.step("vector", f_nb, inc=("dve", 1), waits=[t9])
            return t9, t10

        esM = ExitStack()
        last_evac = {"kT": None, "vb": None, "qT": None, "yT": None}
        CX = SimpleNamespace(qTv=None)
        with esM:
            kT = esM.enter_context(nc.sbuf_tensor("kT", [P, 4 * SEQ], BF16))
            vb = esM.enter_context(nc.sbuf_tensor("vb", [P, NKT * 520], BF16))

            def kTv(r):
                return kT[:, r * SEQ:(r + 1) * SEQ]

            def f_ident(gp):
                gp.memset(identbf[:], 0.0)
                gp.affine_select(out=identbf[:], in_=identbf[:],
                                 compare_op=ALU.not_equal, fill=1.0, base=0,
                                 pattern=[[-1, P]], channel_multiplier=1)
                gp.memset(ident32[:], 0.0)
                return gp.affine_select(out=ident32[:], in_=ident32[:],
                                        compare_op=ALU.not_equal, fill=1.0,
                                        base=0, pattern=[[-1, P]],
                                        channel_multiplier=1)
            t_ident = pl.step("gpsimd", f_ident, inc=("gp", 1))

            def f_const(v):
                v.memset(onesbf[:], 1.0)
                v.memset(onescol[:], 1.0)
                v.memset(epsc[:], EPS)
                return v.memset(vb[:], 1.0)
            t_const = pl.step("vector", f_const, inc=("dve", 1))

            def f_mw(sync):
                sync.dma_start(out=wmix[0:1, 0:1], in_=memw[:, :]
                               ).then_inc(SEM["xdma"], 16)
                return None
            t_mw = pl.step("sync", f_mw, dma_n=1, dma_sem="xdma")

            def f_s1(a):
                return a.activation(wmix[0:1, 1:2], wmix[0:1, 0:1], AF.Exp,
                                    scale=-1.0)
            t_s1 = pl.step("scalar", f_s1, inc=("act", 1), waits=[t_mw])

            def f_s2(v):
                return v.tensor_scalar(wmix[0:1, 2:3], wmix[0:1, 1:2], 1.0, 0.0,
                                       ALU.add, ALU.add)
            t_s2 = pl.step("vector", f_s2, inc=("dve", 1), waits=[t_s1])

            def f_s3(v):
                return v.reciprocal(wmix[0:1, 0:1], wmix[0:1, 2:3])
            t_s3 = pl.step("vector", f_s3, inc=("dve", 1), waits=[t_s2])

            def f_s4(v):
                return v.tensor_mul(wmix[0:1, 1:2], wmix[0:1, 1:2],
                                    wmix[0:1, 0:1])
            t_s4 = pl.step("vector", f_s4, inc=("dve", 1), waits=[t_s3])

            def emit_p0_pass(B, R, srcten, n_chunks, own, t_w):
                for ch in range(n_chunks):
                    cp = ch % B.xnTn
                    ev_toks = []
                    for i in range(4):
                        tt = ch * 4 + i
                        tp = tt % B.xtn
                        xtap = B.xt[:, tp * D:(tp + 1) * D]

                        def f_x(sync, tt=tt, tp=tp, B=B, srcten=srcten):
                            sync.dma_start(out=B.xt[:, tp * D:(tp + 1) * D],
                                           in_=srcten[tt * P:(tt + 1) * P, :]
                                           ).then_inc(SEM["xdma"], 16)
                            return None
                        tx = pl.step("sync", f_x, dma_n=1, dma_sem="xdma",
                                     waits=later(R.xt_free[tp]))
                        sp_ = tt % 2
                        t_rs, t_nb = emit_ln_stats(
                            xtap, B.st6[:, sp_ * 24:(sp_ + 1) * 24],
                            B.sub[:, sp_ * 17:(sp_ + 1) * 17],
                            B.mn0[:, sp_:sp_ + 1], B.vr0[:, sp_:sp_ + 1],
                            B.rs0[:, sp_:sp_ + 1], B.nb0[:, sp_:sp_ + 1],
                            later(tx, R.st_free[sp_]))
                        R.st_free[sp_] = None

                        def f_aff(a, xtap=xtap, B=B, sp_=sp_):
                            return a.activation(B.xn[:], xtap, AF.Identity,
                                                bias=B.nb0[:, sp_:sp_ + 1],
                                                scale=B.rs0[:, sp_:sp_ + 1])
                        t_xn = pl.step("scalar", f_aff, inc=("act", 1),
                                       waits=later(t_rs, t_nb, R.xn_free[0]))
                        R.st_free[sp_] = t_xn

                        xt_consumers = [t_xn, t_rs]
                        if own:
                            for g in range(4):
                                gb = g % 2

                                def f_trx(t, g=g, gb=gb, xtap=xtap, B=B):
                                    last = None
                                    for idx in range(4):
                                        c = g * 4 + idx
                                        last = t.transpose(
                                            B.psX[:, gb * 512 + idx * 128:
                                                  gb * 512 + (idx + 1) * 128],
                                            xtap[:, c * P:(c + 1) * P],
                                            ident32[:])
                                    return last
                                ttr = pl.step("tensor", f_trx, inc=("pe", 1),
                                              waits=later(tx, R.psX_free[gb],
                                                          t_ident))

                                def f_evx(v, g=g, gb=gb, tt=tt, B=B):
                                    last = None
                                    for idx in range(4):
                                        c = g * 4 + idx
                                        last = v.tensor_copy(
                                            yTv(c)[:, tt * P:(tt + 1) * P],
                                            B.psX[:, gb * 512 + idx * 128:
                                                  gb * 512 + (idx + 1) * 128])
                                    return last
                                tev = pl.step("vector", f_evx, inc=("dve", 1),
                                              waits=[ttr])
                                R.psX_free[gb] = tev
                                xt_consumers.append(ttr)
                                last_evac["yT"] = tev
                        R.xt_free[tp] = later(*xt_consumers)

                        tr_toks = []
                        for g in range(2):
                            def f_trn(t, g=g, B=B):
                                last = None
                                for idx in range(8):
                                    c = g * 8 + idx
                                    last = t.transpose(
                                        B.psTR[:, g * 1024 + idx * P:
                                               g * 1024 + (idx + 1) * P],
                                        B.xn[:, c * P:(c + 1) * P], identbf[:])
                                return last
                            ttr = pl.step("tensor", f_trn, inc=("pe", 1),
                                          waits=later(t_xn, R.psTR_free[g],
                                                      t_ident))

                            def f_evn(eng, g=g, cp=cp, i=i, B=B):
                                cp_fn = (eng.copy if g == 0
                                         else eng.tensor_copy)
                                last = None
                                for idx in range(8):
                                    c = g * 8 + idx
                                    last = cp_fn(
                                        B.xnT[:, cp * CT * 512 + c * 512 +
                                              i * P:cp * CT * 512 + c * 512 +
                                              (i + 1) * P],
                                        B.psTR[:, g * 1024 + idx * P:
                                               g * 1024 + (idx + 1) * P])
                                return last
                            ev_eng = "scalar" if g == 0 else "vector"
                            tev = pl.step(ev_eng, f_evn,
                                          inc=("act" if g == 0 else "dve", 1),
                                          waits=later(ttr, R.xnT_free[cp]))
                            R.psTR_free[g] = tev
                            tr_toks.append(ttr)
                            ev_toks.append(tev)
                        R.xn_free[0] = later(*tr_toks)

                    if not own:
                        for r in range(4):
                            kp = r % 2

                            def f_k(t, r=r, cp=cp, kp=kp, B=B):
                                last = None
                                for c in range(CT):
                                    last = t.matmul(
                                        B.psKQ[:, kp * 512:(kp + 1) * 512],
                                        B.wk_sb[:, c * DKV + r * P:
                                                c * DKV + (r + 1) * P],
                                        B.xnT[:, cp * CT * 512 + c * 512:
                                              cp * CT * 512 + (c + 1) * 512],
                                        start=(c == 0), stop=(c == CT - 1))
                                return last
                            tk = pl.step("tensor", f_k, inc=("pe", 1),
                                         waits=later(*ev_toks, R.psKQ_free[kp],
                                                     t_w))

                            def f_ek(v, r=r, ch=ch, kp=kp, B=B):
                                return v.tensor_copy(
                                    kTv(r)[:, ch * 512:(ch + 1) * 512],
                                    B.psKQ[:, kp * 512:(kp + 1) * 512])
                            te = pl.step("vector", f_ek, inc=("dve", 1),
                                         waits=[tk])
                            R.psKQ_free[kp] = te
                            last_evac["kT"] = te
                        for i in range(4):
                            vp = i % 2
                            kt = ch * 4 + i

                            def f_v(t, i=i, cp=cp, vp=vp, B=B):
                                last = None
                                for c in range(CT):
                                    last = t.matmul(
                                        B.psV[:, vp * 512:(vp + 1) * 512],
                                        B.xnT[:, cp * CT * 512 + c * 512 +
                                              i * P:cp * CT * 512 + c * 512 +
                                              (i + 1) * P],
                                        B.wv_sb[:, c * DKV:(c + 1) * DKV],
                                        start=(c == 0), stop=(c == CT - 1))
                                return last
                            tv = pl.step("tensor", f_v, inc=("pe", 1),
                                         waits=later(*ev_toks, R.psV_free[vp],
                                                     t_w))

                            def f_ev(v, kt=kt, vp=vp, B=B):
                                dst = vb[:, kt * 520:(kt + 1) * 520
                                         ].rearrange("p (h d) -> p h d",
                                                     d=65)[:, :, 0:64]
                                srcv = B.psV[:, vp * 512:(vp + 1) * 512
                                             ].rearrange("p (h d) -> p h d",
                                                         d=64)
                                return v.tensor_copy(dst, srcv)
                            te = pl.step("vector", f_ev, inc=("dve", 1),
                                         waits=later(tv, t_const))
                            R.psV_free[vp] = te
                            last_evac["vb"] = te
                    else:
                        nslot = len(R.wqs_free)
                        for rt in range(CT):
                            kp = rt % 2
                            ws = rt % nslot

                            def f_wq(gp, rt=rt, ws=ws, B=B):
                                gp.dma_start(
                                    out=B.wqs[:, ws * CT * P:(ws + 1) * CT * P
                                              ].rearrange("p (c j) -> p c j",
                                                          j=P),
                                    in_=wqT[:, rt * P:(rt + 1) * P
                                            ].rearrange("(c p) j -> p c j",
                                                        p=P)
                                ).then_inc(SEM["wdma"], 16)
                                return None
                            twq = pl.step("gpsimd", f_wq, dma_n=1,
                                          dma_sem="wdma",
                                          waits=later(R.wqs_free[ws]))

                            def f_q(t, cp=cp, kp=kp, ws=ws, B=B):
                                last = None
                                for c in range(CT):
                                    last = t.matmul(
                                        B.psKQ[:, kp * 512:(kp + 1) * 512],
                                        B.wqs[:, ws * CT * P + c * P:
                                              ws * CT * P + (c + 1) * P],
                                        B.xnT[:, cp * CT * 512 + c * 512:
                                              cp * CT * 512 + (c + 1) * 512],
                                        start=(c == 0), stop=(c == CT - 1))
                                return last
                            tq = pl.step("tensor", f_q, inc=("pe", 1),
                                         waits=later(*ev_toks, R.psKQ_free[kp],
                                                     twq))
                            R.wqs_free[ws] = tq

                            def f_eq(v, rt=rt, ch=ch, kp=kp, B=B):
                                return v.tensor_copy(
                                    CX.qTv(rt)[:, ch * 512:(ch + 1) * 512],
                                    B.psKQ[:, kp * 512:(kp + 1) * 512])
                            te = pl.step("vector", f_eq, inc=("dve", 1),
                                         waits=[tq])
                            R.psKQ_free[kp] = te
                            last_evac["qT"] = te
                    R.xnT_free[cp] = ("pe", pl.val["pe"])

            esA = ExitStack()
            with esA:
                B1 = SimpleNamespace()
                B1.wk_sb = esA.enter_context(nc.sbuf_tensor("wk_sb", [P, CT * DKV], BF16))
                B1.wv_sb = esA.enter_context(nc.sbuf_tensor("wv_sb", [P, CT * DKV], BF16))
                B1.xt = esA.enter_context(nc.sbuf_tensor("xt", [P, 2 * D], F32))
                B1.xn = esA.enter_context(nc.sbuf_tensor("xn", [P, D], BF16))
                B1.xnT = esA.enter_context(nc.sbuf_tensor("xnT", [P, CT * 512], BF16))
                B1.xtn, B1.xnTn = 2, 1
                B1.st6 = esA.enter_context(nc.sbuf_tensor("st6", [P, 48], F32))
                B1.sub = esA.enter_context(nc.sbuf_tensor("sub", [P, 34], F32))
                B1.mn0 = esA.enter_context(nc.sbuf_tensor("mn0", [P, 2], F32))
                B1.vr0 = esA.enter_context(nc.sbuf_tensor("vr0", [P, 2], F32))
                B1.rs0 = esA.enter_context(nc.sbuf_tensor("rs0", [P, 2], F32))
                B1.nb0 = esA.enter_context(nc.sbuf_tensor("nb0", [P, 2], F32))
                B1.wqs = None
                B1.psTR = esA.enter_context(nc.psum_tensor("psTR", [P, 2 * 1024], BF16))
                B1.psKQ = esA.enter_context(nc.psum_tensor("psKQ", [P, 2 * 512], F32))
                B1.psV = esA.enter_context(nc.psum_tensor("psV", [P, 2 * 512], F32))
                B1.psX = esA.enter_context(nc.psum_tensor("psX", [P, 2 * 512], F32))

                def f_wkv(gp, B=B1):
                    gp.dma_start(out=B.wk_sb[:].rearrange("p (c j) -> p c j",
                                                          j=DKV),
                                 in_=wkT[:, :].rearrange("(c p) j -> p c j",
                                                         p=P)
                                 ).then_inc(SEM["wdma"], 16)
                    gp.dma_start(out=B.wv_sb[:].rearrange("p (c j) -> p c j",
                                                          j=DKV),
                                 in_=wvT[:, :].rearrange("(c p) j -> p c j",
                                                         p=P)
                                 ).then_inc(SEM["wdma"], 16)
                    return None
                t_wkv = pl.step("gpsimd", f_wkv, dma_n=2, dma_sem="wdma",
                                waits=[t_ident])

                R1 = SimpleNamespace(
                    st_free=[None, None],
                    xt_free=[None, None], xn_free=[None],
                    xnT_free=[None], psTR_free=[None, None],
                    psKQ_free=[None, None], psV_free=[None, None],
                    psX_free=[None, None], wqs_free=[None])
                emit_p0_pass(B1, R1, x, NCH, False, t_wkv)

            qT = esM.enter_context(nc.sbuf_tensor("qT", [P, CT * TOWN], BF16))

            def qTv(r):
                return qT[:, r * TOWN:(r + 1) * TOWN]
            CX.qTv = qTv

            kv_barrier = later(pl.cur("pe"), pl.cur("dve"), pl.cur("act"))
            esA2 = ExitStack()
            with esA2:
                B2 = SimpleNamespace()
                B2.xt = esA2.enter_context(nc.sbuf_tensor("xt2", [P, D], F32))
                B2.xn = esA2.enter_context(nc.sbuf_tensor("xn2", [P, D], BF16))
                B2.xnT = esA2.enter_context(nc.sbuf_tensor("xnT2", [P, CT * 512], BF16))
                B2.xtn, B2.xnTn = 1, 1
                B2.st6 = esA2.enter_context(nc.sbuf_tensor("st62", [P, 48], F32))
                B2.sub = esA2.enter_context(nc.sbuf_tensor("sub2", [P, 34], F32))
                B2.mn0 = esA2.enter_context(nc.sbuf_tensor("mn02", [P, 2], F32))
                B2.vr0 = esA2.enter_context(nc.sbuf_tensor("vr02", [P, 2], F32))
                B2.rs0 = esA2.enter_context(nc.sbuf_tensor("rs02", [P, 2], F32))
                B2.nb0 = esA2.enter_context(nc.sbuf_tensor("nb02", [P, 2], F32))
                B2.wqs = esA2.enter_context(nc.sbuf_tensor("wqs", [P, 2 * CT * P], BF16))
                B2.wk_sb = None
                B2.wv_sb = None
                B2.psTR = esA2.enter_context(nc.psum_tensor("psTR2", [P, 2 * 1024], BF16))
                B2.psKQ = esA2.enter_context(nc.psum_tensor("psKQ2", [P, 2 * 512], F32))
                B2.psV = None
                B2.psX = esA2.enter_context(nc.psum_tensor("psX2", [P, 2 * 512], F32))

                R2 = SimpleNamespace(
                    st_free=[kv_barrier, kv_barrier],
                    xt_free=[kv_barrier], xn_free=[kv_barrier],
                    xnT_free=[kv_barrier],
                    psTR_free=[kv_barrier, kv_barrier],
                    psKQ_free=[kv_barrier, kv_barrier],
                    psV_free=[None, None],
                    psX_free=[kv_barrier, kv_barrier],
                    wqs_free=[kv_barrier, kv_barrier])
                emit_p0_pass(B2, R2, xq, 2, True, None)

            dbg_gate = None
            if debug:
                def f_d0(sync):
                    sync.dma_start(out=dbg["kT"][:, :], in_=kT[:]
                                   ).then_inc(SEM["odma"], 16)
                    sync.dma_start(out=dbg["vb"][:, :], in_=vb[:]
                                   ).then_inc(SEM["odma"], 16)
                    sync.dma_start(out=dbg["qT"][:, :], in_=qT[:]
                                   ).then_inc(SEM["odma"], 16)
                    sync.dma_start(out=dbg["y1"][:, :], in_=yT[:]
                                   ).then_inc(SEM["odma"], 16)
                    return None
                dbg_gate = pl.step("sync", f_d0, dma_n=4, dma_sem="odma",
                                   waits=later(pl.cur("dve")))
                n_out_dma[0] += 4

            marks.append((len(pl.steps), dict(pl.val)))
            esB = ExitStack()
            with esB:
                mem_sb = esB.enter_context(nc.sbuf_tensor("mem_sb", [P, 4 * DKV], BF16))
                mn_sb = esB.enter_context(nc.sbuf_tensor("mn_sb", [P, 4], BF16))
                sig = esB.enter_context(nc.sbuf_tensor("sig", [P, 4 * TOWN], BF16))
                tmin = esB.enter_context(nc.sbuf_tensor("tmin", [P, TOWN], F32))
                e32 = esB.enter_context(nc.sbuf_tensor("e32", [P, TOWN], F32))
                tmax = esB.enter_context(nc.sbuf_tensor("tmax", [P, TOWN], BF16))
                dn = esB.enter_context(nc.sbuf_tensor("dn", [1, TOWN], F32))
                rw = esB.enter_context(nc.sbuf_tensor("rw", [1, TOWN], F32))
                rwb = esB.enter_context(nc.sbuf_tensor("rwb", [P, TOWN], F32))
                tmpa = esB.enter_context(nc.sbuf_tensor("tmpa", [P, TOWN], F32))
                BC.bhi = esB.enter_context(nc.sbuf_tensor("bhi1", [1, TOWN], BF16))
                BC.blo = esB.enter_context(nc.sbuf_tensor("blo1", [1, TOWN], BF16))
                BC.blt = esB.enter_context(nc.sbuf_tensor("blt1", [1, TOWN], F32))
                BC.free = None
                psM = esB.enter_context(nc.psum_tensor("psM", [P, 2 * TOWN], F32))
                psDen = esB.enter_context(nc.psum_tensor("psDen", [1, TOWN], F32))
                psB2 = esB.enter_context(nc.psum_tensor("psB2", [P, TOWN], F32))

                def f_mload(gp):
                    gp.dma_start(out=mem_sb[:].rearrange("p (k j) -> p k j",
                                                         j=DKV),
                                 in_=mem[:, :].rearrange("(k p) j -> p k j",
                                                         p=P)
                                 ).then_inc(SEM["wdma"], 16)
                    gp.dma_start(out=mn_sb[:], in_=memnorm[:, :]
                                 ).then_inc(SEM["wdma"], 16)
                    return None
                t_mem = pl.step("gpsimd", f_mload, dma_n=2, dma_sem="wdma",
                                waits=later(kv_barrier))

                q_done = later(pl.cur("dve"), dbg_gate)
                psM_free = [None, None]
                sig_consumed = later(pl.cur("pe"))
                rw_consumed = None
                for qg in range(NQ):
                    sig_toks = []
                    for rj in range(4):
                        rt = qg * 4 + rj

                        def f_min(v, rt=rt):
                            return v.tensor_scalar(tmin[:], qTv(rt), 0.0, 0.0,
                                                   ALU.min, ALU.add)
                        t1 = pl.step("vector", f_min, inc=("dve", 1),
                                     waits=later(q_done, sig_consumed))

                        def f_exp(a):
                            return a.activation(e32[:], tmin[:], AF.Exp)
                        t2 = pl.step("scalar", f_exp, inc=("act", 1),
                                     waits=[t1])

                        def f_max(v, rt=rt):
                            return v.tensor_scalar(tmax[:], qTv(rt), 0.0, 0.0,
                                                   ALU.max, ALU.add)
                        t3 = pl.step("vector", f_max, inc=("dve", 1),
                                     waits=later(q_done, sig_consumed))

                        def f_sig(v, rj=rj):
                            return v.scalar_tensor_tensor(
                                sig[:, rj * TOWN:(rj + 1) * TOWN], e32[:],
                                -1.0, tmax[:], ALU.add, ALU.add)
                        t4 = pl.step("vector", f_sig, inc=("dve", 1),
                                     waits=[t2, t3])
                        sig_toks.append(t4)

                    def f_dn(t):
                        last = None
                        for h in range(2):
                            for kt in range(4):
                                last = t.matmul(
                                    psDen[0:1, h * 512:(h + 1) * 512],
                                    mn_sb[:, kt:kt + 1],
                                    sig[:, kt * TOWN + h * 512:
                                        kt * TOWN + h * 512 + 512],
                                    start=(kt == 0), stop=(kt == 3))
                        return last
                    tdn = pl.step("tensor", f_dn, inc=("pe", 1),
                                  waits=later(*sig_toks, t_mem, rw_consumed))

                    def f_edn(v):
                        return v.tensor_copy(dn[:], psDen[:])
                    tedn = pl.step("vector", f_edn, inc=("dve", 1),
                                   waits=[tdn])

                    def f_rec(v):
                        return v.reciprocal(rw[:], dn[:])
                    trec = pl.step("vector", f_rec, inc=("dve", 1),
                                   waits=[tedn])

                    def f_rw(v):
                        return v.tensor_scalar(rw[:], rw[:], wmix[0:1, 0:1],
                                               0.0, ALU.mult, ALU.add)
                    trw = pl.step("vector", f_rw, inc=("dve", 1),
                                  waits=later(trec, t_s4))
                    tb = emit_bcast(rw[0:1, :], TOWN, psB2, [trw])
                    rw_consumed = tb

                    def f_erb(v):
                        return v.tensor_copy(rwb[:], psB2[:])
                    terb = pl.step("vector", f_erb, inc=("dve", 1), waits=[tb])

                    am_toks = []
                    for jt in range(4):
                        mp = jt % 2
                        qr2 = qg * 4 + jt

                        def f_am(t, jt=jt, mp=mp):
                            last = None
                            for h in range(2):
                                for kt in range(4):
                                    last = t.matmul(
                                        psM[:, mp * TOWN + h * 512:
                                            mp * TOWN + h * 512 + 512],
                                        mem_sb[:, kt * DKV + jt * P:
                                               kt * DKV + (jt + 1) * P],
                                        sig[:, kt * TOWN + h * 512:
                                            kt * TOWN + h * 512 + 512],
                                        start=(kt == 0), stop=(kt == 3))
                            return last
                        tam = pl.step("tensor", f_am, inc=("pe", 1),
                                      waits=later(*sig_toks, psM_free[mp],
                                                  t_mem))
                        am_toks.append(tam)

                        def f_mul(v, mp=mp):
                            return v.tensor_mul(
                                tmpa[:], psM[:, mp * TOWN:(mp + 1) * TOWN],
                                rwb[:])
                        tm1 = pl.step("vector", f_mul, inc=("dve", 1),
                                      waits=later(tam, terb))
                        psM_free[mp] = tm1

                        def f_add(v, qr2=qr2):
                            return v.tensor_add(yTv(qr2), yTv(qr2), tmpa[:])
                        pl.step("vector", f_add, inc=("dve", 1),
                                waits=later(tm1, last_evac["yT"]))
                    sig_consumed = later(tdn, *am_toks)

            marks.append((len(pl.steps), dict(pl.val)))
            esC = ExitStack()
            with esC:
                PT = esC.enter_context(nc.sbuf_tensor("PT", [P, 4 * 512], BF16))
                rsr = esC.enter_context(nc.sbuf_tensor("rsr", [1, 1024], F32))
                rsb = esC.enter_context(nc.sbuf_tensor("rsb", [P, 2 * 512], F32))
                tmp2 = esC.enter_context(nc.sbuf_tensor("tmp2", [P, 2 * 512], F32))
                stg = esC.enter_context(nc.sbuf_tensor("stg", [P, 2 * 1024], F32))
                BC.bhi = esC.enter_context(nc.sbuf_tensor("bhi2", [1, 1024], BF16))
                BC.blo = esC.enter_context(nc.sbuf_tensor("blo2", [1, 1024], BF16))
                BC.blt = esC.enter_context(nc.sbuf_tensor("blt2", [1, 1024], F32))
                BC.free = None
                psS = esC.enter_context(nc.psum_tensor("psS", [P, 4 * 512], F32))
                psPV = esC.enter_context(nc.psum_tensor("psPV", [P, 2 * 512], F32))
                psBC = esC.enter_context(nc.psum_tensor("psBC", [P, 2 * 512], F32))

                kv_ready = later(pl.cur("dve"))
                psS_free = [None, None]
                PT_free = [None, None]
                psPV_free = None
                rsr_free = None
                stg_free = [None, None]
                pending_tail = [None]
                iters = [(qg, hp, chn) for qg in range(NQ)
                         for hp in range(NH // 2) for chn in range(2)]
                for it, (qg, hp, chn) in enumerate(iters):
                        qr = qg * 4 + hp
                        if True:
                            pv_last = None

                            def emit_pv(kt, g):
                                nonlocal pv_last

                                def f_pv(t, hp=hp, kt=kt, g=g):
                                    t.matmul(
                                        psPV[0:65, 0:512],
                                        vb[:, kt * 520 + (2 * hp) * 65:
                                           kt * 520 + (2 * hp) * 65 + 65],
                                        PT[:, (2 * g) * 512:(2 * g + 1) * 512],
                                        start=(kt == 0),
                                        stop=(kt == NKT - 1))
                                    return t.matmul(
                                        psPV[0:65, 512:1024],
                                        vb[:, kt * 520 + (2 * hp + 1) * 65:
                                           kt * 520 + (2 * hp + 1) * 65 + 65],
                                        PT[:, (2 * g + 1) * 512:
                                           (2 * g + 2) * 512],
                                        start=(kt == 0),
                                        stop=(kt == NKT - 1))
                                tpv = pl.step("tensor", f_pv, inc=("pe", 1),
                                              waits=later(exp_toks[kt],
                                                          psPV_free
                                                          if kt == 0 else None))
                                PT_free[g] = tpv
                                pv_last = tpv

                            exp_toks = {}
                            for kt in range(NKT):
                                g = kt % 2

                                def f_sc(t, hp=hp, qr=qr, chn=chn, kt=kt, g=g):
                                    t.matmul(
                                        psS[:, (2 * g) * 512:
                                            (2 * g + 1) * 512],
                                        kTv(hp)[0:64, kt * P:(kt + 1) * P],
                                        qTv(qr)[0:64,
                                                chn * 512:(chn + 1) * 512],
                                        start=True, stop=True,
                                        tile_position=(0, 0))
                                    return t.matmul(
                                        psS[:, (2 * g + 1) * 512:
                                            (2 * g + 2) * 512],
                                        kTv(hp)[64:128, kt * P:(kt + 1) * P],
                                        qTv(qr)[64:128,
                                                chn * 512:(chn + 1) * 512],
                                        start=True, stop=True,
                                        tile_position=(64, 0))
                                tsc = pl.step("tensor", f_sc, inc=("pe", 1),
                                              waits=later(kv_ready,
                                                          psS_free[g]))

                                def f_exp(a, g=g):
                                    return a.activation(
                                        PT[:, (2 * g) * 512:(2 * g + 2) * 512],
                                        psS[:, (2 * g) * 512:
                                            (2 * g + 2) * 512],
                                        AF.Exp, scale=SCALE)
                                if kt >= 2:
                                    emit_pv(kt - 2, (kt - 2) % 2)
                                if kt == 12 and pending_tail[0] is not None:
                                    pending_tail[0]()
                                    pending_tail[0] = None
                                tex = pl.step("scalar", f_exp, inc=("act", 1),
                                              waits=later(tsc, PT_free[g]))
                                psS_free[g] = tex
                                exp_toks[kt] = tex
                            emit_pv(NKT - 2, (NKT - 2) % 2)
                            emit_pv(NKT - 1, (NKT - 1) % 2)

                            si = (qg * 8 + hp * 2 + chn) % 2
                            def f_stg(v, si=si):
                                return v.tensor_copy(
                                    stg[:, si * 1024:(si + 1) * 1024],
                                    psPV[:, 0:1024])
                            tstg = pl.step("vector", f_stg, inc=("dve", 1),
                                           waits=later(pv_last, stg_free[si]))
                            psPV_free = tstg
                            sv = stg[:, si * 1024:(si + 1) * 1024]

                            def make_tail(sv=sv, si=si, qr=qr, chn=chn,
                                          tstg=tstg):
                                def emit_tail():
                                    nonlocal rsr_free

                                    def f_rec(v):
                                        return v.reciprocal(rsr[:],
                                                            sv[64:65, :])
                                    tr1 = pl.step("vector", f_rec,
                                                  inc=("dve", 1),
                                                  waits=later(tstg, rsr_free))

                                    def f_rs(v):
                                        return v.tensor_scalar(
                                            rsr[:], rsr[:], wmix[0:1, 1:2],
                                            0.0, ALU.mult, ALU.add)
                                    tr2 = pl.step("vector", f_rs,
                                                  inc=("dve", 1),
                                                  waits=later(tr1, t_s4))
                                    tb = emit_bcast(rsr[0:1, :], 1024, psBC,
                                                    [tr2])
                                    rsr_free = tb

                                    def f_eb(v):
                                        return v.tensor_copy(rsb[:],
                                                             psBC[:, 0:1024])
                                    teb = pl.step("vector", f_eb,
                                                  inc=("dve", 1), waits=[tb])

                                    head_toks = []
                                    for hh in range(2):
                                        def f_mul(v, hh=hh):
                                            return v.tensor_mul(
                                                tmp2[hh * 64:(hh + 1) * 64,
                                                     hh * 512:(hh + 1) * 512],
                                                sv[0:64,
                                                   hh * 512:(hh + 1) * 512],
                                                rsb[0:64,
                                                    hh * 512:(hh + 1) * 512])
                                        tm1 = pl.step("vector", f_mul,
                                                      inc=("dve", 1),
                                                      waits=[teb])

                                        def f_add(v, hh=hh):
                                            dst = yTv(qr)[
                                                hh * 64:(hh + 1) * 64,
                                                chn * 512:(chn + 1) * 512]
                                            return v.tensor_add(
                                                dst, dst,
                                                tmp2[hh * 64:(hh + 1) * 64,
                                                     hh * 512:(hh + 1) * 512])
                                        tm2 = pl.step("vector", f_add,
                                                      inc=("dve", 1),
                                                      waits=[tm1])
                                        head_toks.append(tm2)
                                    stg_free[si] = later(*head_toks)
                                return emit_tail
                            pending_tail[0] = make_tail()

                if pending_tail[0] is not None:
                    pending_tail[0]()
                    pending_tail[0] = None

            dbg_gate2 = None
            if debug:
                def f_d2(sync):
                    sync.dma_start(out=dbg["y2"][:, :], in_=yT[:]
                                   ).then_inc(SEM["odma"], 16)
                    return None
                dbg_gate2 = pl.step("sync", f_d2, dma_n=1, dma_sem="odma",
                                    waits=later(pl.cur("dve")))
                n_out_dma[0] += 1

        marks.append((len(pl.steps), dict(pl.val)))
        barrierF = later(pl.cur("pe"), pl.cur("dve"), pl.cur("act"),
                         dbg_gate2)
        esF = ExitStack()
        with esF:
            x_tm = esF.enter_context(nc.sbuf_tensor("x_tm", [P, 8 * D], BF16))
            gw_sb = esF.enter_context(nc.sbuf_tensor("gw_sb", [P, CT * E], F32))
            ones32 = esF.enter_context(nc.sbuf_tensor("ones32", [1, P], F32))
            U128 = esF.enter_context(nc.sbuf_tensor("U128", [P, P], F32))
            iota_c = esF.enter_context(nc.sbuf_tensor("iota_c", [P, CAP], F32))
            twn = esF.enter_context(nc.sbuf_tensor("twn", [P, 64], F32))
            slotbuf = esF.enter_context(nc.sbuf_tensor("slotbuf", [P, 64], F32))
            mkbuf = esF.enter_context(nc.sbuf_tensor("mkbuf", [P, 8], F32))
            sltmp = esF.enter_context(nc.sbuf_tensor("sltmp", [P, 8], F32))
            offs = esF.enter_context(nc.sbuf_tensor("offs", [1, 8], F32))
            cntb = esF.enter_context(nc.sbuf_tensor("cntb", [1, 8], F32))
            BC.bhi = esF.enter_context(nc.sbuf_tensor("bhi3", [1, 64], BF16))
            BC.blo = esF.enter_context(nc.sbuf_tensor("blo3", [1, 64], BF16))
            BC.blt = esF.enter_context(nc.sbuf_tensor("blt3", [1, 64], F32))
            BC.free = None

            def f_gw(sync):
                sync.dma_start(out=gw_sb[:].rearrange("p (c e) -> p c e", e=E),
                               in_=gwT[:, :].rearrange("(c p) e -> p c e", p=P)
                               ).then_inc(SEM["xdma"], 16)
                return None
            t_gw = pl.step("sync", f_gw, dma_n=1, dma_sem="xdma",
                           waits=barrierF)

            def f_o32(v):
                v.memset(ones32[:], 1.0)
                return v.memset(offs[:], 0.0)
            t_o32 = pl.step("vector", f_o32, inc=("dve", 1), waits=barrierF)

            def f_consts2(gp):
                gp.memset(U128[:], 0.0)
                gp.affine_select(out=U128[:], in_=U128[:],
                                 compare_op=ALU.is_ge, fill=1.0, base=0,
                                 pattern=[[-1, P]], channel_multiplier=1)
                return gp.iota(iota_c[:], [[1, CAP]], channel_multiplier=0,
                               allow_small_or_imprecise_dtypes=True)
            t_U = pl.step("gpsimd", f_consts2, inc=("gp", 1), waits=barrierF)

            esD = ExitStack()
            with esD:
                ynat = esD.enter_context(nc.sbuf_tensor("ynat", [P, 2 * D], F32))
                st6b = esD.enter_context(nc.sbuf_tensor("st6b", [P, 24], F32))
                subb = esD.enter_context(nc.sbuf_tensor("subb", [P, 17], F32))
                vrb = esD.enter_context(nc.sbuf_tensor("vrb", [P, 1], F32))
                nbb = esD.enter_context(nc.sbuf_tensor("nbb", [P, 1], F32))
                L0T = esD.enter_context(nc.sbuf_tensor("L0T", [8, TOWN], F32))
                sgr = esD.enter_context(nc.sbuf_tensor("sgr", [8, 1], F32))
                sbc = esD.enter_context(nc.sbuf_tensor("sbc", [P, 8], F32))
                lgt = esD.enter_context(nc.sbuf_tensor("lgt", [P, 8], F32))
                lg2 = esD.enter_context(nc.sbuf_tensor("lg2", [P, 8], F32))
                msr = esD.enter_context(nc.sbuf_tensor("msr", [P, 1], F32))
                mm1 = esD.enter_context(nc.sbuf_tensor("mm1", [P, 1], F32))
                mm2 = esD.enter_context(nc.sbuf_tensor("mm2", [P, 1], F32))
                ee = esD.enter_context(nc.sbuf_tensor("ee", [P, 1], F32))
                ee2 = esD.enter_context(nc.sbuf_tensor("ee2", [P, 1], F32))
                w1 = esD.enter_context(nc.sbuf_tensor("w1", [P, 1], F32))
                w2 = esD.enter_context(nc.sbuf_tensor("w2", [P, 1], F32))
                mk1 = esD.enter_context(nc.sbuf_tensor("mk1", [P, 8], F32))
                mk2 = esD.enter_context(nc.sbuf_tensor("mk2", [P, 8], F32))
                psT3 = esD.enter_context(nc.psum_tensor("psT3", [P, 2048], F32))
                psR = esD.enter_context(nc.psum_tensor("psR", [8, 512], F32))
                psR2 = esD.enter_context(nc.psum_tensor("psR2", [P, 8], F32))
                psP = esD.enter_context(nc.psum_tensor("psP", [P, 8], F32))
                psC = esD.enter_context(nc.psum_tensor("psC", [1, 8], F32))

                psT3_free = [None] * 4
                ynat_free = [None, None]
                xtm_toks = []
                for tt in range(8):
                    tp = tt % 2
                    ev_toks = []
                    for g in range(4):
                        def f_tr(t, g=g, tt=tt):
                            last = None
                            for idx in range(4):
                                rt = g * 4 + idx
                                last = t.transpose(
                                    psT3[:, g * 512 + idx * 128:
                                         g * 512 + (idx + 1) * 128],
                                    yTv(rt)[:, tt * P:(tt + 1) * P],
                                    ident32[:])
                            return last
                        ttr = pl.step("tensor", f_tr, inc=("pe", 1),
                                      waits=later(barrierF, psT3_free[g],
                                                  ynat_free[tp]))

                        def f_ev(v, g=g, tp=tp):
                            return v.tensor_copy(
                                ynat[:, tp * D + g * 512:
                                     tp * D + (g + 1) * 512],
                                psT3[:, g * 512:(g + 1) * 512])
                        tev = pl.step("vector", f_ev, inc=("dve", 1),
                                      waits=[ttr])
                        psT3_free[g] = tev
                        ev_toks.append(tev)
                    ynap = ynat[:, tp * D:(tp + 1) * D]
                    t_rs, t_nb = emit_ln_stats(
                        ynap, st6b, subb, m2s[:, tt:tt + 1], vrb[:, 0:1],
                        r2s[:, tt:tt + 1], nbb[:, 0:1], ev_toks)

                    def f_aff(a, ynap=ynap, tt=tt):
                        return a.activation(x_tm[:, tt * D:(tt + 1) * D],
                                            ynap, AF.Identity,
                                            bias=nbb[:, 0:1],
                                            scale=r2s[:, tt:tt + 1])
                    t_yn = pl.step("scalar", f_aff, inc=("act", 1),
                                   waits=later(t_rs, t_nb))
                    ynat_free[tp] = later(t_yn, t_rs)
                    xtm_toks.append(t_yn)
                xtm_ready = later(*xtm_toks)

                l0_toks = []
                psR_free = None
                for h in range(2):
                    def f_l0(t, h=h):
                        last = None
                        for c in range(CT):
                            last = t.matmul(
                                psR[:, :], gw_sb[:, c * E:(c + 1) * E],
                                yTv(c)[:, h * 512:(h + 1) * 512],
                                start=(c == 0), stop=(c == CT - 1))
                        return last
                    tl0 = pl.step("tensor", f_l0, inc=("pe", 1),
                                  waits=later(t_gw, barrierF, psR_free))

                    def f_el0(v, h=h):
                        return v.tensor_copy(L0T[:, h * 512:(h + 1) * 512],
                                             psR[:])
                    te = pl.step("vector", f_el0, inc=("dve", 1), waits=[tl0])
                    psR_free = te
                    l0_toks.append(te)

                def f_se(t):
                    last = None
                    for c in range(CT):
                        last = t.matmul(psR[:, 0:1],
                                        gw_sb[:, c * E:(c + 1) * E],
                                        onescol[:],
                                        start=(c == 0), stop=(c == CT - 1))
                    return last
                tse = pl.step("tensor", f_se, inc=("pe", 1),
                              waits=later(t_gw, psR_free))

                def f_ese(v):
                    return v.tensor_copy(sgr[:], psR[:, 0:1])
                tese = pl.step("vector", f_ese, inc=("dve", 1), waits=[tse])
                psR_free = tese

                def f_trs(t):
                    return t.transpose(psR2[0:1, 0:8], sgr[:],
                                       ident32[0:8, 0:8])
                ttrs = pl.step("tensor", f_trs, inc=("pe", 1),
                               waits=later(tese, t_ident))

                def f_etrs(v):
                    return v.tensor_copy(BC.blt[:, 0:8], psR2[0:1, 0:8])
                tets = pl.step("vector", f_etrs, inc=("dve", 1),
                               waits=later(ttrs, BC.free))

                def f_sbm(t):
                    return t.matmul(psR2[:, 0:8], ones32[:],
                                    BC.blt[0:1, 0:8], start=True, stop=True)
                tsbm = pl.step("tensor", f_sbm, inc=("pe", 1),
                               waits=later(tets, t_o32))

                def f_esb(v):
                    return v.tensor_copy(sbc[:], psR2[:, 0:8])
                tesb = pl.step("vector", f_esb, inc=("dve", 1), waits=[tsbm])
                BC.free = tsbm
                psR2_free = tesb

                tw_toks = []
                slot_toks = []
                mk_free = None
                psC_free = None
                psP_free = None
                t_offp = None
                for tt in range(8):
                    def f_trl(t, tt=tt):
                        return t.transpose(psR2[:, 0:8],
                                           L0T[:, tt * P:(tt + 1) * P],
                                           ident32[0:8, 0:8])
                    ttl = pl.step("tensor", f_trl, inc=("pe", 1),
                                  waits=later(*l0_toks, psR2_free, t_ident))

                    def f_elg(v):
                        return v.tensor_copy(lgt[:], psR2[:, 0:8])
                    tlg = pl.step("vector", f_elg, inc=("dve", 1),
                                  waits=[ttl])
                    psR2_free = tlg

                    def f_msr(v, tt=tt):
                        return v.scalar_tensor_tensor(
                            msr[:], m2s[:, tt:tt + 1], -1.0,
                            r2s[:, tt:tt + 1], ALU.mult, ALU.mult)
                    t1 = pl.step("vector", f_msr, inc=("dve", 1),
                                 waits=[tlg])

                    def f_lg2(v, tt=tt):
                        return v.tensor_scalar(lg2[:], lgt[:],
                                               r2s[:, tt:tt + 1], 0.0,
                                               ALU.mult, ALU.add)
                    t2 = pl.step("vector", f_lg2, inc=("dve", 1), waits=[tlg])

                    def f_lg3(v):
                        return v.scalar_tensor_tensor(lg2[:], sbc[:], msr[:],
                                                      lg2[:], ALU.mult,
                                                      ALU.add)
                    t3 = pl.step("vector", f_lg3, inc=("dve", 1),
                                 waits=later(t1, t2, tesb))

                    def f_m1(v):
                        return v.reduce_max(mm1[:], lg2[:], axis=AX)
                    t4 = pl.step("vector", f_m1, inc=("dve", 1), waits=[t3])

                    def f_mk1(v):
                        return v.tensor_scalar(mk1[:], lg2[:], mm1[:], 0.0,
                                               ALU.subtract, ALU.is_ge)
                    t5 = pl.step("vector", f_mk1, inc=("dve", 1), waits=[t4])

                    def f_l2(v):
                        return v.scalar_tensor_tensor(lgt[:], mk1[:], -1e30,
                                                      lg2[:], ALU.mult,
                                                      ALU.add)
                    t6 = pl.step("vector", f_l2, inc=("dve", 1), waits=[t5])

                    def f_m2(v):
                        return v.reduce_max(mm2[:], lgt[:], axis=AX)
                    t7 = pl.step("vector", f_m2, inc=("dve", 1), waits=[t6])

                    def f_mk2(v):
                        return v.tensor_scalar(mk2[:], lgt[:], mm2[:], 0.0,
                                               ALU.subtract, ALU.is_ge)
                    t8 = pl.step("vector", f_mk2, inc=("dve", 1), waits=[t7])

                    def f_d(v):
                        return v.tensor_scalar(ee[:], mm2[:], mm1[:], 0.0,
                                               ALU.subtract, ALU.add)
                    t9 = pl.step("vector", f_d, inc=("dve", 1),
                                 waits=[t7, t4])

                    def f_ee(a):
                        return a.activation(ee2[:], ee[:], AF.Exp)
                    t10 = pl.step("scalar", f_ee, inc=("act", 1), waits=[t9])

                    def f_w1(v):
                        return v.tensor_scalar(w1[:], ee2[:], 1.0, 0.0,
                                               ALU.add, ALU.add)
                    t11 = pl.step("vector", f_w1, inc=("dve", 1), waits=[t10])

                    def f_w1r(v):
                        return v.reciprocal(w1[:], w1[:])
                    t12 = pl.step("vector", f_w1r, inc=("dve", 1),
                                  waits=[t11])

                    def f_w2(v):
                        return v.tensor_mul(w2[:], ee2[:], w1[:])
                    t13 = pl.step("vector", f_w2, inc=("dve", 1), waits=[t12])

                    def f_t1(v):
                        return v.tensor_scalar(mk1[:], mk1[:], w1[:], 0.0,
                                               ALU.mult, ALU.add)
                    t14 = pl.step("vector", f_t1, inc=("dve", 1),
                                  waits=[t12, t8])

                    def f_tw(v, tt=tt):
                        return v.scalar_tensor_tensor(
                            twn[:, tt * 8:(tt + 1) * 8], mk2[:], w2[:],
                            mk1[:], ALU.mult, ALU.add)
                    t15 = pl.step("vector", f_tw, inc=("dve", 1),
                                  waits=[t13, t14])
                    tw_toks.append(t15)

                    def f_mk(v, tt=tt):
                        return v.tensor_scalar(mkbuf[:],
                                               twn[:, tt * 8:(tt + 1) * 8],
                                               0.0, 0.0, ALU.is_gt, ALU.add)
                    tmk = pl.step("vector", f_mk, inc=("dve", 1),
                                  waits=later(t15, mk_free))

                    def f_cnt(t):
                        return t.matmul(psC[0:1, 0:8], onescol[:], mkbuf[:],
                                        start=True, stop=True)
                    tcnt = pl.step("tensor", f_cnt, inc=("pe", 1),
                                   waits=later(tmk, psC_free))

                    def f_pos(t):
                        t.matmul(psP[:, 0:8], U128[:], mkbuf[:],
                                 start=True, stop=False)
                        return t.matmul(psP[:, 0:8], ones32[:],
                                        offs[0:1, 0:8],
                                        start=False, stop=True)
                    tpos = pl.step("tensor", f_pos, inc=("pe", 1),
                                   waits=later(tmk, t_offp, psP_free,
                                               t_U, t_o32))

                    def f_ecnt(v):
                        return v.tensor_copy(cntb[0:1, 0:8], psC[0:1, 0:8])
                    tec = pl.step("vector", f_ecnt, inc=("dve", 1),
                                  waits=[tcnt])
                    psC_free = tec

                    def f_off(v):
                        return v.tensor_add(offs[0:1, 0:8], offs[0:1, 0:8],
                                            cntb[0:1, 0:8])
                    t_offp = pl.step("vector", f_off, inc=("dve", 1),
                                     waits=later(tec, tpos))

                    def f_sl1(v):
                        return v.tensor_scalar(sltmp[:], psP[:, 0:8], 1.0,
                                               0.0, ALU.add, ALU.add)
                    ts1 = pl.step("vector", f_sl1, inc=("dve", 1),
                                  waits=[tpos])
                    psP_free = ts1

                    def f_sl2(v):
                        return v.tensor_mul(sltmp[:], sltmp[:], mkbuf[:])
                    ts2 = pl.step("vector", f_sl2, inc=("dve", 1),
                                  waits=[ts1])
                    mk_free = later(ts2, tcnt, tpos)

                    def f_sl3(v, tt=tt):
                        return v.tensor_scalar(
                            slotbuf[:, tt * 8:(tt + 1) * 8], sltmp[:], -1.0,
                            0.0, ALU.add, ALU.add)
                    ts3 = pl.step("vector", f_sl3, inc=("dve", 1),
                                  waits=[ts2])
                    slot_toks.append(ts3)

                if debug:
                    def f_dtw(sync):
                        sync.dma_start(out=dbg["tw"][:, :], in_=twn[:]
                                       ).then_inc(SEM["odma"], 16)
                        return None
                    pl.step("sync", f_dtw, dma_n=1, dma_sem="odma",
                            waits=later(*tw_toks))
                    n_out_dma[0] += 1
                slot_ready = later(*slot_toks, *tw_toks)

            marks.append((len(pl.steps), dict(pl.val)))
            p3_done = later(pl.cur("pe"), pl.cur("dve"), pl.cur("act"))
            esE = ExitStack()
            with esE:
                Pbuf = esE.enter_context(nc.sbuf_tensor("Pbuf", [P, 8 * CAP], BF16))
                Pw = esE.enter_context(nc.sbuf_tensor("Pw", [P, 2 * CAP], BF16))
                Qbuf = esE.enter_context(nc.sbuf_tensor("Qbuf", [P, NSC * TOWN], BF16))
                xc = esE.enter_context(nc.sbuf_tensor("xc", [P, CT * CAP], BF16))
                hc = esE.enter_context(nc.sbuf_tensor("hc", [P, 2 * FRT * CAP], BF16))
                oc = esE.enter_context(nc.sbuf_tensor("oc", [P, 2 * NSC * D], BF16))
                sgl = esE.enter_context(nc.sbuf_tensor("sgl", [P, 2 * CAP], BF16))
                gws = esE.enter_context(nc.sbuf_tensor("gws", [P, 2 * CT * P], BF16))
                uws = esE.enter_context(nc.sbuf_tensor("uws", [P, 2 * CT * P], BF16))
                dws = esE.enter_context(nc.sbuf_tensor("dws", [P, 4 * 512], BF16))
                psA = esE.enter_context(nc.psum_tensor("psA", [P, 1536], F32))
                psG = esE.enter_context(nc.psum_tensor("psG", [P, CAP], F32))
                psS2 = esE.enter_context(nc.psum_tensor("psS2", [P, TOWN], F32))
                psQT = esE.enter_context(nc.psum_tensor("psQT", [P, NSC * P], BF16))

                def pgv(k):
                    return psA[:, 0:CAP] if k == 0 else psA[:, 512:512 + CAP]

                def PDv(sc):
                    return psA[:, sc * 512:(sc + 1) * 512]

                PUv = psA[:, 1024:1024 + CAP]

                def SCv(sp, hh):
                    base = psA if sp == 0 else psS2
                    return base[:, hh * 512:(hh + 1) * 512]

                def SCfull(sp):
                    return (psA if sp == 0 else psS2)[:, 0:TOWN]

                gs_free = [p3_done, p3_done]
                us_free = [p3_done, p3_done]
                dws_free = [p3_done] * 4
                pg_free = [p3_done, p3_done]
                Pbuf_free = p3_done
                Pw_free = [p3_done, p3_done]
                Qbuf_free = p3_done
                psQT_free = p3_done
                xc_free = p3_done
                sgl_free = [p3_done, p3_done]
                hc_free = [p3_done, p3_done]
                oc_free = [p3_done, p3_done]
                SC_free = [p3_done, p3_done]
                psA_clear = p3_done
                psG_free = p3_done
                PU_free = p3_done
                PD_free = p3_done
                yadd = [p3_done] * CT

                for e in range(E):
                    hpar = e % 2
                    opar = e % 2

                    tP_toks = []
                    tQ_toks = []
                    for tt in range(8):
                        def f_P(v, tt=tt, e=e):
                            return v.tensor_scalar(
                                Pbuf[:, tt * CAP:(tt + 1) * CAP],
                                iota_c[:, 0:CAP],
                                slotbuf[:, tt * 8 + e:tt * 8 + e + 1],
                                0.0, ALU.is_equal, ALU.add)
                        tP = pl.step("vector", f_P, inc=("dve", 1),
                                     waits=later(slot_ready, Pbuf_free, t_U))
                        tP_toks.append(tP)

                        wp = tt % 2

                        def f_Pw(v, tt=tt, e=e, wp=wp):
                            return v.tensor_scalar(
                                Pw[:, wp * CAP:(wp + 1) * CAP],
                                Pbuf[:, tt * CAP:(tt + 1) * CAP],
                                twn[:, tt * 8 + e:tt * 8 + e + 1],
                                0.0, ALU.mult, ALU.add)
                        tPw = pl.step("vector", f_Pw, inc=("dve", 1),
                                      waits=later(tP, Pw_free[wp]))

                        def f_tq(t, wp=wp):
                            last = None
                            for sc in range(NSC):
                                last = t.transpose(
                                    psQT[:, sc * P:(sc + 1) * P],
                                    Pw[:, wp * CAP + sc * P:
                                       wp * CAP + (sc + 1) * P],
                                    identbf[:])
                            return last
                        ttq = pl.step("tensor", f_tq, inc=("pe", 1),
                                      waits=later(tPw, psQT_free, t_ident))
                        Pw_free[wp] = ttq

                        def f_eq(v, tt=tt):
                            last = None
                            for sc in range(NSC):
                                last = v.tensor_copy(
                                    Qbuf[:, sc * TOWN + tt * P:
                                         sc * TOWN + (tt + 1) * P],
                                    psQT[:, sc * P:(sc + 1) * P])
                            return last
                        teq = pl.step("vector", f_eq, inc=("dve", 1),
                                      waits=later(ttq, Qbuf_free))
                        psQT_free = teq
                        tQ_toks.append(teq)
                    Q_ready = later(*tQ_toks)
                    P_ready = later(*tP_toks)

                    tg_toks = []
                    ga_toks = []
                    for i in range(CT):
                        gp_par = i % 2

                        def f_ga(t, i=i, gp_par=gp_par):
                            last = None
                            for tt in range(8):
                                last = t.matmul(
                                    pgv(gp_par),
                                    x_tm[:, tt * D + i * P:
                                         tt * D + (i + 1) * P],
                                    Pbuf[:, tt * CAP:(tt + 1) * CAP],
                                    start=(tt == 0), stop=(tt == 7))
                            return last
                        tg = pl.step("tensor", f_ga, inc=("pe", 1),
                                     waits=later(P_ready, xtm_ready,
                                                 pg_free[gp_par], psA_clear))
                        tg_toks.append(tg)

                        def f_eg(v, i=i, gp_par=gp_par):
                            return v.tensor_copy(xc[:, i * CAP:(i + 1) * CAP],
                                                 pgv(gp_par))
                        te = pl.step("vector", f_eg, inc=("dve", 1),
                                     waits=later(tg, xc_free))
                        pg_free[gp_par] = te
                        ga_toks.append(te)
                    Pbuf_free = later(*tg_toks)
                    gather_done = later(*ga_toks)

                    h_toks = []
                    xc_readers = []
                    for fr in range(FRT):
                        ws = fr % 2

                        def f_dg(gp, e=e, fr=fr, ws=ws):
                            gp.dma_start(
                                out=gws[:, ws * CT * P:(ws + 1) * CT * P
                                        ].rearrange("p (c j) -> p c j", j=P),
                                in_=gpT[e, :, fr * P:(fr + 1) * P
                                        ].rearrange("(c p) j -> p c j", p=P)
                            ).then_inc(SEM["wdma"], 16)
                            return None
                        tdg = pl.step("gpsimd", f_dg, dma_n=1, dma_sem="wdma",
                                      waits=later(gs_free[ws]))

                        def f_du(gp, e=e, fr=fr, ws=ws):
                            gp.dma_start(
                                out=uws[:, ws * CT * P:(ws + 1) * CT * P
                                        ].rearrange("p (c j) -> p c j", j=P),
                                in_=upT[e, :, fr * P:(fr + 1) * P
                                        ].rearrange("(c p) j -> p c j", p=P)
                            ).then_inc(SEM["wdma"], 16)
                            return None
                        tdu = pl.step("gpsimd", f_du, dma_n=1, dma_sem="wdma",
                                      waits=later(us_free[ws]))

                        def f_g(t, ws=ws):
                            last = None
                            for c in range(CT):
                                last = t.matmul(
                                    psG[:, 0:CAP],
                                    gws[:, ws * CT * P + c * P:
                                        ws * CT * P + (c + 1) * P],
                                    xc[:, c * CAP:(c + 1) * CAP],
                                    start=(c == 0), stop=(c == CT - 1))
                            return last
                        tg2 = pl.step("tensor", f_g, inc=("pe", 1),
                                      waits=later(tdg, gather_done, psG_free))
                        gs_free[ws] = tg2
                        xc_readers.append(tg2)

                        sp2 = fr % 2

                        def f_silu(a, sp2=sp2):
                            return a.activation(
                                sgl[:, sp2 * CAP:(sp2 + 1) * CAP],
                                psG[:, 0:CAP], AF.Silu)
                        tsl = pl.step("scalar", f_silu, inc=("act", 1),
                                      waits=later(tg2, sgl_free[sp2]))
                        psG_free = tsl

                        def f_u(t, ws=ws):
                            last = None
                            for c in range(CT):
                                last = t.matmul(
                                    PUv,
                                    uws[:, ws * CT * P + c * P:
                                        ws * CT * P + (c + 1) * P],
                                    xc[:, c * CAP:(c + 1) * CAP],
                                    start=(c == 0), stop=(c == CT - 1))
                            return last
                        tu = pl.step("tensor", f_u, inc=("pe", 1),
                                     waits=later(tdu, gather_done, PU_free))
                        us_free[ws] = tu
                        xc_readers.append(tu)

                        def f_h(v, fr=fr, sp2=sp2, hpar=hpar):
                            return v.tensor_mul(
                                hc[:, hpar * FRT * CAP + fr * CAP:
                                   hpar * FRT * CAP + (fr + 1) * CAP],
                                sgl[:, sp2 * CAP:(sp2 + 1) * CAP], PUv)
                        th = pl.step("vector", f_h, inc=("dve", 1),
                                     waits=later(tu, tsl, hc_free[hpar]))
                        PU_free = th
                        sgl_free[sp2] = th
                        h_toks.append(th)
                    xc_free = later(*xc_readers)

                    td_toks = []
                    teo_toks = []
                    for dv in range(4):
                        td = None
                        for fr in range(FRT):
                            dslot = (dv * FRT + fr) % 4

                            def f_dd(gp, e=e, fr=fr, dv=dv, dslot=dslot):
                                gp.dma_start(
                                    out=dws[:, dslot * 512:(dslot + 1) * 512],
                                    in_=dpT[e, fr * P:(fr + 1) * P,
                                            dv * 512:(dv + 1) * 512]
                                ).then_inc(SEM["wdma"], 16)
                                return None
                            tdd = pl.step("gpsimd", f_dd, dma_n=1,
                                          dma_sem="wdma",
                                          waits=later(dws_free[dslot]))

                            def f_d(t, fr=fr, dslot=dslot, hpar=hpar):
                                last = None
                                for sc in range(NSC):
                                    last = t.matmul(
                                        PDv(sc),
                                        hc[:, hpar * FRT * CAP + fr * CAP +
                                           sc * P:hpar * FRT * CAP +
                                           fr * CAP + (sc + 1) * P],
                                        dws[:, dslot * 512:(dslot + 1) * 512],
                                        start=(fr == 0), stop=(fr == FRT - 1))
                                return last
                            td = pl.step("tensor", f_d, inc=("pe", 1),
                                         waits=later(tdd, *h_toks,
                                                     PD_free if fr == 0
                                                     else None))
                            dws_free[dslot] = td
                            td_toks.append(td)

                        def f_eo(v, dv=dv, opar=opar):
                            last = None
                            for sc in range(NSC):
                                last = v.tensor_copy(
                                    oc[:, opar * NSC * D + sc * D +
                                       dv * 512:opar * NSC * D + sc * D +
                                       (dv + 1) * 512],
                                    PDv(sc))
                            return last
                        teo = pl.step("vector", f_eo, inc=("dve", 1),
                                      waits=later(td, oc_free[opar]))
                        PD_free = teo
                        teo_toks.append(teo)
                    hc_free[hpar] = later(*td_toks)
                    oc_ready = later(*teo_toks)
                    PU_free = later(PU_free, teo_toks[-1])

                    tsc_toks = []
                    psA_adds = []
                    for i in range(CT):
                        sp = i % 2

                        def f_s2(t, i=i, sp=sp, opar=opar):
                            last = None
                            for sc in range(NSC):
                                for hh in range(2):
                                    last = t.matmul(
                                        SCv(sp, hh),
                                        oc[:, opar * NSC * D + sc * D +
                                           i * P:opar * NSC * D + sc * D +
                                           i * P + P],
                                        Qbuf[:, sc * TOWN + hh * 512:
                                             sc * TOWN + (hh + 1) * 512],
                                        start=(sc == 0), stop=(sc == NSC - 1))
                            return last
                        tsc = pl.step("tensor", f_s2, inc=("pe", 1),
                                      waits=later(oc_ready, Q_ready,
                                                  SC_free[sp]))
                        tsc_toks.append(tsc)

                        def f_acc(v, i=i, sp=sp):
                            return v.tensor_add(yTv(i), yTv(i), SCfull(sp))
                        ta = pl.step("vector", f_acc, inc=("dve", 1),
                                     waits=later(tsc, yadd[i]))
                        SC_free[sp] = ta
                        yadd[i] = ta
                        if sp == 0:
                            psA_adds.append(ta)
                    psA_clear = psA_adds[-1]
                    Qbuf_free = later(*tsc_toks)
                    oc_free[opar] = later(*tsc_toks)

            marks.append((len(pl.steps), dict(pl.val)))
            if upto < 5:
                cut, snap = marks[upto]
                pl.steps = pl.steps[:cut]
                pl.val = dict(snap)
                n_out_dma[0] = snap.get("odma", 0) // 16
                yadd = [(s, v) for s, v in snap.items()]

            def f_out(sync):
                sync.dma_start(
                    out=out[:, :].rearrange("(r p) n -> p r n", p=P),
                    in_=yT[:].rearrange("p (r n) -> p r n", n=TOWN)
                ).then_inc(SEM["odma"], 16)
                return None
            pl.step("sync", f_out, dma_n=1, dma_sem="odma",
                    waits=later(*yadd))
            n_out_dma[0] += 1

            def f_fin(sync):
                return sync.nop()
            pl.step("sync", f_fin, waits=[("odma", 16 * n_out_dma[0])])

            for name in ("xdma", "wdma", "odma", "pe", "act", "dve", "gp"):
                SEM[name] = es.enter_context(nc.semaphore(name))

            block = es.enter_context(nc.Block())

            def emit_engine(eng_name, eng):
                emitted = {}
                for st in pl.steps:
                    if st["eng"] != eng_name:
                        continue
                    for s, v in st["waits"]:
                        if emitted.get(s, 0) < v:
                            eng.wait_ge(SEM[s], v)
                            emitted[s] = v
                    inst = st["fn"](eng)
                    if st["inc"] is not None:
                        s, a = st["inc"]
                        assert inst is not None
                        inst.then_inc(SEM[s], a)

            @block.sync
            def _(sync):
                emit_engine("sync", sync)

            @block.gpsimd
            def _(gp):
                emit_engine("gpsimd", gp)

            @block.tensor
            def _(t):
                emit_engine("tensor", t)

            @block.vector
            def _(v):
                emit_engine("vector", v)

            @block.scalar
            def _(a):
                emit_engine("scalar", a)

    return nc, dbg


def make_inputs(inp):
    x = np.asarray(inp["L0_attn_x"], np.float32)
    wq = np.asarray(inp["L0_attn_Wq"], np.float32)
    wk = np.asarray(inp["L0_attn_Wk"], np.float32)
    wv = np.asarray(inp["L0_attn_Wv"], np.float32)
    memory = np.ascontiguousarray(np.asarray(inp["L0_attn_memory"], np.float32))
    memnorm = np.ascontiguousarray(np.asarray(inp["L0_attn_memory_norm"], np.float32))
    memw = np.asarray(inp["L0_attn_memory_weight"], np.float32).reshape(1, 1)
    gw = np.asarray(inp["L0_ffn_gate_w"], np.float32)
    gp = np.asarray(inp["L0_ffn_gate_proj"], np.float32)
    up = np.asarray(inp["L0_ffn_up_proj"], np.float32)
    dp = np.asarray(inp["L0_ffn_down_proj"], np.float32)

    shared = {
        "wqT": np.ascontiguousarray(wq.T),
        "wkT": np.ascontiguousarray(wk.T),
        "wvT": np.ascontiguousarray(wv.T),
        "mem": memory,
        "memnorm": np.ascontiguousarray(memnorm.reshape(4, P).T),
        "memw": np.ascontiguousarray(memw),
        "gwT": np.ascontiguousarray(gw.T),
        "gpT": np.ascontiguousarray(gp.transpose(0, 2, 1)),
        "upT": np.ascontiguousarray(up.transpose(0, 2, 1)),
        "dpT": np.ascontiguousarray(dp.transpose(0, 2, 1)),
    }
    in_maps = []
    for c in range(8):
        beta, r0 = c // 4, (c % 4) * TOWN
        m = dict(shared)
        m["x"] = np.ascontiguousarray(x[beta])
        m["xq"] = np.ascontiguousarray(x[beta, r0:r0 + TOWN])
        in_maps.append(m)
    return in_maps


def assemble(results):
    outp = np.empty((2, SEQ, D), np.float32)
    for c in range(8):
        beta, r0 = c // 4, (c % 4) * TOWN
        outp[beta, r0:r0 + TOWN, :] = np.asarray(results[c]["out"]).T
    return outp


_NC_CACHE = {}


def kernel(**inputs):
    from concourse.bass_utils import run_bass_kernel_spmd
    in_maps = make_inputs(inputs)
    if "nc" not in _NC_CACHE:
        nc, _ = build(debug=False)
        _NC_CACHE["nc"] = nc
    res = run_bass_kernel_spmd(_NC_CACHE["nc"], in_maps, core_ids=list(range(8)))
    return assemble(res.results)


# version 08
# speedup vs baseline: 1.2345x; 1.0848x over previous
import os
import numpy as np
import concourse.bass as bass
import concourse.mybir as mybir
from contextlib import ExitStack
from types import SimpleNamespace

F32 = mybir.dt.float32
BF16 = mybir.dt.bfloat16
AF = mybir.ActivationFunctionType
ALU = mybir.AluOpType
AX = mybir.AxisListType.X

P = 128
D = 2048
SEQ = 4096
TOWN = 1024
CT = D // P
DKV = 512
NQ, NH, DH = 4, 8, 64
E, FD, FRT = 8, 1408, 11
CAP = 384
NSC = CAP // P
EPS = 1e-5
NKT = SEQ // P
NCH = SEQ // 512
SCALE = 0.125

ENGS = ("sync", "gpsimd", "tensor", "vector", "scalar")


class Plan:
    def __init__(self):
        self.steps = []
        self.val = {}

    def cur(self, sem):
        v = self.val.get(sem, 0)
        return (sem, v) if v else None

    def step(self, eng, fn, inc=None, waits=(), dma_n=0, dma_sem=None):
        w = []
        for t in waits:
            if t is None:
                continue
            if isinstance(t, list):
                w.extend(xx for xx in t if xx is not None)
            else:
                w.append(t)
        self.steps.append(dict(eng=eng, fn=fn, inc=inc, waits=w))
        if inc is not None:
            s, a = inc
            self.val[s] = self.val.get(s, 0) + a
            return (s, self.val[s])
        if dma_n:
            self.val[dma_sem] = self.val.get(dma_sem, 0) + 16 * dma_n
            return (dma_sem, self.val[dma_sem])
        return None


def later(*toks):
    best = {}
    for t in toks:
        if t is None:
            continue
        if isinstance(t, list):
            for s, v in t:
                best[s] = max(best.get(s, 0), v)
        else:
            s, v = t
            best[s] = max(best.get(s, 0), v)
    return [(s, v) for s, v in best.items()]


def build(debug=False, upto=4):
    nc = bass.Bass("TRN2", target_bir_lowering=False)

    x = nc.dram_tensor("x", [SEQ, D], F32, kind="ExternalInput")
    xq = nc.dram_tensor("xq", [TOWN, D], F32, kind="ExternalInput")
    wqT = nc.dram_tensor("wqT", [D, D], F32, kind="ExternalInput")
    wkT = nc.dram_tensor("wkT", [D, DKV], F32, kind="ExternalInput")
    wvT = nc.dram_tensor("wvT", [D, DKV], F32, kind="ExternalInput")
    mem = nc.dram_tensor("mem", [DKV, DKV], F32, kind="ExternalInput")
    memnorm = nc.dram_tensor("memnorm", [P, 4], F32, kind="ExternalInput")
    memw = nc.dram_tensor("memw", [1, 1], F32, kind="ExternalInput")
    gwT = nc.dram_tensor("gwT", [D, E], F32, kind="ExternalInput")
    gst = nc.dram_tensor("gst", [E * FRT, P, CT * P], F32, kind="ExternalInput")
    ust = nc.dram_tensor("ust", [E * FRT, P, CT * P], F32, kind="ExternalInput")
    dst = nc.dram_tensor("dst", [E * FRT * 4, P, 512], F32, kind="ExternalInput")
    gstage = nc.dram_tensor("gstage", [E * FRT, P, CT * P], BF16, kind="Internal")
    ustage = nc.dram_tensor("ustage", [E * FRT, P, CT * P], BF16, kind="Internal")
    dstage = nc.dram_tensor("dstage", [E * FRT * 4, P, 512], BF16, kind="Internal")
    out = nc.dram_tensor("out", [D, TOWN], F32, kind="ExternalOutput")

    dbg = {}
    if debug:
        dbg["kT"] = nc.dram_tensor("d_kT", [P, 4 * SEQ], BF16, kind="ExternalOutput")
        dbg["vb"] = nc.dram_tensor("d_vb", [P, NKT * 520], BF16, kind="ExternalOutput")
        dbg["qT"] = nc.dram_tensor("d_qT", [P, CT * TOWN], BF16, kind="ExternalOutput")
        dbg["y1"] = nc.dram_tensor("d_y1", [P, CT * TOWN], F32, kind="ExternalOutput")
        dbg["y2"] = nc.dram_tensor("d_y2", [P, CT * TOWN], F32, kind="ExternalOutput")
        dbg["ynT"] = nc.dram_tensor("d_ynT", [P, CT * TOWN], BF16, kind="ExternalOutput")
        dbg["tw"] = nc.dram_tensor("d_tw", [P, 64], F32, kind="ExternalOutput")

    pl = Plan()
    SEM = {}
    n_out_dma = [0]
    marks = []
    stg_tok = [[None] * FRT for _ in range(E)]

    es = ExitStack()
    with es:
        yT = es.enter_context(nc.sbuf_tensor("yT", [P, CT * TOWN], F32))
        identbf = es.enter_context(nc.sbuf_tensor("identbf", [P, P], BF16))
        ident32 = es.enter_context(nc.sbuf_tensor("ident32", [P, P], F32))
        onesbf = es.enter_context(nc.sbuf_tensor("onesbf", [1, P], BF16))
        onescol = es.enter_context(nc.sbuf_tensor("onescol", [P, 1], F32))
        epsc = es.enter_context(nc.sbuf_tensor("epsc", [P, 1], F32))
        wmix = es.enter_context(nc.sbuf_tensor("wmix", [1, 4], F32))
        m2s = es.enter_context(nc.sbuf_tensor("m2s", [P, 8], F32))
        r2s = es.enter_context(nc.sbuf_tensor("r2s", [P, 8], F32))

        def yTv(rt):
            return yT[:, rt * TOWN:(rt + 1) * TOWN]

        BC = SimpleNamespace(bhi=None, blo=None, blt=None, free=None)

        def emit_bcast(src_ap, n, ps_ap, wait_src):
            bhi, blo, blt = BC.bhi, BC.blo, BC.blt

            def f_hi(v):
                return v.tensor_copy(bhi[:, 0:n], src_ap)
            th = pl.step("vector", f_hi, inc=("dve", 1),
                         waits=later(wait_src, BC.free))

            def f_lo(v):
                return v.scalar_tensor_tensor(blt[:, 0:n], bhi[:, 0:n], -1.0,
                                              src_ap, ALU.mult, ALU.add)
            tl = pl.step("vector", f_lo, inc=("dve", 1), waits=[th])

            def f_lo2(v):
                return v.tensor_copy(blo[:, 0:n], blt[:, 0:n])
            tl2 = pl.step("vector", f_lo2, inc=("dve", 1), waits=[tl])

            def f_mm(t):
                last = None
                for j in range(0, n, 512):
                    j1 = min(n, j + 512)
                    t.matmul(ps_ap[:, j:j1], onesbf[:], bhi[:, j:j1],
                             start=True, stop=False)
                    last = t.matmul(ps_ap[:, j:j1], onesbf[:], blo[:, j:j1],
                                    start=False, stop=True)
                return last
            tmm = pl.step("tensor", f_mm, inc=("pe", 1), waits=later(th, tl2))
            BC.free = tmm
            return tmm

        def emit_ln_stats(x_ap, st6, sub, mean_ap, var_ap, rstd_ap, nb_ap,
                          waits):
            def f_stats(v):
                last = None
                for j in range(4):
                    last = v.bn_stats(st6[:, j * 6:(j + 1) * 6],
                                      x_ap[:, j * 512:(j + 1) * 512])
                return last
            t1 = pl.step("vector", f_stats, inc=("dve", 1), waits=waits)
            st6v = st6[:].rearrange("p (a b) -> p a b", b=3)
            means = st6v[:, :, 1:2].rearrange("p a one -> p (a one)")
            cvars = st6v[:, :, 2:3].rearrange("p a one -> p (a one)")

            def f_red(v):
                v.reduce_sum(mean_ap, means, axis=AX)
                return v.reduce_sum(var_ap, cvars, axis=AX)
            t2 = pl.step("vector", f_red, inc=("dve", 1), waits=[t1])

            def f_cp(v):
                return v.tensor_copy(sub[:, 8:16], means)
            t2b = pl.step("vector", f_cp, inc=("dve", 1), waits=[t1])

            def f_mn(v):
                return v.tensor_scalar(mean_ap, mean_ap, 0.125, 0.0,
                                       ALU.mult, ALU.add)
            t3 = pl.step("vector", f_mn, inc=("dve", 1), waits=[t2])

            def f_dev(v):
                return v.tensor_scalar(sub[:, 0:8], sub[:, 8:16], mean_ap, 0.0,
                                       ALU.subtract, ALU.add)
            t4 = pl.step("vector", f_dev, inc=("dve", 1), waits=[t3, t2b])

            def f_sq(v):
                return v.tensor_mul(sub[:, 0:8], sub[:, 0:8], sub[:, 0:8])
            t5 = pl.step("vector", f_sq, inc=("dve", 1), waits=[t4])

            def f_vs(v):
                return v.reduce_sum(sub[:, 16:17], sub[:, 0:8], axis=AX)
            t6 = pl.step("vector", f_vs, inc=("dve", 1), waits=[t5])

            def f_v1(v):
                return v.tensor_scalar(var_ap, var_ap, 1.0 / 2048.0, 0.0,
                                       ALU.mult, ALU.add)
            t7 = pl.step("vector", f_v1, inc=("dve", 1), waits=[t2])

            def f_v2(v):
                return v.scalar_tensor_tensor(var_ap, sub[:, 16:17], 0.125,
                                              var_ap, ALU.mult, ALU.add)
            t8 = pl.step("vector", f_v2, inc=("dve", 1), waits=[t6, t7])

            def f_rstd(a):
                a.activation(rstd_ap, var_ap, AF.Ln, bias=epsc[:])
                return a.activation(rstd_ap, rstd_ap, AF.Exp, scale=-0.5)
            t9 = pl.step("scalar", f_rstd, inc=("act", 1), waits=[t8])

            def f_nb(v):
                return v.scalar_tensor_tensor(nb_ap, mean_ap, -1.0, rstd_ap,
                                              ALU.mult, ALU.mult)
            t10 = pl.step("vector", f_nb, inc=("dve", 1), waits=[t9])
            return t9, t10

        esM = ExitStack()
        last_evac = {"kT": None, "vb": None, "qT": None, "yT": None}
        CX = SimpleNamespace(qTv=None)
        with esM:
            kT = esM.enter_context(nc.sbuf_tensor("kT", [P, 4 * SEQ], BF16))
            vb = esM.enter_context(nc.sbuf_tensor("vb", [P, NKT * 520], BF16))

            def kTv(r):
                return kT[:, r * SEQ:(r + 1) * SEQ]

            def f_ident(gp):
                gp.memset(identbf[:], 0.0)
                gp.affine_select(out=identbf[:], in_=identbf[:],
                                 compare_op=ALU.not_equal, fill=1.0, base=0,
                                 pattern=[[-1, P]], channel_multiplier=1)
                gp.memset(ident32[:], 0.0)
                return gp.affine_select(out=ident32[:], in_=ident32[:],
                                        compare_op=ALU.not_equal, fill=1.0,
                                        base=0, pattern=[[-1, P]],
                                        channel_multiplier=1)
            t_ident = pl.step("gpsimd", f_ident, inc=("gp", 1))

            def f_const(v):
                v.memset(onesbf[:], 1.0)
                v.memset(onescol[:], 1.0)
                v.memset(epsc[:], EPS)
                return v.memset(vb[:], 1.0)
            t_const = pl.step("vector", f_const, inc=("dve", 1))

            def f_mw(sync):
                sync.dma_start(out=wmix[0:1, 0:1], in_=memw[:, :]
                               ).then_inc(SEM["xdma"], 16)
                return None
            t_mw = pl.step("sync", f_mw, dma_n=1, dma_sem="xdma")

            def f_s1(a):
                return a.activation(wmix[0:1, 1:2], wmix[0:1, 0:1], AF.Exp,
                                    scale=-1.0)
            t_s1 = pl.step("scalar", f_s1, inc=("act", 1), waits=[t_mw])

            def f_s2(v):
                return v.tensor_scalar(wmix[0:1, 2:3], wmix[0:1, 1:2], 1.0, 0.0,
                                       ALU.add, ALU.add)
            t_s2 = pl.step("vector", f_s2, inc=("dve", 1), waits=[t_s1])

            def f_s3(v):
                return v.reciprocal(wmix[0:1, 0:1], wmix[0:1, 2:3])
            t_s3 = pl.step("vector", f_s3, inc=("dve", 1), waits=[t_s2])

            def f_s4(v):
                return v.tensor_mul(wmix[0:1, 1:2], wmix[0:1, 1:2],
                                    wmix[0:1, 0:1])
            t_s4 = pl.step("vector", f_s4, inc=("dve", 1), waits=[t_s3])

            def emit_p0_pass(B, R, srcten, n_chunks, own, t_w):
                for ch in range(n_chunks):
                    cp = ch % B.xnTn
                    ev_toks = []
                    for i in range(4):
                        tt = ch * 4 + i
                        tp = tt % B.xtn
                        xtap = B.xt[:, tp * D:(tp + 1) * D]

                        def f_x(sync, tt=tt, tp=tp, B=B, srcten=srcten):
                            sync.dma_start(out=B.xt[:, tp * D:(tp + 1) * D],
                                           in_=srcten[tt * P:(tt + 1) * P, :]
                                           ).then_inc(SEM["xdma"], 16)
                            return None
                        tx = pl.step("sync", f_x, dma_n=1, dma_sem="xdma",
                                     waits=later(R.xt_free[tp]))
                        sp_ = tt % 2
                        t_rs, t_nb = emit_ln_stats(
                            xtap, B.st6[:, sp_ * 24:(sp_ + 1) * 24],
                            B.sub[:, sp_ * 17:(sp_ + 1) * 17],
                            B.mn0[:, sp_:sp_ + 1], B.vr0[:, sp_:sp_ + 1],
                            B.rs0[:, sp_:sp_ + 1], B.nb0[:, sp_:sp_ + 1],
                            later(tx, R.st_free[sp_]))
                        R.st_free[sp_] = None

                        def f_aff(a, xtap=xtap, B=B, sp_=sp_):
                            return a.activation(B.xn[:], xtap, AF.Identity,
                                                bias=B.nb0[:, sp_:sp_ + 1],
                                                scale=B.rs0[:, sp_:sp_ + 1])
                        t_xn = pl.step("scalar", f_aff, inc=("act", 1),
                                       waits=later(t_rs, t_nb, R.xn_free[0]))
                        R.st_free[sp_] = t_xn

                        xt_consumers = [t_xn, t_rs]
                        if own:
                            for g in range(4):
                                gb = g % 2

                                def f_trx(t, g=g, gb=gb, xtap=xtap, B=B):
                                    last = None
                                    for idx in range(4):
                                        c = g * 4 + idx
                                        last = t.transpose(
                                            B.psX[:, gb * 512 + idx * 128:
                                                  gb * 512 + (idx + 1) * 128],
                                            xtap[:, c * P:(c + 1) * P],
                                            ident32[:])
                                    return last
                                ttr = pl.step("tensor", f_trx, inc=("pe", 1),
                                              waits=later(tx, R.psX_free[gb],
                                                          t_ident))

                                def f_evx(v, g=g, gb=gb, tt=tt, B=B):
                                    last = None
                                    for idx in range(4):
                                        c = g * 4 + idx
                                        last = v.tensor_copy(
                                            yTv(c)[:, tt * P:(tt + 1) * P],
                                            B.psX[:, gb * 512 + idx * 128:
                                                  gb * 512 + (idx + 1) * 128])
                                    return last
                                tev = pl.step("vector", f_evx, inc=("dve", 1),
                                              waits=[ttr])
                                R.psX_free[gb] = tev
                                xt_consumers.append(ttr)
                                last_evac["yT"] = tev
                        R.xt_free[tp] = later(*xt_consumers)

                        tr_toks = []
                        for g in range(2):
                            def f_trn(t, g=g, B=B):
                                last = None
                                for idx in range(8):
                                    c = g * 8 + idx
                                    last = t.transpose(
                                        B.psTR[:, g * 1024 + idx * P:
                                               g * 1024 + (idx + 1) * P],
                                        B.xn[:, c * P:(c + 1) * P], identbf[:])
                                return last
                            ttr = pl.step("tensor", f_trn, inc=("pe", 1),
                                          waits=later(t_xn, R.psTR_free[g],
                                                      t_ident))

                            def f_evn(eng, g=g, cp=cp, i=i, B=B):
                                cp_fn = (eng.copy if g == 0
                                         else eng.tensor_copy)
                                last = None
                                for idx in range(8):
                                    c = g * 8 + idx
                                    last = cp_fn(
                                        B.xnT[:, cp * CT * 512 + c * 512 +
                                              i * P:cp * CT * 512 + c * 512 +
                                              (i + 1) * P],
                                        B.psTR[:, g * 1024 + idx * P:
                                               g * 1024 + (idx + 1) * P])
                                return last
                            ev_eng = "scalar" if g == 0 else "vector"
                            tev = pl.step(ev_eng, f_evn,
                                          inc=("act" if g == 0 else "dve", 1),
                                          waits=later(ttr, R.xnT_free[cp]))
                            R.psTR_free[g] = tev
                            tr_toks.append(ttr)
                            ev_toks.append(tev)
                        R.xn_free[0] = later(*tr_toks)

                    if not own:
                        for r in range(4):
                            kp = r % 2

                            def f_k(t, r=r, cp=cp, kp=kp, B=B):
                                last = None
                                for c in range(CT):
                                    last = t.matmul(
                                        B.psKQ[:, kp * 512:(kp + 1) * 512],
                                        B.wk_sb[:, c * DKV + r * P:
                                                c * DKV + (r + 1) * P],
                                        B.xnT[:, cp * CT * 512 + c * 512:
                                              cp * CT * 512 + (c + 1) * 512],
                                        start=(c == 0), stop=(c == CT - 1))
                                return last
                            tk = pl.step("tensor", f_k, inc=("pe", 1),
                                         waits=later(*ev_toks, R.psKQ_free[kp],
                                                     t_w))

                            def f_ek(v, r=r, ch=ch, kp=kp, B=B):
                                return v.tensor_copy(
                                    kTv(r)[:, ch * 512:(ch + 1) * 512],
                                    B.psKQ[:, kp * 512:(kp + 1) * 512])
                            te = pl.step("vector", f_ek, inc=("dve", 1),
                                         waits=[tk])
                            R.psKQ_free[kp] = te
                            last_evac["kT"] = te
                        for i in range(4):
                            vp = i % 2
                            kt = ch * 4 + i

                            def f_v(t, i=i, cp=cp, vp=vp, B=B):
                                last = None
                                for c in range(CT):
                                    last = t.matmul(
                                        B.psV[:, vp * 512:(vp + 1) * 512],
                                        B.xnT[:, cp * CT * 512 + c * 512 +
                                              i * P:cp * CT * 512 + c * 512 +
                                              (i + 1) * P],
                                        B.wv_sb[:, c * DKV:(c + 1) * DKV],
                                        start=(c == 0), stop=(c == CT - 1))
                                return last
                            tv = pl.step("tensor", f_v, inc=("pe", 1),
                                         waits=later(*ev_toks, R.psV_free[vp],
                                                     t_w))

                            def f_ev(v, kt=kt, vp=vp, B=B):
                                dst = vb[:, kt * 520:(kt + 1) * 520
                                         ].rearrange("p (h d) -> p h d",
                                                     d=65)[:, :, 0:64]
                                srcv = B.psV[:, vp * 512:(vp + 1) * 512
                                             ].rearrange("p (h d) -> p h d",
                                                         d=64)
                                return v.tensor_copy(dst, srcv)
                            te = pl.step("vector", f_ev, inc=("dve", 1),
                                         waits=later(tv, t_const))
                            R.psV_free[vp] = te
                            last_evac["vb"] = te
                    else:
                        nslot = len(R.wqs_free)
                        for rt in range(CT):
                            kp = rt % 2
                            ws = rt % nslot

                            def f_wq(gp, rt=rt, ws=ws, B=B):
                                gp.dma_start(
                                    out=B.wqs[:, ws * CT * P:(ws + 1) * CT * P
                                              ].rearrange("p (c j) -> p c j",
                                                          j=P),
                                    in_=wqT[:, rt * P:(rt + 1) * P
                                            ].rearrange("(c p) j -> p c j",
                                                        p=P)
                                ).then_inc(SEM["wdma"], 16)
                                return None
                            twq = pl.step("gpsimd", f_wq, dma_n=1,
                                          dma_sem="wdma",
                                          waits=later(R.wqs_free[ws]))

                            def f_q(t, cp=cp, kp=kp, ws=ws, B=B):
                                last = None
                                for c in range(CT):
                                    last = t.matmul(
                                        B.psKQ[:, kp * 512:(kp + 1) * 512],
                                        B.wqs[:, ws * CT * P + c * P:
                                              ws * CT * P + (c + 1) * P],
                                        B.xnT[:, cp * CT * 512 + c * 512:
                                              cp * CT * 512 + (c + 1) * 512],
                                        start=(c == 0), stop=(c == CT - 1))
                                return last
                            tq = pl.step("tensor", f_q, inc=("pe", 1),
                                         waits=later(*ev_toks, R.psKQ_free[kp],
                                                     twq))
                            R.wqs_free[ws] = tq

                            def f_eq(v, rt=rt, ch=ch, kp=kp, B=B):
                                return v.tensor_copy(
                                    CX.qTv(rt)[:, ch * 512:(ch + 1) * 512],
                                    B.psKQ[:, kp * 512:(kp + 1) * 512])
                            te = pl.step("vector", f_eq, inc=("dve", 1),
                                         waits=[tq])
                            R.psKQ_free[kp] = te
                            last_evac["qT"] = te
                    R.xnT_free[cp] = ("pe", pl.val["pe"])

            esA = ExitStack()
            with esA:
                B1 = SimpleNamespace()
                B1.wk_sb = esA.enter_context(nc.sbuf_tensor("wk_sb", [P, CT * DKV], BF16))
                B1.wv_sb = esA.enter_context(nc.sbuf_tensor("wv_sb", [P, CT * DKV], BF16))
                B1.xt = esA.enter_context(nc.sbuf_tensor("xt", [P, 2 * D], F32))
                B1.xn = esA.enter_context(nc.sbuf_tensor("xn", [P, D], BF16))
                B1.xnT = esA.enter_context(nc.sbuf_tensor("xnT", [P, CT * 512], BF16))
                B1.xtn, B1.xnTn = 2, 1
                B1.st6 = esA.enter_context(nc.sbuf_tensor("st6", [P, 48], F32))
                B1.sub = esA.enter_context(nc.sbuf_tensor("sub", [P, 34], F32))
                B1.mn0 = esA.enter_context(nc.sbuf_tensor("mn0", [P, 2], F32))
                B1.vr0 = esA.enter_context(nc.sbuf_tensor("vr0", [P, 2], F32))
                B1.rs0 = esA.enter_context(nc.sbuf_tensor("rs0", [P, 2], F32))
                B1.nb0 = esA.enter_context(nc.sbuf_tensor("nb0", [P, 2], F32))
                B1.wqs = None
                B1.psTR = esA.enter_context(nc.psum_tensor("psTR", [P, 2 * 1024], BF16))
                B1.psKQ = esA.enter_context(nc.psum_tensor("psKQ", [P, 2 * 512], F32))
                B1.psV = esA.enter_context(nc.psum_tensor("psV", [P, 2 * 512], F32))
                B1.psX = esA.enter_context(nc.psum_tensor("psX", [P, 2 * 512], F32))

                def f_wkv(gp, B=B1):
                    gp.dma_start(out=B.wk_sb[:].rearrange("p (c j) -> p c j",
                                                          j=DKV),
                                 in_=wkT[:, :].rearrange("(c p) j -> p c j",
                                                         p=P)
                                 ).then_inc(SEM["wdma"], 16)
                    gp.dma_start(out=B.wv_sb[:].rearrange("p (c j) -> p c j",
                                                          j=DKV),
                                 in_=wvT[:, :].rearrange("(c p) j -> p c j",
                                                         p=P)
                                 ).then_inc(SEM["wdma"], 16)
                    return None
                t_wkv = pl.step("gpsimd", f_wkv, dma_n=2, dma_sem="wdma",
                                waits=[t_ident])

                R1 = SimpleNamespace(
                    st_free=[None, None],
                    xt_free=[None, None], xn_free=[None],
                    xnT_free=[None], psTR_free=[None, None],
                    psKQ_free=[None, None], psV_free=[None, None],
                    psX_free=[None, None], wqs_free=[None])
                emit_p0_pass(B1, R1, x, NCH, False, t_wkv)

            qT = esM.enter_context(nc.sbuf_tensor("qT", [P, CT * TOWN], BF16))

            def qTv(r):
                return qT[:, r * TOWN:(r + 1) * TOWN]
            CX.qTv = qTv

            kv_barrier = later(pl.cur("pe"), pl.cur("dve"), pl.cur("act"))
            esA2 = ExitStack()
            with esA2:
                B2 = SimpleNamespace()
                B2.xt = esA2.enter_context(nc.sbuf_tensor("xt2", [P, D], F32))
                B2.xn = esA2.enter_context(nc.sbuf_tensor("xn2", [P, D], BF16))
                B2.xnT = esA2.enter_context(nc.sbuf_tensor("xnT2", [P, CT * 512], BF16))
                B2.xtn, B2.xnTn = 1, 1
                B2.st6 = esA2.enter_context(nc.sbuf_tensor("st62", [P, 48], F32))
                B2.sub = esA2.enter_context(nc.sbuf_tensor("sub2", [P, 34], F32))
                B2.mn0 = esA2.enter_context(nc.sbuf_tensor("mn02", [P, 2], F32))
                B2.vr0 = esA2.enter_context(nc.sbuf_tensor("vr02", [P, 2], F32))
                B2.rs0 = esA2.enter_context(nc.sbuf_tensor("rs02", [P, 2], F32))
                B2.nb0 = esA2.enter_context(nc.sbuf_tensor("nb02", [P, 2], F32))
                B2.wqs = esA2.enter_context(nc.sbuf_tensor("wqs", [P, 2 * CT * P], BF16))
                B2.wk_sb = None
                B2.wv_sb = None
                B2.psTR = esA2.enter_context(nc.psum_tensor("psTR2", [P, 2 * 1024], BF16))
                B2.psKQ = esA2.enter_context(nc.psum_tensor("psKQ2", [P, 2 * 512], F32))
                B2.psV = None
                B2.psX = esA2.enter_context(nc.psum_tensor("psX2", [P, 2 * 512], F32))

                R2 = SimpleNamespace(
                    st_free=[kv_barrier, kv_barrier],
                    xt_free=[kv_barrier], xn_free=[kv_barrier],
                    xnT_free=[kv_barrier],
                    psTR_free=[kv_barrier, kv_barrier],
                    psKQ_free=[kv_barrier, kv_barrier],
                    psV_free=[None, None],
                    psX_free=[kv_barrier, kv_barrier],
                    wqs_free=[kv_barrier, kv_barrier])
                emit_p0_pass(B2, R2, xq, 2, True, None)

            dbg_gate = None
            if debug:
                def f_d0(sync):
                    sync.dma_start(out=dbg["kT"][:, :], in_=kT[:]
                                   ).then_inc(SEM["odma"], 16)
                    sync.dma_start(out=dbg["vb"][:, :], in_=vb[:]
                                   ).then_inc(SEM["odma"], 16)
                    sync.dma_start(out=dbg["qT"][:, :], in_=qT[:]
                                   ).then_inc(SEM["odma"], 16)
                    sync.dma_start(out=dbg["y1"][:, :], in_=yT[:]
                                   ).then_inc(SEM["odma"], 16)
                    return None
                dbg_gate = pl.step("sync", f_d0, dma_n=4, dma_sem="odma",
                                   waits=later(pl.cur("dve")))
                n_out_dma[0] += 4

            marks.append((len(pl.steps), dict(pl.val)))
            esB = ExitStack()
            with esB:
                mem_sb = esB.enter_context(nc.sbuf_tensor("mem_sb", [P, 4 * DKV], BF16))
                mn_sb = esB.enter_context(nc.sbuf_tensor("mn_sb", [P, 4], BF16))
                sig = esB.enter_context(nc.sbuf_tensor("sig", [P, 4 * TOWN], BF16))
                tmin = esB.enter_context(nc.sbuf_tensor("tmin", [P, TOWN], F32))
                e32 = esB.enter_context(nc.sbuf_tensor("e32", [P, TOWN], F32))
                tmax = esB.enter_context(nc.sbuf_tensor("tmax", [P, TOWN], BF16))
                dn = esB.enter_context(nc.sbuf_tensor("dn", [1, TOWN], F32))
                rw = esB.enter_context(nc.sbuf_tensor("rw", [1, TOWN], F32))
                rwb = esB.enter_context(nc.sbuf_tensor("rwb", [P, TOWN], F32))
                tmpa = esB.enter_context(nc.sbuf_tensor("tmpa", [P, TOWN], F32))
                BC.bhi = esB.enter_context(nc.sbuf_tensor("bhi1", [1, TOWN], BF16))
                BC.blo = esB.enter_context(nc.sbuf_tensor("blo1", [1, TOWN], BF16))
                BC.blt = esB.enter_context(nc.sbuf_tensor("blt1", [1, TOWN], F32))
                BC.free = None
                psM = esB.enter_context(nc.psum_tensor("psM", [P, 2 * TOWN], F32))
                psDen = esB.enter_context(nc.psum_tensor("psDen", [1, TOWN], F32))
                psB2 = esB.enter_context(nc.psum_tensor("psB2", [P, TOWN], F32))

                def f_mload(gp):
                    gp.dma_start(out=mem_sb[:].rearrange("p (k j) -> p k j",
                                                         j=DKV),
                                 in_=mem[:, :].rearrange("(k p) j -> p k j",
                                                         p=P)
                                 ).then_inc(SEM["wdma"], 16)
                    gp.dma_start(out=mn_sb[:], in_=memnorm[:, :]
                                 ).then_inc(SEM["wdma"], 16)
                    return None
                t_mem = pl.step("gpsimd", f_mload, dma_n=2, dma_sem="wdma",
                                waits=later(kv_barrier))

                for e_ in range(E):
                    for fr_ in range(FRT):
                        def f_stg(gp, e_=e_, fr_=fr_):
                            b = e_ * FRT + fr_
                            gp.dma_start(out=gstage[b, :, :],
                                         in_=gst[b, :, :]
                                         ).then_inc(SEM["sdma"], 16)
                            gp.dma_start(out=ustage[b, :, :],
                                         in_=ust[b, :, :]
                                         ).then_inc(SEM["sdma"], 16)
                            for dv_ in range(4):
                                gp.dma_start(out=dstage[b * 4 + dv_, :, :],
                                             in_=dst[b * 4 + dv_, :, :]
                                             ).then_inc(SEM["sdma"], 16)
                            return None
                        stg_tok[e_][fr_] = pl.step("gpsimd", f_stg, dma_n=6,
                                                   dma_sem="sdma")

                q_done = later(pl.cur("dve"), dbg_gate)
                psM_free = [None, None]
                sig_consumed = later(pl.cur("pe"))
                rw_consumed = None
                for qg in range(NQ):
                    sig_toks = []
                    for rj in range(4):
                        rt = qg * 4 + rj

                        def f_min(v, rt=rt):
                            return v.tensor_scalar(tmin[:], qTv(rt), 0.0, 0.0,
                                                   ALU.min, ALU.add)
                        t1 = pl.step("vector", f_min, inc=("dve", 1),
                                     waits=later(q_done, sig_consumed))

                        def f_exp(a):
                            return a.activation(e32[:], tmin[:], AF.Exp)
                        t2 = pl.step("scalar", f_exp, inc=("act", 1),
                                     waits=[t1])

                        def f_max(v, rt=rt):
                            return v.tensor_scalar(tmax[:], qTv(rt), 0.0, 0.0,
                                                   ALU.max, ALU.add)
                        t3 = pl.step("vector", f_max, inc=("dve", 1),
                                     waits=later(q_done, sig_consumed))

                        def f_sig(v, rj=rj):
                            return v.scalar_tensor_tensor(
                                sig[:, rj * TOWN:(rj + 1) * TOWN], e32[:],
                                -1.0, tmax[:], ALU.add, ALU.add)
                        t4 = pl.step("vector", f_sig, inc=("dve", 1),
                                     waits=[t2, t3])
                        sig_toks.append(t4)

                    def f_dn(t):
                        last = None
                        for h in range(2):
                            for kt in range(4):
                                last = t.matmul(
                                    psDen[0:1, h * 512:(h + 1) * 512],
                                    mn_sb[:, kt:kt + 1],
                                    sig[:, kt * TOWN + h * 512:
                                        kt * TOWN + h * 512 + 512],
                                    start=(kt == 0), stop=(kt == 3))
                        return last
                    tdn = pl.step("tensor", f_dn, inc=("pe", 1),
                                  waits=later(*sig_toks, t_mem, rw_consumed))

                    def f_edn(v):
                        return v.tensor_copy(dn[:], psDen[:])
                    tedn = pl.step("vector", f_edn, inc=("dve", 1),
                                   waits=[tdn])

                    def f_rec(v):
                        return v.reciprocal(rw[:], dn[:])
                    trec = pl.step("vector", f_rec, inc=("dve", 1),
                                   waits=[tedn])

                    def f_rw(v):
                        return v.tensor_scalar(rw[:], rw[:], wmix[0:1, 0:1],
                                               0.0, ALU.mult, ALU.add)
                    trw = pl.step("vector", f_rw, inc=("dve", 1),
                                  waits=later(trec, t_s4))
                    tb = emit_bcast(rw[0:1, :], TOWN, psB2, [trw])
                    rw_consumed = tb

                    def f_erb(v):
                        return v.tensor_copy(rwb[:], psB2[:])
                    terb = pl.step("vector", f_erb, inc=("dve", 1), waits=[tb])

                    am_toks = []
                    for jt in range(4):
                        mp = jt % 2
                        qr2 = qg * 4 + jt

                        def f_am(t, jt=jt, mp=mp):
                            last = None
                            for h in range(2):
                                for kt in range(4):
                                    last = t.matmul(
                                        psM[:, mp * TOWN + h * 512:
                                            mp * TOWN + h * 512 + 512],
                                        mem_sb[:, kt * DKV + jt * P:
                                               kt * DKV + (jt + 1) * P],
                                        sig[:, kt * TOWN + h * 512:
                                            kt * TOWN + h * 512 + 512],
                                        start=(kt == 0), stop=(kt == 3))
                            return last
                        tam = pl.step("tensor", f_am, inc=("pe", 1),
                                      waits=later(*sig_toks, psM_free[mp],
                                                  t_mem))
                        am_toks.append(tam)

                        def f_mul(v, mp=mp):
                            return v.tensor_mul(
                                tmpa[:], psM[:, mp * TOWN:(mp + 1) * TOWN],
                                rwb[:])
                        tm1 = pl.step("vector", f_mul, inc=("dve", 1),
                                      waits=later(tam, terb))
                        psM_free[mp] = tm1

                        def f_add(v, qr2=qr2):
                            return v.tensor_add(yTv(qr2), yTv(qr2), tmpa[:])
                        pl.step("vector", f_add, inc=("dve", 1),
                                waits=later(tm1, last_evac["yT"]))
                    sig_consumed = later(tdn, *am_toks)

            marks.append((len(pl.steps), dict(pl.val)))
            esC = ExitStack()
            with esC:
                PT = esC.enter_context(nc.sbuf_tensor("PT", [P, 4 * 512], BF16))
                rsr = esC.enter_context(nc.sbuf_tensor("rsr", [1, 1024], F32))
                rsb = esC.enter_context(nc.sbuf_tensor("rsb", [P, 2 * 512], F32))
                tmp2 = esC.enter_context(nc.sbuf_tensor("tmp2", [P, 2 * 512], F32))
                stg = esC.enter_context(nc.sbuf_tensor("stg", [P, 2 * 1024], F32))
                BC.bhi = esC.enter_context(nc.sbuf_tensor("bhi2", [1, 1024], BF16))
                BC.blo = esC.enter_context(nc.sbuf_tensor("blo2", [1, 1024], BF16))
                BC.blt = esC.enter_context(nc.sbuf_tensor("blt2", [1, 1024], F32))
                BC.free = None
                psS = esC.enter_context(nc.psum_tensor("psS", [P, 4 * 512], F32))
                psPV = esC.enter_context(nc.psum_tensor("psPV", [P, 2 * 512], F32))
                psBC = esC.enter_context(nc.psum_tensor("psBC", [P, 2 * 512], F32))

                kv_ready = later(pl.cur("dve"))
                psS_free = [None, None]
                PT_free = [None, None]
                psPV_free = None
                rsr_free = None
                stg_free = [None, None]
                pending_tail = [None]
                iters = [(qg, hp, chn) for qg in range(NQ)
                         for hp in range(NH // 2) for chn in range(2)]
                for it, (qg, hp, chn) in enumerate(iters):
                        qr = qg * 4 + hp
                        if True:
                            pv_last = None

                            def emit_pv(kt, g):
                                nonlocal pv_last

                                def f_pv(t, hp=hp, kt=kt, g=g):
                                    t.matmul(
                                        psPV[0:65, 0:512],
                                        vb[:, kt * 520 + (2 * hp) * 65:
                                           kt * 520 + (2 * hp) * 65 + 65],
                                        PT[:, (2 * g) * 512:(2 * g + 1) * 512],
                                        start=(kt == 0),
                                        stop=(kt == NKT - 1))
                                    return t.matmul(
                                        psPV[0:65, 512:1024],
                                        vb[:, kt * 520 + (2 * hp + 1) * 65:
                                           kt * 520 + (2 * hp + 1) * 65 + 65],
                                        PT[:, (2 * g + 1) * 512:
                                           (2 * g + 2) * 512],
                                        start=(kt == 0),
                                        stop=(kt == NKT - 1))
                                tpv = pl.step("tensor", f_pv, inc=("pe", 1),
                                              waits=later(exp_toks[kt],
                                                          psPV_free
                                                          if kt == 0 else None))
                                PT_free[g] = tpv
                                pv_last = tpv

                            exp_toks = {}
                            for kt in range(NKT):
                                g = kt % 2

                                def f_sc(t, hp=hp, qr=qr, chn=chn, kt=kt, g=g):
                                    t.matmul(
                                        psS[:, (2 * g) * 512:
                                            (2 * g + 1) * 512],
                                        kTv(hp)[0:64, kt * P:(kt + 1) * P],
                                        qTv(qr)[0:64,
                                                chn * 512:(chn + 1) * 512],
                                        start=True, stop=True,
                                        tile_position=(0, 0))
                                    return t.matmul(
                                        psS[:, (2 * g + 1) * 512:
                                            (2 * g + 2) * 512],
                                        kTv(hp)[64:128, kt * P:(kt + 1) * P],
                                        qTv(qr)[64:128,
                                                chn * 512:(chn + 1) * 512],
                                        start=True, stop=True,
                                        tile_position=(64, 0))
                                tsc = pl.step("tensor", f_sc, inc=("pe", 1),
                                              waits=later(kv_ready,
                                                          psS_free[g]))

                                def f_exp(a, g=g):
                                    return a.activation(
                                        PT[:, (2 * g) * 512:(2 * g + 2) * 512],
                                        psS[:, (2 * g) * 512:
                                            (2 * g + 2) * 512],
                                        AF.Exp, scale=SCALE)
                                if kt >= 2:
                                    emit_pv(kt - 2, (kt - 2) % 2)
                                if kt == 12 and pending_tail[0] is not None:
                                    pending_tail[0]()
                                    pending_tail[0] = None
                                tex = pl.step("scalar", f_exp, inc=("act", 1),
                                              waits=later(tsc, PT_free[g]))
                                psS_free[g] = tex
                                exp_toks[kt] = tex
                            emit_pv(NKT - 2, (NKT - 2) % 2)
                            emit_pv(NKT - 1, (NKT - 1) % 2)

                            si = (qg * 8 + hp * 2 + chn) % 2
                            def f_stg(v, si=si):
                                return v.tensor_copy(
                                    stg[:, si * 1024:(si + 1) * 1024],
                                    psPV[:, 0:1024])
                            tstg = pl.step("vector", f_stg, inc=("dve", 1),
                                           waits=later(pv_last, stg_free[si]))
                            psPV_free = tstg
                            sv = stg[:, si * 1024:(si + 1) * 1024]

                            def make_tail(sv=sv, si=si, qr=qr, chn=chn,
                                          tstg=tstg):
                                def emit_tail():
                                    nonlocal rsr_free

                                    def f_rec(v):
                                        return v.reciprocal(rsr[:],
                                                            sv[64:65, :])
                                    tr1 = pl.step("vector", f_rec,
                                                  inc=("dve", 1),
                                                  waits=later(tstg, rsr_free))

                                    def f_rs(v):
                                        return v.tensor_scalar(
                                            rsr[:], rsr[:], wmix[0:1, 1:2],
                                            0.0, ALU.mult, ALU.add)
                                    tr2 = pl.step("vector", f_rs,
                                                  inc=("dve", 1),
                                                  waits=later(tr1, t_s4))
                                    tb = emit_bcast(rsr[0:1, :], 1024, psBC,
                                                    [tr2])
                                    rsr_free = tb

                                    def f_eb(v):
                                        return v.tensor_copy(rsb[:],
                                                             psBC[:, 0:1024])
                                    teb = pl.step("vector", f_eb,
                                                  inc=("dve", 1), waits=[tb])

                                    head_toks = []
                                    for hh in range(2):
                                        def f_mul(v, hh=hh):
                                            return v.tensor_mul(
                                                tmp2[hh * 64:(hh + 1) * 64,
                                                     hh * 512:(hh + 1) * 512],
                                                sv[0:64,
                                                   hh * 512:(hh + 1) * 512],
                                                rsb[0:64,
                                                    hh * 512:(hh + 1) * 512])
                                        tm1 = pl.step("vector", f_mul,
                                                      inc=("dve", 1),
                                                      waits=[teb])

                                        def f_add(v, hh=hh):
                                            dst = yTv(qr)[
                                                hh * 64:(hh + 1) * 64,
                                                chn * 512:(chn + 1) * 512]
                                            return v.tensor_add(
                                                dst, dst,
                                                tmp2[hh * 64:(hh + 1) * 64,
                                                     hh * 512:(hh + 1) * 512])
                                        tm2 = pl.step("vector", f_add,
                                                      inc=("dve", 1),
                                                      waits=[tm1])
                                        head_toks.append(tm2)
                                    stg_free[si] = later(*head_toks)
                                return emit_tail
                            pending_tail[0] = make_tail()

                if pending_tail[0] is not None:
                    pending_tail[0]()
                    pending_tail[0] = None

            dbg_gate2 = None
            if debug:
                def f_d2(sync):
                    sync.dma_start(out=dbg["y2"][:, :], in_=yT[:]
                                   ).then_inc(SEM["odma"], 16)
                    return None
                dbg_gate2 = pl.step("sync", f_d2, dma_n=1, dma_sem="odma",
                                    waits=later(pl.cur("dve")))
                n_out_dma[0] += 1

        marks.append((len(pl.steps), dict(pl.val)))
        barrierF = later(pl.cur("pe"), pl.cur("dve"), pl.cur("act"),
                         dbg_gate2)
        esF = ExitStack()
        with esF:
            x_tm = esF.enter_context(nc.sbuf_tensor("x_tm", [P, 8 * D], BF16))
            gw_sb = esF.enter_context(nc.sbuf_tensor("gw_sb", [P, CT * E], F32))
            ones32 = esF.enter_context(nc.sbuf_tensor("ones32", [1, P], F32))
            U128 = esF.enter_context(nc.sbuf_tensor("U128", [P, P], F32))
            iota_c = esF.enter_context(nc.sbuf_tensor("iota_c", [P, CAP], F32))
            twn = esF.enter_context(nc.sbuf_tensor("twn", [P, 64], F32))
            slotbuf = esF.enter_context(nc.sbuf_tensor("slotbuf", [P, 64], F32))
            mkbuf = esF.enter_context(nc.sbuf_tensor("mkbuf", [P, 8], F32))
            sltmp = esF.enter_context(nc.sbuf_tensor("sltmp", [P, 8], F32))
            offs = esF.enter_context(nc.sbuf_tensor("offs", [1, 8], F32))
            cntb = esF.enter_context(nc.sbuf_tensor("cntb", [1, 8], F32))
            BC.bhi = esF.enter_context(nc.sbuf_tensor("bhi3", [1, 64], BF16))
            BC.blo = esF.enter_context(nc.sbuf_tensor("blo3", [1, 64], BF16))
            BC.blt = esF.enter_context(nc.sbuf_tensor("blt3", [1, 64], F32))
            BC.free = None

            def f_gw(sync):
                sync.dma_start(out=gw_sb[:].rearrange("p (c e) -> p c e", e=E),
                               in_=gwT[:, :].rearrange("(c p) e -> p c e", p=P)
                               ).then_inc(SEM["xdma"], 16)
                return None
            t_gw = pl.step("sync", f_gw, dma_n=1, dma_sem="xdma",
                           waits=barrierF)

            def f_o32(v):
                v.memset(ones32[:], 1.0)
                return v.memset(offs[:], 0.0)
            t_o32 = pl.step("vector", f_o32, inc=("dve", 1), waits=barrierF)

            def f_consts2(gp):
                gp.memset(U128[:], 0.0)
                gp.affine_select(out=U128[:], in_=U128[:],
                                 compare_op=ALU.is_ge, fill=1.0, base=0,
                                 pattern=[[-1, P]], channel_multiplier=1)
                return gp.iota(iota_c[:], [[1, CAP]], channel_multiplier=0,
                               allow_small_or_imprecise_dtypes=True)
            t_U = pl.step("gpsimd", f_consts2, inc=("gp", 1), waits=barrierF)

            esD = ExitStack()
            with esD:
                ynat = esD.enter_context(nc.sbuf_tensor("ynat", [P, 2 * D], F32))
                st6b = esD.enter_context(nc.sbuf_tensor("st6b", [P, 24], F32))
                subb = esD.enter_context(nc.sbuf_tensor("subb", [P, 17], F32))
                vrb = esD.enter_context(nc.sbuf_tensor("vrb", [P, 1], F32))
                nbb = esD.enter_context(nc.sbuf_tensor("nbb", [P, 1], F32))
                L0T = esD.enter_context(nc.sbuf_tensor("L0T", [8, TOWN], F32))
                sgr = esD.enter_context(nc.sbuf_tensor("sgr", [8, 1], F32))
                sbc = esD.enter_context(nc.sbuf_tensor("sbc", [P, 8], F32))
                lgt = esD.enter_context(nc.sbuf_tensor("lgt", [P, 8], F32))
                lg2 = esD.enter_context(nc.sbuf_tensor("lg2", [P, 8], F32))
                msr = esD.enter_context(nc.sbuf_tensor("msr", [P, 1], F32))
                mm1 = esD.enter_context(nc.sbuf_tensor("mm1", [P, 1], F32))
                mm2 = esD.enter_context(nc.sbuf_tensor("mm2", [P, 1], F32))
                ee = esD.enter_context(nc.sbuf_tensor("ee", [P, 1], F32))
                ee2 = esD.enter_context(nc.sbuf_tensor("ee2", [P, 1], F32))
                w1 = esD.enter_context(nc.sbuf_tensor("w1", [P, 1], F32))
                w2 = esD.enter_context(nc.sbuf_tensor("w2", [P, 1], F32))
                mk1 = esD.enter_context(nc.sbuf_tensor("mk1", [P, 8], F32))
                mk2 = esD.enter_context(nc.sbuf_tensor("mk2", [P, 8], F32))
                psT3 = esD.enter_context(nc.psum_tensor("psT3", [P, 2048], F32))
                psR = esD.enter_context(nc.psum_tensor("psR", [8, 512], F32))
                psR2 = esD.enter_context(nc.psum_tensor("psR2", [P, 8], F32))
                psP = esD.enter_context(nc.psum_tensor("psP", [P, 8], F32))
                psC = esD.enter_context(nc.psum_tensor("psC", [1, 8], F32))

                psT3_free = [None] * 4
                ynat_free = [None, None]
                xtm_toks = []
                for tt in range(8):
                    tp = tt % 2
                    ev_toks = []
                    for g in range(4):
                        def f_tr(t, g=g, tt=tt):
                            last = None
                            for idx in range(4):
                                rt = g * 4 + idx
                                last = t.transpose(
                                    psT3[:, g * 512 + idx * 128:
                                         g * 512 + (idx + 1) * 128],
                                    yTv(rt)[:, tt * P:(tt + 1) * P],
                                    ident32[:])
                            return last
                        ttr = pl.step("tensor", f_tr, inc=("pe", 1),
                                      waits=later(barrierF, psT3_free[g],
                                                  ynat_free[tp]))

                        def f_ev(v, g=g, tp=tp):
                            return v.tensor_copy(
                                ynat[:, tp * D + g * 512:
                                     tp * D + (g + 1) * 512],
                                psT3[:, g * 512:(g + 1) * 512])
                        tev = pl.step("vector", f_ev, inc=("dve", 1),
                                      waits=[ttr])
                        psT3_free[g] = tev
                        ev_toks.append(tev)
                    ynap = ynat[:, tp * D:(tp + 1) * D]
                    t_rs, t_nb = emit_ln_stats(
                        ynap, st6b, subb, m2s[:, tt:tt + 1], vrb[:, 0:1],
                        r2s[:, tt:tt + 1], nbb[:, 0:1], ev_toks)

                    def f_aff(a, ynap=ynap, tt=tt):
                        return a.activation(x_tm[:, tt * D:(tt + 1) * D],
                                            ynap, AF.Identity,
                                            bias=nbb[:, 0:1],
                                            scale=r2s[:, tt:tt + 1])
                    t_yn = pl.step("scalar", f_aff, inc=("act", 1),
                                   waits=later(t_rs, t_nb))
                    ynat_free[tp] = later(t_yn, t_rs)
                    xtm_toks.append(t_yn)
                xtm_ready = later(*xtm_toks)

                l0_toks = []
                psR_free = None
                for h in range(2):
                    def f_l0(t, h=h):
                        last = None
                        for c in range(CT):
                            last = t.matmul(
                                psR[:, :], gw_sb[:, c * E:(c + 1) * E],
                                yTv(c)[:, h * 512:(h + 1) * 512],
                                start=(c == 0), stop=(c == CT - 1))
                        return last
                    tl0 = pl.step("tensor", f_l0, inc=("pe", 1),
                                  waits=later(t_gw, barrierF, psR_free))

                    def f_el0(v, h=h):
                        return v.tensor_copy(L0T[:, h * 512:(h + 1) * 512],
                                             psR[:])
                    te = pl.step("vector", f_el0, inc=("dve", 1), waits=[tl0])
                    psR_free = te
                    l0_toks.append(te)

                def f_se(t):
                    last = None
                    for c in range(CT):
                        last = t.matmul(psR[:, 0:1],
                                        gw_sb[:, c * E:(c + 1) * E],
                                        onescol[:],
                                        start=(c == 0), stop=(c == CT - 1))
                    return last
                tse = pl.step("tensor", f_se, inc=("pe", 1),
                              waits=later(t_gw, psR_free))

                def f_ese(v):
                    return v.tensor_copy(sgr[:], psR[:, 0:1])
                tese = pl.step("vector", f_ese, inc=("dve", 1), waits=[tse])
                psR_free = tese

                def f_trs(t):
                    return t.transpose(psR2[0:1, 0:8], sgr[:],
                                       ident32[0:8, 0:8])
                ttrs = pl.step("tensor", f_trs, inc=("pe", 1),
                               waits=later(tese, t_ident))

                def f_etrs(v):
                    return v.tensor_copy(BC.blt[:, 0:8], psR2[0:1, 0:8])
                tets = pl.step("vector", f_etrs, inc=("dve", 1),
                               waits=later(ttrs, BC.free))

                def f_sbm(t):
                    return t.matmul(psR2[:, 0:8], ones32[:],
                                    BC.blt[0:1, 0:8], start=True, stop=True)
                tsbm = pl.step("tensor", f_sbm, inc=("pe", 1),
                               waits=later(tets, t_o32))

                def f_esb(v):
                    return v.tensor_copy(sbc[:], psR2[:, 0:8])
                tesb = pl.step("vector", f_esb, inc=("dve", 1), waits=[tsbm])
                BC.free = tsbm
                psR2_free = tesb

                tw_toks = []
                slot_toks = []
                mk_free = None
                psC_free = None
                psP_free = None
                t_offp = None
                for tt in range(8):
                    def f_trl(t, tt=tt):
                        return t.transpose(psR2[:, 0:8],
                                           L0T[:, tt * P:(tt + 1) * P],
                                           ident32[0:8, 0:8])
                    ttl = pl.step("tensor", f_trl, inc=("pe", 1),
                                  waits=later(*l0_toks, psR2_free, t_ident))

                    def f_elg(v):
                        return v.tensor_copy(lgt[:], psR2[:, 0:8])
                    tlg = pl.step("vector", f_elg, inc=("dve", 1),
                                  waits=[ttl])
                    psR2_free = tlg

                    def f_msr(v, tt=tt):
                        return v.scalar_tensor_tensor(
                            msr[:], m2s[:, tt:tt + 1], -1.0,
                            r2s[:, tt:tt + 1], ALU.mult, ALU.mult)
                    t1 = pl.step("vector", f_msr, inc=("dve", 1),
                                 waits=[tlg])

                    def f_lg2(v, tt=tt):
                        return v.tensor_scalar(lg2[:], lgt[:],
                                               r2s[:, tt:tt + 1], 0.0,
                                               ALU.mult, ALU.add)
                    t2 = pl.step("vector", f_lg2, inc=("dve", 1), waits=[tlg])

                    def f_lg3(v):
                        return v.scalar_tensor_tensor(lg2[:], sbc[:], msr[:],
                                                      lg2[:], ALU.mult,
                                                      ALU.add)
                    t3 = pl.step("vector", f_lg3, inc=("dve", 1),
                                 waits=later(t1, t2, tesb))

                    def f_m1(v):
                        return v.reduce_max(mm1[:], lg2[:], axis=AX)
                    t4 = pl.step("vector", f_m1, inc=("dve", 1), waits=[t3])

                    def f_mk1(v):
                        return v.tensor_scalar(mk1[:], lg2[:], mm1[:], 0.0,
                                               ALU.subtract, ALU.is_ge)
                    t5 = pl.step("vector", f_mk1, inc=("dve", 1), waits=[t4])

                    def f_l2(v):
                        return v.scalar_tensor_tensor(lgt[:], mk1[:], -1e30,
                                                      lg2[:], ALU.mult,
                                                      ALU.add)
                    t6 = pl.step("vector", f_l2, inc=("dve", 1), waits=[t5])

                    def f_m2(v):
                        return v.reduce_max(mm2[:], lgt[:], axis=AX)
                    t7 = pl.step("vector", f_m2, inc=("dve", 1), waits=[t6])

                    def f_mk2(v):
                        return v.tensor_scalar(mk2[:], lgt[:], mm2[:], 0.0,
                                               ALU.subtract, ALU.is_ge)
                    t8 = pl.step("vector", f_mk2, inc=("dve", 1), waits=[t7])

                    def f_d(v):
                        return v.tensor_scalar(ee[:], mm2[:], mm1[:], 0.0,
                                               ALU.subtract, ALU.add)
                    t9 = pl.step("vector", f_d, inc=("dve", 1),
                                 waits=[t7, t4])

                    def f_ee(a):
                        return a.activation(ee2[:], ee[:], AF.Exp)
                    t10 = pl.step("scalar", f_ee, inc=("act", 1), waits=[t9])

                    def f_w1(v):
                        return v.tensor_scalar(w1[:], ee2[:], 1.0, 0.0,
                                               ALU.add, ALU.add)
                    t11 = pl.step("vector", f_w1, inc=("dve", 1), waits=[t10])

                    def f_w1r(v):
                        return v.reciprocal(w1[:], w1[:])
                    t12 = pl.step("vector", f_w1r, inc=("dve", 1),
                                  waits=[t11])

                    def f_w2(v):
                        return v.tensor_mul(w2[:], ee2[:], w1[:])
                    t13 = pl.step("vector", f_w2, inc=("dve", 1), waits=[t12])

                    def f_t1(v):
                        return v.tensor_scalar(mk1[:], mk1[:], w1[:], 0.0,
                                               ALU.mult, ALU.add)
                    t14 = pl.step("vector", f_t1, inc=("dve", 1),
                                  waits=[t12, t8])

                    def f_tw(v, tt=tt):
                        return v.scalar_tensor_tensor(
                            twn[:, tt * 8:(tt + 1) * 8], mk2[:], w2[:],
                            mk1[:], ALU.mult, ALU.add)
                    t15 = pl.step("vector", f_tw, inc=("dve", 1),
                                  waits=[t13, t14])
                    tw_toks.append(t15)

                    def f_mk(v, tt=tt):
                        return v.tensor_scalar(mkbuf[:],
                                               twn[:, tt * 8:(tt + 1) * 8],
                                               0.0, 0.0, ALU.is_gt, ALU.add)
                    tmk = pl.step("vector", f_mk, inc=("dve", 1),
                                  waits=later(t15, mk_free))

                    def f_cnt(t):
                        return t.matmul(psC[0:1, 0:8], onescol[:], mkbuf[:],
                                        start=True, stop=True)
                    tcnt = pl.step("tensor", f_cnt, inc=("pe", 1),
                                   waits=later(tmk, psC_free))

                    def f_pos(t):
                        t.matmul(psP[:, 0:8], U128[:], mkbuf[:],
                                 start=True, stop=False)
                        return t.matmul(psP[:, 0:8], ones32[:],
                                        offs[0:1, 0:8],
                                        start=False, stop=True)
                    tpos = pl.step("tensor", f_pos, inc=("pe", 1),
                                   waits=later(tmk, t_offp, psP_free,
                                               t_U, t_o32))

                    def f_ecnt(v):
                        return v.tensor_copy(cntb[0:1, 0:8], psC[0:1, 0:8])
                    tec = pl.step("vector", f_ecnt, inc=("dve", 1),
                                  waits=[tcnt])
                    psC_free = tec

                    def f_off(v):
                        return v.tensor_add(offs[0:1, 0:8], offs[0:1, 0:8],
                                            cntb[0:1, 0:8])
                    t_offp = pl.step("vector", f_off, inc=("dve", 1),
                                     waits=later(tec, tpos))

                    def f_sl1(v):
                        return v.tensor_scalar(sltmp[:], psP[:, 0:8], 1.0,
                                               0.0, ALU.add, ALU.add)
                    ts1 = pl.step("vector", f_sl1, inc=("dve", 1),
                                  waits=[tpos])
                    psP_free = ts1

                    def f_sl2(v):
                        return v.tensor_mul(sltmp[:], sltmp[:], mkbuf[:])
                    ts2 = pl.step("vector", f_sl2, inc=("dve", 1),
                                  waits=[ts1])
                    mk_free = later(ts2, tcnt, tpos)

                    def f_sl3(v, tt=tt):
                        return v.tensor_scalar(
                            slotbuf[:, tt * 8:(tt + 1) * 8], sltmp[:], -1.0,
                            0.0, ALU.add, ALU.add)
                    ts3 = pl.step("vector", f_sl3, inc=("dve", 1),
                                  waits=[ts2])
                    slot_toks.append(ts3)

                if debug:
                    def f_dtw(sync):
                        sync.dma_start(out=dbg["tw"][:, :], in_=twn[:]
                                       ).then_inc(SEM["odma"], 16)
                        return None
                    pl.step("sync", f_dtw, dma_n=1, dma_sem="odma",
                            waits=later(*tw_toks))
                    n_out_dma[0] += 1
                slot_ready = later(*slot_toks, *tw_toks)

            marks.append((len(pl.steps), dict(pl.val)))
            p3_done = later(pl.cur("pe"), pl.cur("dve"), pl.cur("act"))
            esE = ExitStack()
            with esE:
                Pbuf = esE.enter_context(nc.sbuf_tensor("Pbuf", [P, 8 * CAP], BF16))
                Pw = esE.enter_context(nc.sbuf_tensor("Pw", [P, 2 * CAP], BF16))
                Qbuf = esE.enter_context(nc.sbuf_tensor("Qbuf", [P, NSC * TOWN], BF16))
                xc = esE.enter_context(nc.sbuf_tensor("xc", [P, CT * CAP], BF16))
                hc = esE.enter_context(nc.sbuf_tensor("hc", [P, 2 * FRT * CAP], BF16))
                oc = esE.enter_context(nc.sbuf_tensor("oc", [P, 2 * NSC * D], BF16))
                sgl = esE.enter_context(nc.sbuf_tensor("sgl", [P, 2 * CAP], BF16))
                gws = esE.enter_context(nc.sbuf_tensor("gws", [P, 2 * CT * P], BF16))
                uws = esE.enter_context(nc.sbuf_tensor("uws", [P, 2 * CT * P], BF16))
                dws = esE.enter_context(nc.sbuf_tensor("dws", [P, 4 * 512], BF16))
                psA = esE.enter_context(nc.psum_tensor("psA", [P, 1536], F32))
                psG = esE.enter_context(nc.psum_tensor("psG", [P, CAP], F32))
                psS2 = esE.enter_context(nc.psum_tensor("psS2", [P, TOWN], F32))
                psQT = esE.enter_context(nc.psum_tensor("psQT", [P, NSC * P], BF16))

                def pgv(k):
                    return psA[:, 0:CAP] if k == 0 else psA[:, 512:512 + CAP]

                def PDv(sc):
                    return psA[:, sc * 512:(sc + 1) * 512]

                PUv = psA[:, 1024:1024 + CAP]

                def SCv(sp, hh):
                    base = psA if sp == 0 else psS2
                    return base[:, hh * 512:(hh + 1) * 512]

                def SCfull(sp):
                    return (psA if sp == 0 else psS2)[:, 0:TOWN]

                gs_free = [p3_done, p3_done]
                us_free = [p3_done, p3_done]
                dws_free = [p3_done] * 4
                pg_free = [p3_done, p3_done]
                Pbuf_free = p3_done
                Pw_free = [p3_done, p3_done]
                Qbuf_free = p3_done
                psQT_free = p3_done
                xc_free = p3_done
                sgl_free = [p3_done, p3_done]
                hc_free = [p3_done, p3_done]
                oc_free = [p3_done, p3_done]
                SC_free = [p3_done, p3_done]
                psA_clear = p3_done
                psG_free = p3_done
                PU_free = p3_done
                PD_free = p3_done
                yadd = [p3_done] * CT

                for e in range(E):
                    hpar = e % 2
                    opar = e % 2

                    tP_toks = []
                    tQ_toks = []
                    for tt in range(8):
                        def f_P(v, tt=tt, e=e):
                            return v.tensor_scalar(
                                Pbuf[:, tt * CAP:(tt + 1) * CAP],
                                iota_c[:, 0:CAP],
                                slotbuf[:, tt * 8 + e:tt * 8 + e + 1],
                                0.0, ALU.is_equal, ALU.add)
                        tP = pl.step("vector", f_P, inc=("dve", 1),
                                     waits=later(slot_ready, Pbuf_free, t_U))
                        tP_toks.append(tP)

                        wp = tt % 2

                        def f_Pw(v, tt=tt, e=e, wp=wp):
                            return v.tensor_scalar(
                                Pw[:, wp * CAP:(wp + 1) * CAP],
                                Pbuf[:, tt * CAP:(tt + 1) * CAP],
                                twn[:, tt * 8 + e:tt * 8 + e + 1],
                                0.0, ALU.mult, ALU.add)
                        tPw = pl.step("vector", f_Pw, inc=("dve", 1),
                                      waits=later(tP, Pw_free[wp]))

                        def f_tq(t, wp=wp):
                            last = None
                            for sc in range(NSC):
                                last = t.transpose(
                                    psQT[:, sc * P:(sc + 1) * P],
                                    Pw[:, wp * CAP + sc * P:
                                       wp * CAP + (sc + 1) * P],
                                    identbf[:])
                            return last
                        ttq = pl.step("tensor", f_tq, inc=("pe", 1),
                                      waits=later(tPw, psQT_free, t_ident))
                        Pw_free[wp] = ttq

                        def f_eq(v, tt=tt):
                            last = None
                            for sc in range(NSC):
                                last = v.tensor_copy(
                                    Qbuf[:, sc * TOWN + tt * P:
                                         sc * TOWN + (tt + 1) * P],
                                    psQT[:, sc * P:(sc + 1) * P])
                            return last
                        teq = pl.step("vector", f_eq, inc=("dve", 1),
                                      waits=later(ttq, Qbuf_free))
                        psQT_free = teq
                        tQ_toks.append(teq)
                    Q_ready = later(*tQ_toks)
                    P_ready = later(*tP_toks)

                    tg_toks = []
                    ga_toks = []
                    for i in range(CT):
                        gp_par = i % 2

                        def f_ga(t, i=i, gp_par=gp_par):
                            last = None
                            for tt in range(8):
                                last = t.matmul(
                                    pgv(gp_par),
                                    x_tm[:, tt * D + i * P:
                                         tt * D + (i + 1) * P],
                                    Pbuf[:, tt * CAP:(tt + 1) * CAP],
                                    start=(tt == 0), stop=(tt == 7))
                            return last
                        tg = pl.step("tensor", f_ga, inc=("pe", 1),
                                     waits=later(P_ready, xtm_ready,
                                                 pg_free[gp_par], psA_clear))
                        tg_toks.append(tg)

                        def f_eg(v, i=i, gp_par=gp_par):
                            return v.tensor_copy(xc[:, i * CAP:(i + 1) * CAP],
                                                 pgv(gp_par))
                        te = pl.step("vector", f_eg, inc=("dve", 1),
                                     waits=later(tg, xc_free))
                        pg_free[gp_par] = te
                        ga_toks.append(te)
                    Pbuf_free = later(*tg_toks)
                    gather_done = later(*ga_toks)

                    h_toks = []
                    xc_readers = []
                    for fr in range(FRT):
                        ws = fr % 2

                        def f_dg(gp, e=e, fr=fr, ws=ws):
                            gp.dma_start(
                                out=gws[:, ws * CT * P:(ws + 1) * CT * P],
                                in_=gstage[e * FRT + fr, :, :]
                            ).then_inc(SEM["wdma"], 16)
                            return None
                        tdg = pl.step("gpsimd", f_dg, dma_n=1, dma_sem="wdma",
                                      waits=later(gs_free[ws],
                                                  stg_tok[e][fr]))

                        def f_du(gp, e=e, fr=fr, ws=ws):
                            gp.dma_start(
                                out=uws[:, ws * CT * P:(ws + 1) * CT * P],
                                in_=ustage[e * FRT + fr, :, :]
                            ).then_inc(SEM["wdma"], 16)
                            return None
                        tdu = pl.step("gpsimd", f_du, dma_n=1, dma_sem="wdma",
                                      waits=later(us_free[ws],
                                                  stg_tok[e][fr]))

                        def f_g(t, ws=ws):
                            last = None
                            for c in range(CT):
                                last = t.matmul(
                                    psG[:, 0:CAP],
                                    gws[:, ws * CT * P + c * P:
                                        ws * CT * P + (c + 1) * P],
                                    xc[:, c * CAP:(c + 1) * CAP],
                                    start=(c == 0), stop=(c == CT - 1))
                            return last
                        tg2 = pl.step("tensor", f_g, inc=("pe", 1),
                                      waits=later(tdg, gather_done, psG_free))
                        gs_free[ws] = tg2
                        xc_readers.append(tg2)

                        sp2 = fr % 2

                        def f_silu(a, sp2=sp2):
                            return a.activation(
                                sgl[:, sp2 * CAP:(sp2 + 1) * CAP],
                                psG[:, 0:CAP], AF.Silu)
                        tsl = pl.step("scalar", f_silu, inc=("act", 1),
                                      waits=later(tg2, sgl_free[sp2]))
                        psG_free = tsl

                        def f_u(t, ws=ws):
                            last = None
                            for c in range(CT):
                                last = t.matmul(
                                    PUv,
                                    uws[:, ws * CT * P + c * P:
                                        ws * CT * P + (c + 1) * P],
                                    xc[:, c * CAP:(c + 1) * CAP],
                                    start=(c == 0), stop=(c == CT - 1))
                            return last
                        tu = pl.step("tensor", f_u, inc=("pe", 1),
                                     waits=later(tdu, gather_done, PU_free))
                        us_free[ws] = tu
                        xc_readers.append(tu)

                        def f_h(v, fr=fr, sp2=sp2, hpar=hpar):
                            return v.tensor_mul(
                                hc[:, hpar * FRT * CAP + fr * CAP:
                                   hpar * FRT * CAP + (fr + 1) * CAP],
                                sgl[:, sp2 * CAP:(sp2 + 1) * CAP], PUv)
                        th = pl.step("vector", f_h, inc=("dve", 1),
                                     waits=later(tu, tsl, hc_free[hpar]))
                        PU_free = th
                        sgl_free[sp2] = th
                        h_toks.append(th)
                    xc_free = later(*xc_readers)

                    td_toks = []
                    teo_toks = []
                    for dv in range(4):
                        td = None
                        for fr in range(FRT):
                            dslot = (dv * FRT + fr) % 4

                            def f_dd(gp, e=e, fr=fr, dv=dv, dslot=dslot):
                                gp.dma_start(
                                    out=dws[:, dslot * 512:(dslot + 1) * 512],
                                    in_=dstage[(e * FRT + fr) * 4 + dv, :, :]
                                ).then_inc(SEM["wdma"], 16)
                                return None
                            tdd = pl.step("gpsimd", f_dd, dma_n=1,
                                          dma_sem="wdma",
                                          waits=later(dws_free[dslot],
                                                      stg_tok[e][fr]))

                            def f_d(t, fr=fr, dslot=dslot, hpar=hpar):
                                last = None
                                for sc in range(NSC):
                                    last = t.matmul(
                                        PDv(sc),
                                        hc[:, hpar * FRT * CAP + fr * CAP +
                                           sc * P:hpar * FRT * CAP +
                                           fr * CAP + (sc + 1) * P],
                                        dws[:, dslot * 512:(dslot + 1) * 512],
                                        start=(fr == 0), stop=(fr == FRT - 1))
                                return last
                            td = pl.step("tensor", f_d, inc=("pe", 1),
                                         waits=later(tdd, *h_toks,
                                                     PD_free if fr == 0
                                                     else None))
                            dws_free[dslot] = td
                            td_toks.append(td)

                        def f_eo(v, dv=dv, opar=opar):
                            last = None
                            for sc in range(NSC):
                                last = v.tensor_copy(
                                    oc[:, opar * NSC * D + sc * D +
                                       dv * 512:opar * NSC * D + sc * D +
                                       (dv + 1) * 512],
                                    PDv(sc))
                            return last
                        teo = pl.step("vector", f_eo, inc=("dve", 1),
                                      waits=later(td, oc_free[opar]))
                        PD_free = teo
                        teo_toks.append(teo)
                    hc_free[hpar] = later(*td_toks)
                    oc_ready = later(*teo_toks)
                    PU_free = later(PU_free, teo_toks[-1])

                    tsc_toks = []
                    psA_adds = []
                    for i in range(CT):
                        sp = i % 2

                        def f_s2(t, i=i, sp=sp, opar=opar):
                            last = None
                            for sc in range(NSC):
                                for hh in range(2):
                                    last = t.matmul(
                                        SCv(sp, hh),
                                        oc[:, opar * NSC * D + sc * D +
                                           i * P:opar * NSC * D + sc * D +
                                           i * P + P],
                                        Qbuf[:, sc * TOWN + hh * 512:
                                             sc * TOWN + (hh + 1) * 512],
                                        start=(sc == 0), stop=(sc == NSC - 1))
                            return last
                        tsc = pl.step("tensor", f_s2, inc=("pe", 1),
                                      waits=later(oc_ready, Q_ready,
                                                  SC_free[sp]))
                        tsc_toks.append(tsc)

                        def f_acc(v, i=i, sp=sp):
                            return v.tensor_add(yTv(i), yTv(i), SCfull(sp))
                        ta = pl.step("vector", f_acc, inc=("dve", 1),
                                     waits=later(tsc, yadd[i]))
                        SC_free[sp] = ta
                        yadd[i] = ta
                        if sp == 0:
                            psA_adds.append(ta)
                    psA_clear = psA_adds[-1]
                    Qbuf_free = later(*tsc_toks)
                    oc_free[opar] = later(*tsc_toks)

            marks.append((len(pl.steps), dict(pl.val)))
            if upto < 5:
                cut, snap = marks[upto]
                pl.steps = pl.steps[:cut]
                pl.val = dict(snap)
                n_out_dma[0] = snap.get("odma", 0) // 16
                yadd = [(s, v) for s, v in snap.items()]

            def f_out(sync):
                sync.dma_start(
                    out=out[:, :].rearrange("(r p) n -> p r n", p=P),
                    in_=yT[:].rearrange("p (r n) -> p r n", n=TOWN)
                ).then_inc(SEM["odma"], 16)
                return None
            pl.step("sync", f_out, dma_n=1, dma_sem="odma",
                    waits=later(*yadd))
            n_out_dma[0] += 1

            def f_fin(sync):
                return sync.nop()
            pl.step("sync", f_fin, waits=[("odma", 16 * n_out_dma[0])])

            for name in ("xdma", "wdma", "sdma", "odma", "pe", "act", "dve",
                         "gp"):
                SEM[name] = es.enter_context(nc.semaphore(name))

            block = es.enter_context(nc.Block())

            def emit_engine(eng_name, eng):
                emitted = {}
                for st in pl.steps:
                    if st["eng"] != eng_name:
                        continue
                    for s, v in st["waits"]:
                        if emitted.get(s, 0) < v:
                            eng.wait_ge(SEM[s], v)
                            emitted[s] = v
                    inst = st["fn"](eng)
                    if st["inc"] is not None:
                        s, a = st["inc"]
                        assert inst is not None
                        inst.then_inc(SEM[s], a)

            @block.sync
            def _(sync):
                emit_engine("sync", sync)

            @block.gpsimd
            def _(gp):
                emit_engine("gpsimd", gp)

            @block.tensor
            def _(t):
                emit_engine("tensor", t)

            @block.vector
            def _(v):
                emit_engine("vector", v)

            @block.scalar
            def _(a):
                emit_engine("scalar", a)

    return nc, dbg


def make_inputs(inp):
    x = np.asarray(inp["L0_attn_x"], np.float32)
    wq = np.asarray(inp["L0_attn_Wq"], np.float32)
    wk = np.asarray(inp["L0_attn_Wk"], np.float32)
    wv = np.asarray(inp["L0_attn_Wv"], np.float32)
    memory = np.ascontiguousarray(np.asarray(inp["L0_attn_memory"], np.float32))
    memnorm = np.ascontiguousarray(np.asarray(inp["L0_attn_memory_norm"], np.float32))
    memw = np.asarray(inp["L0_attn_memory_weight"], np.float32).reshape(1, 1)
    gw = np.asarray(inp["L0_ffn_gate_w"], np.float32)
    gp = np.asarray(inp["L0_ffn_gate_proj"], np.float32)
    up = np.asarray(inp["L0_ffn_up_proj"], np.float32)
    dp = np.asarray(inp["L0_ffn_down_proj"], np.float32)

    gst = gp.reshape(E, FRT, P, CT, P).transpose(0, 1, 4, 3, 2)
    ust = up.reshape(E, FRT, P, CT, P).transpose(0, 1, 4, 3, 2)
    dst = dp.reshape(E, 4, 512, FRT, P).transpose(0, 3, 1, 4, 2)
    shared = {
        "wqT": np.ascontiguousarray(wq.T),
        "wkT": np.ascontiguousarray(wk.T),
        "wvT": np.ascontiguousarray(wv.T),
        "mem": memory,
        "memnorm": np.ascontiguousarray(memnorm.reshape(4, P).T),
        "memw": np.ascontiguousarray(memw),
        "gwT": np.ascontiguousarray(gw.T),
        "gst": np.ascontiguousarray(gst.reshape(E * FRT, P, CT * P)),
        "ust": np.ascontiguousarray(ust.reshape(E * FRT, P, CT * P)),
        "dst": np.ascontiguousarray(dst.reshape(E * FRT * 4, P, 512)),
    }
    in_maps = []
    for c in range(8):
        beta, r0 = c // 4, (c % 4) * TOWN
        m = dict(shared)
        m["x"] = np.ascontiguousarray(x[beta])
        m["xq"] = np.ascontiguousarray(x[beta, r0:r0 + TOWN])
        in_maps.append(m)
    return in_maps


def assemble(results):
    outp = np.empty((2, SEQ, D), np.float32)
    for c in range(8):
        beta, r0 = c // 4, (c % 4) * TOWN
        outp[beta, r0:r0 + TOWN, :] = np.asarray(results[c]["out"]).T
    return outp


_NC_CACHE = {}


def kernel(**inputs):
    from concourse.bass_utils import run_bass_kernel_spmd
    in_maps = make_inputs(inputs)
    if "nc" not in _NC_CACHE:
        nc, _ = build(debug=False)
        _NC_CACHE["nc"] = nc
    res = run_bass_kernel_spmd(_NC_CACHE["nc"], in_maps, core_ids=list(range(8)))
    return assemble(res.results)
